# version 01
# baseline (speedup 1.0000x reference)
import numpy as np
from contextlib import ExitStack
import concourse.bass as bass
import concourse.mybir as mybir
from concourse.bass_utils import run_bass_kernel_spmd
from concourse.alu_op_type import AluOpType as OP
from bass_rust import IndirectOffsetOnAxis

F32 = mybir.dt.float32
BF16 = mybir.dt.bfloat16
U32 = mybir.dt.uint32
I32 = mybir.dt.int32
AF = mybir.ActivationFunctionType
AX = mybir.AxisListType

NB, NT, DM = 8, 2048, 1024
NCORE = 8

ENGS = ("sync", "gpsimd", "vector", "scalar", "tensor")


class Prog:

    def __init__(self, name="k", serial=False):
        self.nc = bass.Bass("TRN2", target_bir_lowering=False)
        self.st = ExitStack()
        self.q = {e: [] for e in ENGS}
        self.sem = {}
        self.cnt = {}
        self.serial = serial
        self.last = None
        for e in ENGS:
            self.sem[e] = self.st.enter_context(self.nc.semaphore("ms_" + e))
            self.cnt[e] = 0

    def dram(self, name, shape, dt=F32, kind="ExternalInput", **kw):
        return self.nc.dram_tensor(name, list(shape), dt, kind=kind, **kw).ap()

    def sb(self, name, p, f, dt=F32, scope=None, side=None):
        kw = {}
        if side is not None:
            kw["side"] = side
        return (scope or self.st).enter_context(self.nc.sbuf_tensor(name, [p, f], dt, **kw))

    def ps(self, name, p, f, dt=F32):
        return self.st.enter_context(self.nc.psum_tensor(name, [p, f], dt))

    def op(self, eng, fn, waits=(), step=1):
        wl = [w for w in waits if w is not None]
        if self.serial and self.last is not None and self.last[0] is not self.sem[eng]:
            wl = wl + [self.last]
        sem = self.sem[eng]
        if step:
            self.cnt[eng] += step
        tokval = self.cnt[eng]

        def emit(h, fn=fn, wl=wl, step=step, sem=sem):
            for (ws, wv) in wl:
                h.wait_ge(ws, wv)
            inst = fn(h)
            if step:
                inst.then_inc(sem, step)
        self.q[eng].append(emit)
        if step:
            self.last = (sem, tokval)
        return (sem, tokval) if step else None

    def dma(self, eng, out, in_, waits=()):
        return self.op(eng, lambda h: h.dma_start(out=out, in_=in_), waits, step=16)

    def tok(self, eng):
        return (self.sem[eng], self.cnt[eng]) if self.cnt[eng] else None

    def finish(self, final_waits=()):
        blk = self.st.enter_context(self.nc.Block())
        for e in ENGS:
            ops = self.q[e]
            if e == "sync":
                def body_sync(h, ops=ops):
                    for f in ops:
                        f(h)
                    for (ws, wv) in final_waits:
                        h.wait_ge(ws, wv)
                blk.sync(body_sync)
            else:
                if not ops:
                    continue
                def body(h, ops=ops):
                    for f in ops:
                        f(h)
                getattr(blk, e)(body)
        self.st.close()
        return self.nc


def _cdiv(a, b):
    return (a + b - 1) // b



def emit_ln_stats(P, rhs_tiles, sq_fn, mean_b, rstd_b, ps_sum, ps_sq, ones_b,
                  nch, ntok, sq_bufs, tmp, epscol, waits_per_kc=None, extra_first=()):
    nt = _cdiv(ntok, 512)
    inv = 1.0 / (nch * 128)
    sq_war = [None, None]
    mm_tok = None
    for kc in range(nch):
        b = kc % 2
        w = [sq_war[b]]
        if waits_per_kc is not None:
            w.append(waits_per_kc[kc])
        t_sq = P.op("scalar", lambda h, kc=kc, b=b: sq_fn(h, kc, sq_bufs[b]), w)

        def mm(h, kc=kc, b=b):
            last = None
            for j in range(nt):
                c0, c1 = j * 512, min((j + 1) * 512, ntok)
                last = h.matmul(ps_sum[j][:, 0:c1 - c0], ones_b[:, :], rhs_tiles[kc][:, c0:c1],
                                start=(kc == 0), stop=(kc == nch - 1), skip_group_check=True)
                last = h.matmul(ps_sq[j][:, 0:c1 - c0], ones_b[:, :], sq_bufs[b][:, c0:c1],
                                start=(kc == 0), stop=(kc == nch - 1), skip_group_check=True)
            return last
        mm_tok = P.op("tensor", mm, [t_sq] + (list(extra_first) if kc == 0 else []))
        sq_war[b] = mm_tok
    toks = []
    for j in range(nt):
        c0, c1 = j * 512, min((j + 1) * 512, ntok)
        t1 = P.op("scalar", lambda h, j=j, c0=c0, c1=c1: h.activation(
            mean_b[:, c0:c1], ps_sum[j][:, 0:c1 - c0], AF.Copy, scale=inv), [mm_tok])
        t2 = P.op("scalar", lambda h, j=j, c0=c0, c1=c1: h.activation(
            rstd_b[:, c0:c1], ps_sq[j][:, 0:c1 - c0], AF.Copy, scale=inv), [mm_tok])
        t3 = P.op("vector", lambda h, c0=c0, c1=c1: h.tensor_tensor(
            tmp[:, c0:c1], mean_b[:, c0:c1], mean_b[:, c0:c1], OP.mult), [t1])
        t4 = P.op("vector", lambda h, c0=c0, c1=c1: h.tensor_tensor(
            rstd_b[:, c0:c1], rstd_b[:, c0:c1], tmp[:, c0:c1], OP.subtract), [t2, t3])
        t5 = P.op("scalar", lambda h, c0=c0, c1=c1: h.activation(
            rstd_b[:, c0:c1], rstd_b[:, c0:c1], AF.Sqrt, bias=epscol[:, 0:1]), [t4])
        t6 = P.op("vector", lambda h, c0=c0, c1=c1: h.reciprocal(
            rstd_b[:, c0:c1], rstd_b[:, c0:c1]), [t5])
        toks.append(t6)
    return toks



def build_launch_A():
    P = Prog("A")
    C = 128
    NCH = NT // C

    x0T = P.dram("x0T", [DM, NT])
    wqT = P.dram("wqT", [DM, 256])
    wkT = P.dram("wkT", [DM, 256])
    wvT = P.dram("wvT", [DM, 256])
    woT = P.dram("woT", [256, DM])
    memI = P.dram("memI", [256, 256])
    mnI = P.dram("mnI", [256, 1])
    gateI = P.dram("gateI", [1, 1])
    triI = P.dram("triI", [128, 128])
    identI = P.dram("identI", [128, 128])
    x1T = P.dram("x1T", [DM, NT], kind="ExternalOutput")
    y0T = P.dram("y0T", [DM, NT], kind="ExternalOutput")

    X = [P.sb(f"X{i}", 128, NT) for i in range(8)]
    LB = [P.sb(f"LB{i}", 128, NT, BF16) for i in range(8)]
    ATT = [P.sb(f"ATT{i}", 128, NT, BF16) for i in range(8)]
    SQ = [P.sb(f"SQ{i}", 128, NT, BF16) for i in range(2)]
    TMPF = P.sb("TMPF", 128, NT)
    TMPG = P.sb("TMPG", 128, NT)
    mean_b = P.sb("mean_b", 128, NT)
    rstd_b = P.sb("rstd_b", 128, NT)
    wqb = P.sb("wqb", 128, 2048, BF16)
    wkb = P.sb("wkb", 128, 2048, BF16)
    wvb = P.sb("wvb", 128, 2048, BF16)
    wob = P.sb("wob", 128, 2048, BF16)
    memb = P.sb("memb", 128, 512, BF16)
    mnb = P.sb("mnb", 128, 2, BF16)
    mnf = P.sb("mnf", 128, 2)
    trib = P.sb("trib", 128, 128, BF16)
    identb = P.sb("identb", 128, 128, BF16)
    ones_b = P.sb("ones_b", 128, 128, BF16)
    ones1f = P.sb("ones1f", 1, 128)
    onescol = P.sb("onescol", 128, 1, BF16)
    onescolf = P.sb("onescolf", 128, 1)
    epscol = P.sb("epscol", 128, 1)
    gf = P.sb("gf", 1, 4)
    gcol = P.sb("gcol", 128, 2)
    sgq = P.sb("sgq", 128, 256, BF16)
    sgk = P.sb("sgk", 128, 256, BF16)
    sgkT = P.sb("sgkT", 128, 256, BF16)
    vTb = P.sb("vTb", 128, 256, BF16)
    Amb = P.sb("Amb", 128, 128, BF16)
    Ssb = P.sb("Ssb", 128, 512, BF16)
    zb = P.sb("zb", 128, 2, BF16)
    dsc = P.sb("dsc", 1, 512)
    rb_sb = P.sb("rb_sb", 128, 128)
    rmn_sb = P.sb("rmn_sb", 128, 128)
    LOC = P.sb("LOC", 128, 256)
    MON = P.sb("MON", 128, 256)
    combb = P.sb("combb", 128, 256, BF16)
    TS = P.sb("TS", 128, 256)
    TS2 = P.sb("TS2", 128, 256)

    pb = [P.ps(f"pb{i}", 128, 512) for i in range(8)]

    ldx = [P.dma("sync", X[i][:, :], x0T[128 * i:128 * (i + 1), :]) for i in range(8)]
    for kc in range(8):
        P.dma("gpsimd", wqb[:, 256 * kc:256 * (kc + 1)], wqT[128 * kc:128 * (kc + 1), :])
        P.dma("gpsimd", wkb[:, 256 * kc:256 * (kc + 1)], wkT[128 * kc:128 * (kc + 1), :])
        P.dma("gpsimd", wvb[:, 256 * kc:256 * (kc + 1)], wvT[128 * kc:128 * (kc + 1), :])
    for ec in range(2):
        P.dma("gpsimd", wob[:, 1024 * ec:1024 * (ec + 1)], woT[128 * ec:128 * (ec + 1), :])
        P.dma("gpsimd", memb[:, 256 * ec:256 * (ec + 1)], memI[128 * ec:128 * (ec + 1), :])
        P.dma("gpsimd", mnb[:, ec:ec + 1], mnI[128 * ec:128 * (ec + 1), :])
        P.dma("gpsimd", mnf[:, ec:ec + 1], mnI[128 * ec:128 * (ec + 1), :])
    P.dma("gpsimd", trib[:, :], triI[:, :])
    P.dma("gpsimd", identb[:, :], identI[:, :])
    wall = P.dma("gpsimd", gf[0:1, 0:1], gateI[:, :])
    P.op("gpsimd", lambda h: h.memset(ones_b[:, :], 1.0))
    P.op("gpsimd", lambda h: h.memset(ones1f[:, :], 1.0))
    P.op("gpsimd", lambda h: h.memset(onescol[:, :], 1.0))
    P.op("gpsimd", lambda h: h.memset(onescolf[:, :], 1.0))
    call = P.op("gpsimd", lambda h: h.memset(epscol[:, :], 1e-5))

    t_g = P.op("scalar", lambda h: h.activation(gf[0:1, 0:1], gf[0:1, 0:1], AF.Sigmoid),
               [wall, call])
    t_1mg = P.op("vector", lambda h: h.tensor_scalar(gf[0:1, 1:2], gf[0:1, 0:1], -1.0, 1.0,
                                                     OP.mult, OP.add), [t_g])

    def ms_mm(h):
        h.matmul(pb[2][0:1, 0:1], mnf[:, 0:1], onescolf[:, :], start=True, stop=False,
                 skip_group_check=True)
        return h.matmul(pb[2][0:1, 0:1], mnf[:, 1:2], onescolf[:, :], start=False, stop=True,
                        skip_group_check=True)
    t_ms = P.op("tensor", ms_mm, [wall, call])
    t_msc = P.op("vector", lambda h: h.tensor_scalar(gf[0:1, 2:3], pb[2][0:1, 0:1], 1e-6, None,
                                                     OP.is_ge), [t_ms])
    t_gf = P.op("vector", lambda h: h.tensor_tensor(gf[0:1, 3:4], gf[0:1, 0:1], gf[0:1, 2:3],
                                                    OP.mult), [t_msc])

    def bc_mm(h):
        h.matmul(pb[3][:, 0:1], ones1f[:, :], gf[0:1, 1:2], start=True, stop=True,
                 skip_group_check=True)
        return h.matmul(pb[3][:, 1:2], ones1f[:, :], gf[0:1, 3:4], start=False, stop=True,
                        skip_group_check=True)
    t_bc = P.op("tensor", bc_mm, [t_1mg, t_gf])
    t_gcol = P.op("vector", lambda h: h.tensor_copy(gcol[:, :], pb[3][:, 0:2]), [t_bc])

    cast_tok = [P.op("vector", lambda h, kc=kc: h.tensor_copy(LB[kc][:, :], X[kc][:, :]),
                     [ldx[kc]]) for kc in range(8)]

    stat = emit_ln_stats(P, LB, lambda h, kc, dst: h.activation(dst[:, :], X[kc][:, :], AF.Square),
                         mean_b, rstd_b, pb[0:4], pb[4:8], ones_b, 8, NT, SQ, TMPF, epscol,
                         waits_per_kc=[cast_tok[kc] for kc in range(8)], extra_first=[call])
    lnall = None
    for kc in range(8):
        t1 = P.op("vector", lambda h, kc=kc: h.tensor_tensor(TMPG[:, :], X[kc][:, :], mean_b[:, :],
                                                             OP.subtract), [stat[-1]])
        lnall = P.op("vector", lambda h, kc=kc: h.tensor_tensor(LB[kc][:, :], TMPG[:, :],
                                                                rstd_b[:, :], OP.mult), [t1])

    prev = {"q_bank": t_gcol, "k_bank": None, "vT_bank": None, "den_read": None,
            "Ssb": None, "zb": t_msc, "tr_bank": None, "rb_bank": None,
            "out47_bank": None}
    for ci in range(NCH):
        sl = slice(ci * C, ci * C + C)
        first = (ci == 0)
        last = (ci == NCH - 1)

        def q_mm(h, sl=sl):
            r = None
            for m in range(2):
                for kc in range(8):
                    r = h.matmul(pb[3][:, 128 * m:128 * (m + 1)],
                                 wqb[:, 256 * kc + 128 * m:256 * kc + 128 * (m + 1)],
                                 LB[kc][:, sl], start=(kc == 0 and m == 0),
                                 stop=(kc == 7 and m == 1), skip_group_check=True)
            return r
        t_q = P.op("tensor", q_mm, [lnall, prev["q_bank"]])

        def k_mm(h, sl=sl):
            r = None
            for m in range(2):
                for kc in range(8):
                    r = h.matmul(pb[5][:, 128 * m:128 * (m + 1)],
                                 wkb[:, 256 * kc + 128 * m:256 * kc + 128 * (m + 1)],
                                 LB[kc][:, sl], start=(kc == 0 and m == 0),
                                 stop=(kc == 7 and m == 1), skip_group_check=True)
            return r
        t_k = P.op("tensor", k_mm, [prev["k_bank"]])

        def vT_mm(h, sl=sl):
            r = None
            for kc in range(8):
                r = h.matmul(pb[4][:, 0:256], LB[kc][:, sl], wvb[:, 256 * kc:256 * (kc + 1)],
                             start=(kc == 0), stop=(kc == 7), skip_group_check=True)
            return r
        t_vT = P.op("tensor", vT_mm, [prev["vT_bank"]])
        t_vTe = P.op("vector", lambda h: h.tensor_copy(vTb[:, :], pb[4][:, 0:256]), [t_vT])

        t1 = P.op("vector", lambda h: h.tensor_scalar(TS[:, :], pb[3][:, 0:256], 0.0, None,
                                                      OP.min), [t_q])
        t2 = P.op("scalar", lambda h: h.activation(TS[:, :], TS[:, :], AF.Exp), [t1])
        t3 = P.op("scalar", lambda h: h.activation(TS2[:, :], pb[3][:, 0:256], AF.Relu), [t_q])
        t_sgq = P.op("vector", lambda h: h.tensor_tensor(sgq[:, :], TS[:, :], TS2[:, :], OP.add),
                     [t2, t3])
        t1 = P.op("vector", lambda h: h.tensor_scalar(TS[:, :], pb[5][:, 0:256], 0.0, None,
                                                      OP.min), [t_k, t_sgq])
        t2 = P.op("scalar", lambda h: h.activation(TS[:, :], TS[:, :], AF.Exp), [t1])
        t3 = P.op("scalar", lambda h: h.activation(TS2[:, :], pb[5][:, 0:256], AF.Relu), [t_k])
        t_sgk = P.op("vector", lambda h: h.tensor_tensor(sgk[:, :], TS[:, :], TS2[:, :], OP.add),
                     [t2, t3])

        def tr_mm(h):
            r = None
            for dc in range(2):
                r = h.transpose(pb[6][:, 64 * dc:64 * (dc + 1)].bitcast(BF16),
                                sgk[:, 128 * dc:128 * (dc + 1)], identb[:, :])
            return r
        t_tr = P.op("tensor", tr_mm, [t_sgk, prev["tr_bank"]])
        t_sgkTe = P.op("vector", lambda h: h.tensor_copy(sgkT[:, :], pb[6][:, 0:128].bitcast(BF16)),
                       [t_tr])

        def A_mm(h):
            r = None
            for dc in range(2):
                r = h.matmul(pb[6][:, 256:384], sgk[:, 128 * dc:128 * (dc + 1)],
                             sgq[:, 128 * dc:128 * (dc + 1)], start=(dc == 0), stop=(dc == 1),
                             skip_group_check=True)
            return r
        t_A = P.op("tensor", A_mm, [t_sgq, t_sgk, t_sgkTe])
        t_Am = P.op("vector", lambda h: h.tensor_tensor(Amb[:, :], pb[6][:, 256:384], trib[:, :],
                                                        OP.mult), [t_A])

        def den_mm(h, first=first):
            r = h.matmul(pb[4][0:1, 256:384], onescol[:, :], Amb[:, :], start=True, stop=False,
                         skip_group_check=True)
            if not first:
                for dc in range(2):
                    r = h.matmul(pb[4][0:1, 256:384], zb[:, dc:dc + 1],
                                 sgq[:, 128 * dc:128 * (dc + 1)], start=False, stop=False,
                                 skip_group_check=True)
            for dc in range(2):
                r = h.matmul(pb[4][0:1, 384:512], mnb[:, dc:dc + 1],
                             sgq[:, 128 * dc:128 * (dc + 1)], start=(dc == 0), stop=(dc == 1),
                             skip_group_check=True)
            return r
        t_den = P.op("tensor", den_mm, [t_Am, t_vTe, prev["den_read"], prev["zb"]])

        def num_mm(h, first=first):
            r = None
            for m in range(2):
                st = True
                if not first:
                    for dc in range(2):
                        r = h.matmul(pb[6][:, 128 * m:128 * (m + 1)],
                                     Ssb[:, 256 * dc + 128 * m:256 * dc + 128 * (m + 1)],
                                     sgq[:, 128 * dc:128 * (dc + 1)], start=st, stop=False,
                                     skip_group_check=True)
                        st = False
                r = h.matmul(pb[6][:, 128 * m:128 * (m + 1)], vTb[:, 128 * m:128 * (m + 1)],
                             Amb[:, :], start=st, stop=True, skip_group_check=True)
            return r
        t_num = P.op("tensor", num_mm, [t_sgkTe, t_Am, t_vTe, prev["Ssb"]])

        def mo_mm(h):
            r = None
            for m in range(2):
                for dc in range(2):
                    r = h.matmul(pb[5][:, 128 * m:128 * (m + 1)],
                                 memb[:, 256 * dc + 128 * m:256 * dc + 128 * (m + 1)],
                                 sgq[:, 128 * dc:128 * (dc + 1)],
                                 start=(dc == 0 and m == 0), stop=(dc == 1 and m == 1),
                                 skip_group_check=True)
            return r
        t_mo = P.op("tensor", mo_mm, [t_sgq, t_sgk])

        def S_mm(h, first=first, last=last):
            r = None
            for dc in range(2):
                r = h.matmul(pb[dc][:, 0:256], sgkT[:, 128 * dc:128 * (dc + 1)], vTb[:, :],
                             start=first, stop=last, skip_group_check=True)
            for dc in range(2):
                r = h.matmul(pb[2][:, 128 * dc:128 * dc + 1], sgkT[:, 128 * dc:128 * (dc + 1)],
                             onescol[:, :], start=first, stop=last, skip_group_check=True)
            return r
        t_S = P.op("tensor", S_mm, [t_sgkTe, t_vTe, prev["Ssb"], prev["zb"]])
        t_Ssb = P.op("vector", lambda h: (h.tensor_copy(Ssb[:, 0:256], pb[0][:, 0:256]),
                                          h.tensor_copy(Ssb[:, 256:512], pb[1][:, 0:256]))[-1],
                     [t_S])
        t_zb = P.op("vector", lambda h: (h.tensor_copy(zb[:, 0:1], pb[2][:, 0:1]),
                                         h.tensor_copy(zb[:, 1:2], pb[2][:, 128:129]))[-1], [t_S])

        t_d1 = P.op("vector", lambda h: h.tensor_scalar(dsc[0:1, 0:128], pb[4][0:1, 256:384],
                                                        1e-6, None, OP.max), [t_den])
        t_d2 = P.op("vector", lambda h: h.reciprocal(dsc[0:1, 0:128], dsc[0:1, 0:128]), [t_d1])
        t_m1 = P.op("vector", lambda h: h.tensor_scalar(dsc[0:1, 128:256], pb[4][0:1, 384:512],
                                                        1e-6, None, OP.max), [t_den])
        t_m2 = P.op("vector", lambda h: h.reciprocal(dsc[0:1, 128:256], dsc[0:1, 128:256]), [t_m1])

        def rb_mm(h):
            h.matmul(pb[7][:, 0:128], ones1f[:, :], dsc[0:1, 0:128], start=True, stop=True,
                     skip_group_check=True)
            return h.matmul(pb[7][:, 128:256], ones1f[:, :], dsc[0:1, 128:256], start=False,
                            stop=True, skip_group_check=True)
        t_rb = P.op("tensor", rb_mm, [t_d2, t_m2, prev["rb_bank"]])
        t_rbe = P.op("vector", lambda h: h.tensor_copy(rb_sb[:, :], pb[7][:, 0:128]), [t_rb])
        t_rmne = P.op("vector", lambda h: h.tensor_copy(rmn_sb[:, :], pb[7][:, 128:256]), [t_rb])

        t_loc = P.op("vector", lambda h: (h.tensor_tensor(LOC[:, 0:128], pb[6][:, 0:128],
                                                          rb_sb[:, :], OP.mult),
                                          h.tensor_tensor(LOC[:, 128:256], pb[6][:, 128:256],
                                                          rb_sb[:, :], OP.mult))[-1],
                     [t_num, t_rbe])
        t_mon = P.op("vector", lambda h: (h.tensor_tensor(MON[:, 0:128], pb[5][:, 0:128],
                                                          rmn_sb[:, :], OP.mult),
                                          h.tensor_tensor(MON[:, 128:256], pb[5][:, 128:256],
                                                          rmn_sb[:, :], OP.mult))[-1],
                     [t_mo, t_rmne])
        t_c1 = P.op("vector", lambda h: h.tensor_scalar(LOC[:, :], LOC[:, :], gcol[:, 0:1], None,
                                                        OP.mult), [t_loc, t_gcol])
        t_c2 = P.op("vector", lambda h: h.tensor_scalar(MON[:, :], MON[:, :], gcol[:, 1:2], None,
                                                        OP.mult), [t_mon])
        t_comb = P.op("vector", lambda h: h.tensor_tensor(combb[:, :], LOC[:, :], MON[:, :],
                                                          OP.add), [t_c1, t_c2])

        def out_mm(h):
            r = None
            for m in range(8):
                bank = pb[3] if m < 4 else pb[7]
                off = (m % 4) * 128
                for ec in range(2):
                    r = h.matmul(bank[:, off:off + 128],
                                 wob[:, 1024 * ec + 128 * m:1024 * ec + 128 * (m + 1)],
                                 combb[:, 128 * ec:128 * (ec + 1)],
                                 start=(ec == 0 and m % 4 == 0), stop=(ec == 1),
                                 skip_group_check=True)
            return r
        t_out = P.op("tensor", out_mm, [t_comb, t_rbe, t_rmne])

        def ev03(h, sl=sl):
            r = None
            for m in range(4):
                r = h.copy(ATT[m][:, sl], pb[3][:, 128 * m:128 * (m + 1)])
            return r
        def ev47(h, sl=sl):
            r = None
            for m in range(4):
                r = h.copy(ATT[4 + m][:, sl], pb[7][:, 128 * m:128 * (m + 1)])
            return r
        t_e03 = P.op("scalar", ev03, [t_out])
        t_e47 = P.op("scalar", ev47, [t_out])

        prev = {"q_bank": t_e03, "k_bank": t_mon, "vT_bank": t_m1, "den_read": t_m2,
                "Ssb": t_Ssb, "zb": t_zb, "tr_bank": t_loc, "rb_bank": t_e47,
                "out47_bank": t_e47}

    scan_tail_v = P.tok("vector")
    scan_tail_s = P.tok("scalar")

    stat2 = emit_ln_stats(P, ATT, lambda h, kc, dst: h.activation(dst[:, :], ATT[kc][:, :],
                                                                  AF.Square),
                          mean_b, rstd_b, pb[0:4], pb[4:8], ones_b, 8, NT, SQ, TMPF, epscol,
                          waits_per_kc=[scan_tail_v if kc == 0 else None for kc in range(8)],
                          extra_first=[scan_tail_v, scan_tail_s])
    x1_tok = []
    for kc in range(8):
        t1 = P.op("vector", lambda h, kc=kc: h.tensor_tensor(TMPG[:, :], ATT[kc][:, :],
                                                             mean_b[:, :], OP.subtract),
                  [stat2[-1]])
        t2 = P.op("vector", lambda h, kc=kc: h.tensor_tensor(TMPG[:, :], TMPG[:, :], rstd_b[:, :],
                                                             OP.mult), [t1])
        t3 = P.op("vector", lambda h, kc=kc: h.tensor_tensor(X[kc][:, :], X[kc][:, :], TMPG[:, :],
                                                             OP.add), [t2])
        t4 = P.dma("sync", x1T[128 * kc:128 * (kc + 1), :], X[kc][:, :], [t3])
        t5 = P.op("vector", lambda h, kc=kc: h.tensor_copy(LB[kc][:, :], X[kc][:, :]), [t3])
        x1_tok.append((t3, t4, t5))

    stat3 = emit_ln_stats(P, LB, lambda h, kc, dst: h.activation(dst[:, :], X[kc][:, :],
                                                                 AF.Square),
                          mean_b, rstd_b, pb[0:4], pb[4:8], ones_b, 8, NT, SQ, TMPF, epscol,
                          waits_per_kc=[x1_tok[kc][2] for kc in range(8)],
                          extra_first=[P.tok("vector"), P.tok("scalar")])
    fin = []
    dma_prev = [None, None]
    for kc in range(8):
        buf = TMPG if kc % 2 == 0 else TMPF
        t1 = P.op("vector", lambda h, kc=kc, buf=buf: h.tensor_tensor(
            buf[:, :], X[kc][:, :], mean_b[:, :], OP.subtract),
            [stat3[-1], dma_prev[kc % 2]])
        t2 = P.op("vector", lambda h, kc=kc, buf=buf: h.tensor_tensor(
            buf[:, :], buf[:, :], rstd_b[:, :], OP.mult), [t1])
        t3 = P.dma("sync", y0T[128 * kc:128 * (kc + 1), :], buf[:, :], [t2])
        dma_prev[kc % 2] = t3
        fin.append(t3)
    final = fin + [t[1] for t in x1_tok]
    return P.finish(final_waits=final)


def host_maps_A(inputs):
    t_idx = np.arange(128)
    tri = np.zeros((128, 128), np.float32)
    tri[t_idx[:, None] <= t_idx[None, :]] = 1.0
    base = {
        "wqT": np.ascontiguousarray(inputs["L0_attn_w_q"].T),
        "wkT": np.ascontiguousarray(inputs["L0_attn_w_k"].T),
        "wvT": np.ascontiguousarray(inputs["L0_attn_w_v"].T),
        "woT": np.ascontiguousarray(inputs["L0_attn_w_o"].T),
        "memI": np.ascontiguousarray(inputs["L0_attn_memory"]),
        "mnI": np.ascontiguousarray(inputs["L0_attn_memory_norm"].reshape(256, 1)),
        "gateI": np.ascontiguousarray(inputs["L0_attn_gate"].reshape(1, 1)),
        "triI": tri,
        "identI": np.eye(128, dtype=np.float32),
    }
    x0 = inputs["L0_attn_hidden_states"]
    maps = []
    for c in range(NCORE):
        m = dict(base)
        m["x0T"] = np.ascontiguousarray(x0[c].T)
        maps.append(m)
    return maps





def build_launch_Battn(debug_w=None, serial=False):
    P = Prog("Ba", serial=serial)
    W = 256
    NW = NT // W

    yT = P.dram("yT", [DM, NT])
    qkvI = P.dram("qkvI", [DM, 3 * DM])
    woI = P.dram("woI", [DM, DM])
    aqI = P.dram("aqI", [128, NT])
    bqI = P.dram("bqI", [128, NT])
    akI = P.dram("akI", [128, NT])
    bkI = P.dram("bkI", [128, NT])
    qsI = P.dram("qsI", [128, 2])
    ksI = P.dram("ksI", [128, 2])
    maskI = P.dram("maskI", [512, 512])
    identI = P.dram("identI", [128, 128])
    permI = P.dram("permI", [128, 128])
    youtT = P.dram("youtT", [DM, NT], kind="ExternalOutput")
    if debug_w is not None:
        qrD = P.dram("qrD", [128, 2048], kind="ExternalOutput")
        krD = P.dram("krD", [128, 2048], kind="ExternalOutput")
        vtD = P.dram("vtD", [128, 2048], kind="ExternalOutput")
        aoD = P.dram("aoD", [128, 2048], kind="ExternalOutput")
        ybD = P.dram("ybD", [128, 2056], kind="ExternalOutput")
        simD = P.dram("simD", [128, 512], kind="ExternalOutput")
        avD = P.dram("avD", [64, 128], kind="ExternalOutput")
        prbD = P.dram("prbD", [128, 512], kind="ExternalOutput")
        ptrD = P.dram("ptrD", [128, 512], kind="ExternalOutput")

    Y = [P.sb(f"Y{i}", 128, NT + 1) for i in range(8)]
    qkvb = P.sb("qkvb", 128, 24576, BF16)
    wo2b = P.sb("wo2b", 128, 8192, BF16)
    AQ = P.sb("AQ", 128, NT, BF16)
    BQ = P.sb("BQ", 128, NT, BF16)
    AK = P.sb("AK", 128, NT, BF16)
    BK = P.sb("BK", 128, NT, BF16)
    qsc = P.sb("qsc", 128, 2)
    ksc = P.sb("ksc", 128, 2)
    identb = P.sb("identb", 128, 128, BF16)
    permb = P.sb("permb", 128, 128, BF16)
    maskb = P.sb("maskb", 128, 2048, BF16)
    hmaskb = P.sb("hmaskb", 128, 128, BF16)
    YBc = [P.sb(f"YBc{i}", 128, 8 * 257, BF16) for i in range(2)]
    QR = P.sb("QR", 128, 2048, BF16)
    KR = [P.sb(f"KR{i}", 128, 2048, BF16) for i in range(2)]
    VT = [P.sb(f"VT{i}", 128, 2048, BF16) for i in range(2)]
    AO = P.sb("AO", 128, 2048, BF16)
    PRB = P.sb("PRB", 128, 512, BF16)
    RQN = P.sb("RQN", 128, 256)
    QN = P.sb("QN", 128, 256, BF16)
    SWP = P.sb("SWP", 128, 256, BF16)
    T256 = P.sb("T256", 128, 256, BF16)
    nbias = P.sb("nbias", 128, 2)
    rsum = P.sb("rsum", 128, 2)

    pb = [P.ps(f"pb{i}", 128, 512) for i in range(8)]

    ldy = [P.dma("sync", Y[i][:, 1:NT + 1], yT[128 * i:128 * (i + 1), :]) for i in range(8)]
    zpad = [P.op("gpsimd", lambda h, i=i: h.memset(Y[i][:, 0:1], 0.0)) for i in range(8)]
    for kc in range(8):
        P.dma("gpsimd", qkvb[:, 3072 * kc:3072 * (kc + 1)], qkvI[128 * kc:128 * (kc + 1), :])
        P.dma("gpsimd", wo2b[:, 1024 * kc:1024 * (kc + 1)], woI[128 * kc:128 * (kc + 1), :])
    P.dma("gpsimd", AQ[:, :], aqI[:, :])
    P.dma("gpsimd", BQ[:, :], bqI[:, :])
    P.dma("gpsimd", AK[:, :], akI[:, :])
    P.dma("gpsimd", BK[:, :], bkI[:, :])
    P.dma("sync", qsc[:, :], qsI[:, :])
    P.dma("sync", ksc[:, :], ksI[:, :])
    P.dma("gpsimd", identb[:, :], identI[:, :])
    P.dma("gpsimd", permb[:, :], permI[:, :])
    for t in range(4):
        P.dma("gpsimd", maskb[:, 512 * t:512 * (t + 1)], maskI[128 * t:128 * (t + 1), :])
    wall = P.tok("gpsimd")
    wsync = P.tok("sync")

    P.op("gpsimd", lambda h: h.memset(hmaskb[:, :], 0.0))
    P.op("gpsimd", lambda h: h.memset(hmaskb[0:64, 0:64], 1.0))
    hm_tok = P.op("gpsimd", lambda h: h.memset(hmaskb[64:128, 64:128], 1.0))

    f1 = P.op("vector", lambda h: h.tensor_scalar(AQ[:, :], AQ[:, :], qsc[:, 0:1], None, OP.mult),
              [wall, wsync])
    f2 = P.op("vector", lambda h: h.tensor_scalar(BQ[:, :], BQ[:, :], qsc[:, 1:2], None, OP.mult), [f1])
    f3 = P.op("vector", lambda h: h.tensor_scalar(AK[:, :], AK[:, :], ksc[:, 0:1], None, OP.mult), [f2])
    tabs = P.op("vector", lambda h: h.tensor_scalar(BK[:, :], BK[:, :], ksc[:, 1:2], None, OP.mult), [f3])

    def cast_win(w, slot, extra=()):
        toks = []
        for kc in range(8):
            lo = 256 * w
            t = P.op("scalar", lambda h, kc=kc, lo=lo, slot=slot: h.copy(
                YBc[slot][:, 257 * kc:257 * (kc + 1)], Y[kc][:, lo:lo + 257]),
                list(extra) + ([ldy[kc], zpad[kc]] if w == 0 else []))
            toks.append(t)
        return toks

    cast_tok = {0: cast_win(0, 0, [wall])}

    prev_resid = [None] * 8
    prev_banks = {"qk": None, "vT": None, "ssq": None, "rb": None, "sim": [None, None],
                  "ptr": None, "av": None, "out": None}
    kr_war = [None, None]

    for w in range(NW):
        slot = w % 2
        c0 = 256 * w
        csl = slice(c0, c0 + 256)
        if w + 1 < NW:
            cast_tok[w + 1] = cast_win(w + 1, (w + 1) % 2,
                                       [prev_resid[kc] for kc in range(8)])
        ycast = cast_tok[w]

        def q_mm(h, slot=slot):
            r = None
            for m in range(8):
                bank, off = divmod(m, 2)
                for kc in range(8):
                    shift = 0 if kc < 4 else 1
                    rhs = YBc[slot][:, 257 * kc + 1 - shift:257 * kc + 257 - shift]
                    r = h.matmul(pb[bank][:, 256 * off:256 * off + 256],
                                 qkvb[:, 3072 * kc + 128 * m:3072 * kc + 128 * (m + 1)],
                                 rhs, start=(kc == 0 and off == 0), stop=(kc == 7),
                                 skip_group_check=True)
            return r
        t_qp = P.op("tensor", q_mm, ycast + [prev_banks["qk"], tabs])

        ssq_tok = None
        qr_tok = []
        for mt in range(8):
            bank, off = divmod(mt, 2)
            qsl = (pb[bank], 256 * off)
            t_s = P.op("scalar", lambda h, qsl=qsl: h.activation(
                T256[:, :], qsl[0][:, qsl[1]:qsl[1] + 256], AF.Square), [t_qp, ssq_tok])
            t_mm = P.op("tensor", lambda h: h.matmul(pb[4][:, 0:256], hmaskb[:, :], T256[:, :],
                                                     start=True, stop=True, skip_group_check=True),
                        [t_s, hm_tok, prev_banks["ssq"]])
            t_r1 = P.op("scalar", lambda h: h.activation(RQN[:, :], pb[4][:, 0:256], AF.Sqrt),
                        [t_mm])
            t_r2 = P.op("vector", lambda h: h.tensor_scalar(RQN[:, :], RQN[:, :], 1e-12, None,
                                                            OP.max), [t_r1])
            t_r3 = P.op("vector", lambda h: h.reciprocal(RQN[:, :], RQN[:, :]), [t_r2])
            t_qn = P.op("vector", lambda h, qsl=qsl: h.tensor_tensor(
                QN[:, :], qsl[0][:, qsl[1]:qsl[1] + 256], RQN[:, :], OP.mult), [t_r3])
            t_sw = P.op("tensor", lambda h: h.matmul(pb[4][:, 256:512], permb[:, :],
                                                     QN[:, :], start=True, stop=True,
                                                     skip_group_check=True),
                        [t_qn, prev_banks["ssq"]])
            t_a = P.op("vector", lambda h, mt=mt, csl=csl: h.tensor_tensor(
                T256[:, :], QN[:, :], AQ[:, csl], OP.mult), [t_qn])
            t_b = P.op("vector", lambda h, mt=mt, csl=csl: h.tensor_tensor(
                SWP[:, :], pb[4][:, 256:512], BQ[:, csl], OP.mult), [t_sw])
            t_qr = P.op("vector", lambda h, mt=mt: h.tensor_tensor(
                QR[:, 256 * mt:256 * (mt + 1)], T256[:, :], SWP[:, :], OP.add), [t_a, t_b])
            ssq_tok = t_qr
            qr_tok.append(t_qr)

        def k_mm(h, slot=slot):
            r = None
            for m in range(8):
                bank, off = divmod(m, 2)
                for kc in range(8):
                    shift = 0 if kc < 4 else 1
                    rhs = YBc[slot][:, 257 * kc + 1 - shift:257 * kc + 257 - shift]
                    r = h.matmul(pb[bank][:, 256 * off:256 * off + 256],
                                 qkvb[:, 3072 * kc + 1024 + 128 * m:3072 * kc + 1024 + 128 * (m + 1)],
                                 rhs, start=(kc == 0 and off == 0), stop=(kc == 7),
                                 skip_group_check=True)
            return r
        t_kp = P.op("tensor", k_mm, [qr_tok[-1]])

        kr_tok = []
        ssq_tok2 = None
        for mt in range(8):
            bank, off = divmod(mt, 2)
            qsl = (pb[bank], 256 * off)
            t_s = P.op("scalar", lambda h, qsl=qsl: h.activation(
                T256[:, :], qsl[0][:, qsl[1]:qsl[1] + 256], AF.Square), [t_kp, ssq_tok2])
            t_mm = P.op("tensor", lambda h: h.matmul(pb[4][:, 0:256], hmaskb[:, :], T256[:, :],
                                                     start=True, stop=True, skip_group_check=True),
                        [t_s, prev_banks["ssq"]])
            t_r1 = P.op("scalar", lambda h: h.activation(RQN[:, :], pb[4][:, 0:256], AF.Sqrt), [t_mm])
            t_r2 = P.op("vector", lambda h: h.tensor_scalar(RQN[:, :], RQN[:, :], 1e-12, None,
                                                            OP.max), [t_r1])
            t_r3 = P.op("vector", lambda h: h.reciprocal(RQN[:, :], RQN[:, :]), [t_r2])
            t_qn = P.op("vector", lambda h, qsl=qsl: h.tensor_tensor(
                QN[:, :], qsl[0][:, qsl[1]:qsl[1] + 256], RQN[:, :], OP.mult), [t_r3])
            t_sw = P.op("tensor", lambda h: h.matmul(pb[4][:, 256:512], permb[:, :],
                                                     QN[:, :], start=True, stop=True,
                                                     skip_group_check=True), [t_qn])
            t_a = P.op("vector", lambda h, csl=csl: h.tensor_tensor(
                T256[:, :], QN[:, :], AK[:, csl], OP.mult), [t_qn])
            t_b = P.op("vector", lambda h, csl=csl: h.tensor_tensor(
                SWP[:, :], pb[4][:, 256:512], BK[:, csl], OP.mult), [t_sw])
            t_kr = P.op("vector", lambda h, mt=mt, slot=slot: h.tensor_tensor(
                KR[slot][:, 256 * mt:256 * (mt + 1)], T256[:, :], SWP[:, :], OP.add),
                [t_a, t_b, kr_war[slot]])
            ssq_tok2 = t_kr
            kr_tok.append(t_kr)

        def vT_mm(h, slot=slot):
            r = None
            for ts_ in range(2):
                for nh in range(2):
                    bank = 2 * ts_ + nh
                    for kc in range(8):
                        shift = 0 if kc < 4 else 1
                        rhs = qkvb[:, 3072 * kc + 2048 + 512 * nh:3072 * kc + 2048 + 512 * (nh + 1)]
                        lhsT = YBc[slot][:, 257 * kc + 1 - shift + 128 * ts_:257 * kc + 1 - shift + 128 * (ts_ + 1)]
                        r = h.matmul(pb[bank][:, 0:512], lhsT, rhs, start=(kc == 0), stop=(kc == 7),
                                     skip_group_check=True)
            return r
        t_vp = P.op("tensor", vT_mm, [kr_tok[-1]])
        t_ve = []
        for ts_ in range(2):
            for nh in range(2):
                bank = 2 * ts_ + nh
                t = P.op("vector", lambda h, bank=bank, ts_=ts_, nh=nh, slot=slot: h.tensor_copy(
                    VT[slot][:, 1024 * ts_ + 512 * nh:1024 * ts_ + 512 * (nh + 1)],
                    pb[bank][:, 0:512]), [t_vp, kr_war[slot]])
                t_ve.append(t)
        vT_done = t_ve[-1]

        ao_tok = []
        sim_war = list(prev_banks["sim"])
        ptr_war = prev_banks["ptr"]
        av_war = prev_banks["av"]
        av_war_pe = None
        for hh in range(16):
            tile = hh // 2
            pofs = 64 * (hh % 2)
            for qs_ in range(2):
                bank = pb[5] if qs_ == 0 else pb[6]
                mvar = 2 if w == 0 else 0
                def sim_mm(h, tile=tile, pofs=pofs, qs_=qs_, bank=bank, mvar=mvar, slot=slot, w=w):
                    r = h.matmul(bank[:, 0:512], identb[:, :],
                                 maskb[:, 512 * (mvar + qs_):512 * (mvar + qs_ + 1)],
                                 start=True, stop=False, skip_group_check=True)
                    lhsT = QR[pofs:pofs + 64, 256 * tile + 128 * qs_:256 * tile + 128 * (qs_ + 1)]
                    if w > 0:
                        r = h.matmul(bank[:, 0:256], lhsT,
                                     KR[1 - slot][pofs:pofs + 64, 256 * tile:256 * (tile + 1)],
                                     start=False, stop=False, skip_group_check=True)
                    r = h.matmul(bank[:, 256:512], lhsT,
                                 KR[slot][pofs:pofs + 64, 256 * tile:256 * (tile + 1)],
                                 start=False, stop=True, skip_group_check=True)
                    return r
                t_sim = P.op("tensor", sim_mm,
                             [qr_tok[tile], kr_tok[tile], sim_war[qs_], wall])
                t_mx = P.op("vector", lambda h, bank=bank, qs_=qs_: h.reduce_max(
                    nbias[:, qs_:qs_ + 1], bank[:, 0:512], axis=AX.X, negate=True), [t_sim])
                t_nb = P.op("vector", lambda h, qs_=qs_: h.tensor_scalar(
                    nbias[:, qs_:qs_ + 1], nbias[:, qs_:qs_ + 1], 8.0, None, OP.mult), [t_mx])
                t_ex = P.op("scalar", lambda h, bank=bank, qs_=qs_: h.activation(
                    PRB[:, :], bank[:, 0:512], AF.Exp, bias=nbias[:, qs_:qs_ + 1], scale=8.0,
                    accum_out=rsum[:, qs_:qs_ + 1]), [t_nb, av_war_pe])
                t_rs = P.op("vector", lambda h, qs_=qs_: h.reciprocal(
                    rsum[:, qs_:qs_ + 1], rsum[:, qs_:qs_ + 1]), [t_ex])
                t_pn = P.op("vector", lambda h, qs_=qs_: h.tensor_scalar(
                    PRB[:, :], PRB[:, :], rsum[:, qs_:qs_ + 1], None, OP.mult), [t_rs])
                def ptr_mm(h):
                    r = None
                    for j in range(4):
                        r = h.transpose(pb[7][:, 64 * j:64 * (j + 1)].bitcast(BF16),
                                        PRB[:, 128 * j:128 * (j + 1)], identb[:, :])
                    return r
                t_tr = P.op("tensor", ptr_mm, [t_pn, ptr_war, av_war])
                t_pte = P.op("vector", lambda h: h.tensor_copy(
                    PRB[:, 0:512], pb[7][:, 0:256].bitcast(BF16)), [t_tr])
                def av_mm(h, tile=tile, pofs=pofs, qs_=qs_, slot=slot, w=w, hh=hh):
                    r = None
                    first = True
                    for kc4 in range(4):
                        vslot = (1 - slot) if kc4 < 2 else slot
                        if w == 0 and kc4 < 2:
                            continue
                        ts_ = kc4 % 2
                        lhsT = VT[vslot][:, 1024 * ts_ + 64 * hh:1024 * ts_ + 64 * (hh + 1)]
                        rhs = PRB[:, 128 * kc4:128 * (kc4 + 1)]
                        r = h.matmul(pb[4][0:64, 0:128], lhsT, rhs, start=first, stop=(kc4 == 3),
                                     skip_group_check=True)
                        first = False
                    return r
                if debug_w is not None and w == debug_w and hh == 0 and qs_ == 0:
                    simE = P.sb("simE", 128, 512)
                    te1 = P.op("vector", lambda h, bank=bank: h.tensor_copy(simE[:, :], bank[:, 0:512]), [t_sim])
                    P._dbg2 = [P.dma("gpsimd", simD[:, :], simE[:, :], [te1]),
                               P.dma("gpsimd", prbD[:, :], PRB[:, :], [t_pn]),
                               ]
                t_av = P.op("tensor", av_mm, [t_pte, vT_done, av_war])
                if debug_w is not None and w == debug_w and hh == 0 and qs_ == 0:
                    P._dbg2.append(P.dma("gpsimd", ptrD[:, :], PRB[:, :], [t_pte, t_av]))
                    avE = P.sb("avE", 64, 128)
                    tav1 = P.op("vector", lambda h: h.tensor_copy(avE[:, :], pb[4][0:64, 0:128]), [t_av])
                    P._dbg2.append(P.dma("gpsimd", avD[:, :], avE[:, :], [tav1]))
                t_ao = P.op("scalar", lambda h, tile=tile, pofs=pofs, qs_=qs_: h.copy(
                    AO[pofs:pofs + 64, 256 * tile + 128 * qs_:256 * tile + 128 * (qs_ + 1)],
                    pb[4][0:64, 0:128]), [t_av])
                ao_tok.append(t_ao)
                sim_war[qs_] = t_ex
                ptr_war = t_pte
                av_war = t_ao
                av_war_pe = t_av

        if debug_w is not None and w == debug_w:
            dbg_toks = []
            dbg_toks.append(P.dma("gpsimd", qrD[:, :], QR[:, :], ao_tok))
            dbg_toks.append(P.dma("gpsimd", krD[:, :], KR[slot][:, :], ao_tok))
            dbg_toks.append(P.dma("gpsimd", vtD[:, :], VT[slot][:, :], ao_tok))
            dbg_toks.append(P.dma("gpsimd", aoD[:, :], AO[:, :], ao_tok))
            dbg_toks.append(P.dma("gpsimd", ybD[:, :], YBc[slot][:, :], ao_tok))
            P._dbg = dbg_toks

        def out_mm(h):
            r = None
            for m in range(8):
                bank, off = divmod(m, 2)
                for kc in range(8):
                    r = h.matmul(pb[bank][:, 256 * off:256 * off + 256],
                                 wo2b[:, 1024 * kc + 128 * m:1024 * kc + 128 * (m + 1)],
                                 AO[:, 256 * kc:256 * (kc + 1)],
                                 start=(kc == 0 and off == 0), stop=(kc == 7),
                                 skip_group_check=True)
            return r
        t_om = P.op("tensor", out_mm, ao_tok + [vT_done])
        res_tok = []
        for m in range(8):
            bank, off = divmod(m, 2)
            t = P.op("vector", lambda h, m=m, bank=bank, off=off, c0=c0: h.tensor_tensor(
                Y[m][:, c0 + 1:c0 + 257], Y[m][:, c0 + 1:c0 + 257],
                pb[bank][:, 256 * off:256 * off + 256], OP.add),
                [t_om] + ([cast_tok[w + 1][m]] if w + 1 < NW else []))
            res_tok.append(t)
        prev_resid = res_tok

        prev_banks = {"qk": res_tok[-1], "vT": res_tok[-1], "ssq": t_kr,
                      "rb": None, "sim": [sim_war[0], sim_war[1]], "ptr": ptr_war,
                      "av": av_war, "out": res_tok[-1]}
        kr_war[0] = t_om
        kr_war[1] = t_om

    fin = []
    for kc in range(8):
        fin.append(P.dma("sync", youtT[128 * kc:128 * (kc + 1), :], Y[kc][:, 1:NT + 1],
                         [prev_resid[kc]]))
    if debug_w is not None:
        fin = fin + P._dbg + getattr(P, "_dbg2", [])
    return P.finish(final_waits=fin)


def _rotary_tables_host():
    DH, WIN = 64, 256
    half = np.arange(0, DH, 2, dtype=np.float64)
    inv_freq = 1.0 / 10000.0 ** (half / DH)
    t = np.arange(NT, dtype=np.float64)
    fh = t[:, None] * inv_freq[None, :]
    freqs = np.concatenate([fh, fh], -1)
    base = (half + 0.4 * DH) / (1.4 * DH)
    power = (t - NT // 2) / (WIN // 2)
    sc = base[None, :] ** power[:, None]
    scale = np.concatenate([sc, sc], -1)
    cos, sin = np.cos(freqs), np.sin(freqs)
    sign = np.where(np.arange(DH) < 32, -1.0, 1.0)[None, :]
    AQt = (cos * scale).T
    BQt = (sign * sin * scale).T
    AKt = (cos / scale).T
    BKt = (sign * sin / scale).T
    def dup(a):
        return np.ascontiguousarray(np.concatenate([a, a], 0).astype(np.float32))
    return dup(AQt), dup(BQt), dup(AKt), dup(BKt)


def _attn_masks_host():
    m = np.zeros((512, 512), np.float32)
    NEG = -1e9
    for qs_ in range(2):
        for i in range(128):
            iloc = qs_ * 128 + i
            for j in range(256):
                if j < iloc:
                    m[qs_ * 128 + i, j] = NEG
                if j > iloc:
                    m[qs_ * 128 + i, 256 + j] = NEG
            m[256 + qs_ * 128 + i, 0:256] = NEG
            m[256 + qs_ * 128 + i, 256:512] = m[qs_ * 128 + i, 256:512]
    return m


def _perm_swap_host():
    p = np.zeros((128, 128), np.float32)
    for i in range(128):
        j = i + 32 if (i % 64) < 32 else i - 32
        p[j, i] = 1.0
    return p


def host_maps_Battn(inputs, y_cur, depth):
    AQt, BQt, AKt, BKt = _rotary_tables_host()
    qs = inputs["L0_ffn_q_scale"][depth]
    ks = inputs["L0_ffn_k_scale"][depth]
    qs128 = np.tile(qs, 2).astype(np.float32)
    ks128 = np.tile(ks, 2).astype(np.float32)
    swap = np.concatenate([np.arange(32, 64), np.arange(0, 32)])
    qs_sw = np.tile(qs[swap], 2).astype(np.float32)
    ks_sw = np.tile(ks[swap], 2).astype(np.float32)
    base = {
        "qkvI": np.ascontiguousarray(inputs["L0_ffn_Wqkv"][depth]),
        "woI": np.ascontiguousarray(inputs["L0_ffn_Wo"][depth]),
        "aqI": AQt, "bqI": BQt, "akI": AKt, "bkI": BKt,
        "qsI": np.stack([qs128, qs_sw], 1).copy(),
        "ksI": np.stack([ks128, ks_sw], 1).copy(),
        "maskI": _attn_masks_host(),
        "identI": np.eye(128, dtype=np.float32),
        "permI": _perm_swap_host(),
    }
    maps = []
    for c in range(NCORE):
        m = dict(base)
        m["yT"] = np.ascontiguousarray(y_cur[c])
        maps.append(m)
    return maps





FB = [512, 512, 512, 512, 512, 170]


def build_launch_Bffn(debug=False, serial=False):
    P = Prog("Bf", serial=serial)
    CHT = 1024

    yT = P.dram("yT", [DM, NT])
    w1I = P.dram("w1I", [DM, 5460])
    w2I = P.dram("w2I", [2730, DM])
    b1aI = P.dram("b1aI", [2730, 1])
    b1gI = P.dram("b1gI", [2730, 1])
    b2I = P.dram("b2I", [DM, 1])
    lngI = P.dram("lngI", [DM, 1])
    lnbI = P.dram("lnbI", [DM, 1])
    youtT = P.dram("youtT", [DM, NT], kind="ExternalOutput")
    if debug:
        xnbD = P.dram("xnbD", [DM, CHT], kind="ExternalOutput")
        outD = P.dram("outD", [DM, CHT], kind="ExternalOutput")
        hbD = P.dram("hbD", [128, 4096], kind="ExternalOutput")

    Y = [P.sb(f"Y{i}", 128, NT + 1) for i in range(8)]
    YC = [P.sb(f"YC{i}", 128, 8 * (CHT + 1), BF16) for i in range(2)]
    XNB = [P.sb(f"XNB{i}", 128, CHT, BF16) for i in range(8)]
    OUT = [P.sb(f"OUT{i}", 128, CHT) for i in range(8)]
    Hb = P.sb("Hb", 128, 4096, BF16)
    GEL = P.sb("GEL", 128, 512, BF16)
    w1aB = P.sb("w1aB", 128, 4096, BF16)
    w1gB = P.sb("w1gB", 128, 4096, BF16)
    w2B = P.sb("w2B", 128, 4096, BF16)
    b1aC = P.sb("b1aC", 128, 22)
    b1gC = P.sb("b1gC", 128, 22)
    b2C = P.sb("b2C", 128, 8)
    lngC = P.sb("lngC", 128, 8)
    lnbC = P.sb("lnbC", 128, 8)
    ones_b = P.sb("ones_b", 128, 128, BF16)
    epscol = P.sb("epscol", 128, 1)
    mean_b = P.sb("mean_b", 128, CHT)
    rstd_b = P.sb("rstd_b", 128, CHT)
    SQ = [P.sb(f"SQ{i}", 128, CHT, BF16) for i in range(2)]
    TMPF = P.sb("TMPF", 128, CHT)
    TMPG = P.sb("TMPG", 128, CHT)

    pb = [P.ps(f"pb{i}", 128, 512) for i in range(8)]

    ldy = [P.dma("sync", Y[i][:, 1:NT + 1], yT[128 * i:128 * (i + 1), :]) for i in range(8)]
    zpad = [P.op("gpsimd", lambda h, i=i: h.memset(Y[i][:, 0:1], 0.0)) for i in range(8)]
    for t in range(22):
        p0 = 128 * t
        p1 = min(p0 + 128, 2730)
        P.dma("sync", b1aC[0:p1 - p0, t:t + 1], b1aI[p0:p1, :])
        P.dma("sync", b1gC[0:p1 - p0, t:t + 1], b1gI[p0:p1, :])
    for t in range(8):
        P.dma("sync", b2C[:, t:t + 1], b2I[128 * t:128 * (t + 1), :])
        P.dma("sync", lngC[:, t:t + 1], lngI[128 * t:128 * (t + 1), :])
        P.dma("sync", lnbC[:, t:t + 1], lnbI[128 * t:128 * (t + 1), :])
    P.op("gpsimd", lambda h: h.memset(ones_b[:, :], 1.0))
    c_eps = P.op("gpsimd", lambda h: h.memset(epscol[:, :], 1e-5))
    wsync = P.tok("sync")
    wg = P.tok("gpsimd")

    cast_tok = {}
    for j in range(2):
        toks = []
        for kc in range(8):
            lo = CHT * j
            t = P.op("scalar", lambda h, kc=kc, lo=lo, j=j: h.copy(
                YC[j][:, (CHT + 1) * kc:(CHT + 1) * (kc + 1)], Y[kc][:, lo:lo + CHT + 1]),
                [ldy[kc], zpad[kc]])
            toks.append(t)
        cast_tok[j] = toks

    def ycs(j, kc):
        base = (CHT + 1) * kc
        sh = 0 if kc < 4 else 1
        return YC[j][:, base + 1 - sh:base + 1 - sh + CHT]

    prev_w2_dma_war = None
    prev_a_read = None
    prev_g_read = None
    prev_w2_read = None
    prev_wmm = None
    stat_war = [None]
    for j in range(2):
        csl = slice(CHT * j, CHT * (j + 1))
        rhs_tiles = [ycs(j, kc) for kc in range(8)]
        stat = emit_ln_stats(P, rhs_tiles,
                             lambda h, kc, dst, j=j: h.activation(dst[:, :], ycs(j, kc), AF.Square),
                             mean_b, rstd_b, pb[0:2], pb[2:4], ones_b,
                             8, CHT, SQ, TMPF, epscol,
                             waits_per_kc=[cast_tok[j][kc] for kc in range(8)],
                             extra_first=[wg, wsync, c_eps] + ([prev_a_read, prev_g_read,
                                                                prev_w2_read] if j else []))
        xnb_tok = []
        for kc in range(8):
            t1 = P.op("vector", lambda h, kc=kc, j=j: h.tensor_tensor(
                TMPF[:, :], ycs(j, kc), mean_b[:, :], OP.subtract), [stat[-1]])
            t2 = P.op("vector", lambda h, kc=kc: h.tensor_tensor(
                TMPG[:, :], TMPF[:, :], rstd_b[:, :], OP.mult), [t1])
            t3 = P.op("vector", lambda h, kc=kc: h.tensor_scalar(
                XNB[kc][:, :], TMPG[:, :], lngC[:, kc:kc + 1], lnbC[:, kc:kc + 1],
                OP.mult, OP.add), [t2])
            xnb_tok.append(t3)
        xnb_all = xnb_tok[-1]

        for b, BW in enumerate(FB):
            NMI = _cdiv(BW, 128)
            wd = []
            for kc in range(8):
                wd.append(P.dma("gpsimd", w1aB[:, 512 * kc:512 * kc + BW],
                                w1I[128 * kc:128 * (kc + 1), 512 * b:512 * b + BW],
                                [prev_wmm]))
                wd.append(P.dma("gpsimd", w1gB[:, 512 * kc:512 * kc + BW],
                                w1I[128 * kc:128 * (kc + 1), 2730 + 512 * b:2730 + 512 * b + BW],
                                [prev_wmm]))
            for ks in range(NMI):
                p0 = 512 * b + 128 * ks
                p1 = min(p0 + 128, 2730)
                wd.append(P.dma("gpsimd", w2B[0:p1 - p0, 1024 * ks:1024 * (ks + 1)],
                                w2I[p0:p1, :], [prev_wmm]))
            wtok = wd[-1]

            for nt in range(2):
                nsl = slice(512 * nt, 512 * (nt + 1))
                def a_mm(h, b=b, BW=BW, nt=nt, NMI=NMI):
                    r = None
                    for mi in range(NMI):
                        mw = min(128, BW - 128 * mi)
                        for kc in range(8):
                            r = h.matmul(pb[mi][0:mw, 0:512],
                                         w1aB[:, 512 * kc + 128 * mi:512 * kc + 128 * mi + mw],
                                         XNB[kc][:, 512 * nt:512 * (nt + 1)],
                                         start=(kc == 0), stop=(kc == 7), skip_group_check=True)
                    return r
                t_amm = P.op("tensor", a_mm, [xnb_all, wtok, prev_a_read])

                def g_mm(h, b=b, BW=BW, nt=nt, NMI=NMI):
                    r = None
                    for mi in range(NMI):
                        mw = min(128, BW - 128 * mi)
                        for kc in range(8):
                            r = h.matmul(pb[4 + mi][0:mw, 0:512],
                                         w1gB[:, 512 * kc + 128 * mi:512 * kc + 128 * mi + mw],
                                         XNB[kc][:, 512 * nt:512 * (nt + 1)],
                                         start=(kc == 0), stop=(kc == 7), skip_group_check=True)
                    return r
                t_gmm = P.op("tensor", g_mm, [xnb_all, wtok, prev_g_read])

                h_tok = []
                prev_th = None
                for mi in range(NMI):
                    mw = min(128, BW - 128 * mi)
                    ct = 4 * b + mi
                    t_gl = P.op("scalar", lambda h, mi=mi, mw=mw, ct=ct: h.activation(
                        GEL[0:mw, :], pb[4 + mi][0:mw, 0:512], AF.Gelu,
                        bias=b1gC[0:mw, ct:ct + 1]), [t_gmm, prev_th])
                    t_ad = P.op("vector", lambda h, mi=mi, mw=mw, ct=ct: h.tensor_scalar(
                        TMPF[0:mw, 0:512], pb[mi][0:mw, 0:512], b1aC[0:mw, ct:ct + 1], None,
                        OP.add), [t_amm, t_gl])
                    t_h = P.op("vector", lambda h, mi=mi, mw=mw: h.tensor_tensor(
                        Hb[0:mw, 1024 * mi + 512 * 0:1024 * mi + 512], TMPF[0:mw, 0:512],
                        GEL[0:mw, :], OP.mult), [t_ad])
                    prev_th = t_h
                    h_tok.append(t_h)
                h_all = h_tok[-1]
                prev_a_read = h_all
                prev_g_read = h_all

                def w2_mm(h, b=b, BW=BW, NMI=NMI):
                    r = None
                    for m in range(8):
                        for ks in range(NMI):
                            kw = min(128, BW - 128 * ks)
                            r = h.matmul(pb[m][:, 0:512],
                                         w2B[0:kw, 1024 * ks + 128 * m:1024 * ks + 128 * (m + 1)],
                                         Hb[0:kw, 1024 * ks:1024 * ks + 512],
                                         start=(ks == 0), stop=(ks == NMI - 1),
                                         skip_group_check=True)
                    return r
                t_w2 = P.op("tensor", w2_mm, [h_all, prev_w2_read, prev_a_read])
                prev_wmm = t_w2

                acc_tok = []
                for m in range(8):
                    if b == 0:
                        t = P.op("vector", lambda h, m=m, nsl=nsl: h.tensor_copy(
                            OUT[m][:, nsl], pb[m][:, 0:512]), [t_w2])
                    else:
                        t = P.op("vector", lambda h, m=m, nsl=nsl: h.tensor_tensor(
                            OUT[m][:, nsl], OUT[m][:, nsl], pb[m][:, 0:512], OP.add), [t_w2])
                    acc_tok.append(t)
                prev_w2_read = acc_tok[-1]
                prev_a_read = acc_tok[-1]
                prev_g_read = acc_tok[-1]

        if debug and j == 0:
            P._dbgf = [P.dma("gpsimd", xnbD[128 * kc:128 * (kc + 1), :], XNB[kc][:, :], [xnb_all])
                       for kc in range(8)]
            P._dbgf += [P.dma("gpsimd", outD[128 * m:128 * (m + 1), :], OUT[m][:, :],
                              [prev_w2_read]) for m in range(8)]
            P._dbgf.append(P.dma("gpsimd", hbD[:, :], Hb[:, :], [prev_w2_read]))

        res_tok = []
        for m in range(8):
            t1 = P.op("vector", lambda h, m=m: h.tensor_scalar(
                OUT[m][:, :], OUT[m][:, :], b2C[:, m:m + 1], None, OP.add), [prev_w2_read])
            t2 = P.op("vector", lambda h, m=m, j=j: h.tensor_tensor(
                Y[m][:, CHT * j + 1:CHT * (j + 1) + 1], Y[m][:, CHT * j + 1:CHT * (j + 1) + 1],
                OUT[m][:, :], OP.add), [t1] + (cast_tok[1] if j == 0 else []))
            res_tok.append(t2)
        prev_a_read = res_tok[-1]
        prev_g_read = res_tok[-1]
        prev_w2_read = res_tok[-1]

    fin = [P.dma("sync", youtT[128 * kc:128 * (kc + 1), :], Y[kc][:, 1:NT + 1],
                 [res_tok[kc]]) for kc in range(8)]
    if debug:
        fin = fin + P._dbgf
    return P.finish(final_waits=fin)


def host_maps_Bffn(inputs, y_cur, depth):
    base = {
        "w1I": np.ascontiguousarray(inputs["L0_ffn_W1"][depth]),
        "w2I": np.ascontiguousarray(inputs["L0_ffn_W2"][depth]),
        "b1aI": np.ascontiguousarray(inputs["L0_ffn_b1"][depth][:2730].reshape(-1, 1)),
        "b1gI": np.ascontiguousarray(inputs["L0_ffn_b1"][depth][2730:].reshape(-1, 1)),
        "b2I": np.ascontiguousarray(inputs["L0_ffn_b2"][depth].reshape(-1, 1)),
        "lngI": np.ascontiguousarray(inputs["L0_ffn_ln_g"][depth].reshape(-1, 1)),
        "lnbI": np.ascontiguousarray(inputs["L0_ffn_ln_b"][depth].reshape(-1, 1)),
    }
    maps = []
    for c in range(NCORE):
        m = dict(base)
        m["yT"] = np.ascontiguousarray(y_cur[c])
        maps.append(m)
    return maps





def build_launch_C():
    P = Prog("C")
    aT = P.dram("aT", [DM, NT])
    bT = P.dram("bT", [DM, NT])
    o1T = P.dram("o1T", [DM, NT], kind="ExternalOutput")
    o2T = P.dram("o2T", [DM, NT], kind="ExternalOutput")

    A_ = [P.sb(f"A{i}", 128, NT) for i in range(8)]
    B_ = [P.sb(f"B{i}", 128, NT, BF16) for i in range(8)]
    Bf = P.sb("Bf", 128, NT)
    SQ = [P.sb(f"SQ{i}", 128, NT, BF16) for i in range(2)]
    TMPF = P.sb("TMPF", 128, NT)
    TMPG = P.sb("TMPG", 128, NT)
    mean_b = P.sb("mean_b", 128, NT)
    rstd_b = P.sb("rstd_b", 128, NT)
    ones_b = P.sb("ones_b", 128, 128, BF16)
    epscol = P.sb("epscol", 128, 1)
    pb = [P.ps(f"pb{i}", 128, 512) for i in range(8)]

    lda = [P.dma("sync", A_[i][:, :], aT[128 * i:128 * (i + 1), :]) for i in range(8)]
    ldb = [P.dma("gpsimd", B_[i][:, :], bT[128 * i:128 * (i + 1), :]) for i in range(8)]
    P.op("gpsimd", lambda h: h.memset(ones_b[:, :], 1.0))
    c_eps = P.op("gpsimd", lambda h: h.memset(epscol[:, :], 1e-5))

    stat = emit_ln_stats(P, B_, lambda h, kc, dst: h.activation(dst[:, :], B_[kc][:, :], AF.Square),
                         mean_b, rstd_b, pb[0:4], pb[4:8], ones_b, 8, NT, SQ, TMPF, epscol,
                         waits_per_kc=[ldb[kc] for kc in range(8)], extra_first=[c_eps])
    x_tok = []
    for kc in range(8):
        t1 = P.op("vector", lambda h, kc=kc: h.tensor_tensor(TMPG[:, :], B_[kc][:, :],
                                                             mean_b[:, :], OP.subtract), [stat[-1]])
        t2 = P.op("vector", lambda h, kc=kc: h.tensor_tensor(TMPG[:, :], TMPG[:, :],
                                                             rstd_b[:, :], OP.mult), [t1])
        t3 = P.op("vector", lambda h, kc=kc: h.tensor_tensor(A_[kc][:, :], A_[kc][:, :],
                                                             TMPG[:, :], OP.add), [t2, lda[kc]])
        t4 = P.dma("sync", o1T[128 * kc:128 * (kc + 1), :], A_[kc][:, :], [t3])
        t5 = P.op("vector", lambda h, kc=kc: h.tensor_copy(B_[kc][:, :], A_[kc][:, :]), [t3])
        x_tok.append((t3, t4, t5))

    stat2 = emit_ln_stats(P, B_, lambda h, kc, dst: h.activation(dst[:, :], A_[kc][:, :], AF.Square),
                          mean_b, rstd_b, pb[0:4], pb[4:8], ones_b, 8, NT, SQ, TMPF, epscol,
                          waits_per_kc=[x_tok[kc][2] for kc in range(8)],
                          extra_first=[P.tok("vector"), P.tok("scalar")])
    fin = []
    dma_prev = [None, None]
    for kc in range(8):
        buf = TMPG if kc % 2 == 0 else TMPF
        t1 = P.op("vector", lambda h, kc=kc, buf=buf: h.tensor_tensor(
            buf[:, :], A_[kc][:, :], mean_b[:, :], OP.subtract), [stat2[-1], dma_prev[kc % 2]])
        t2 = P.op("vector", lambda h, kc=kc, buf=buf: h.tensor_tensor(
            buf[:, :], buf[:, :], rstd_b[:, :], OP.mult), [t1])
        t3 = P.dma("sync", o2T[128 * kc:128 * (kc + 1), :], buf[:, :], [t2])
        dma_prev[kc % 2] = t3
        fin.append(t3)
    return P.finish(final_waits=fin + [t[1] for t in x_tok])


def host_maps_C(a_cur, b_cur):
    return [{"aT": np.ascontiguousarray(a_cur[c]), "bT": np.ascontiguousarray(b_cur[c])}
            for c in range(NCORE)]







def build_launch_K2a(serial=False, debug=False):
    P = Prog("K2", serial=serial)
    NTOK = 2048
    ln2sI = P.dram("ln2sI", [8192, 256])
    qwI = P.dram("qwI", [DM, DM])
    qwTI = P.dram("qwTI", [DM, DM])
    vwI = P.dram("vwI", [DM, DM])
    owTI = P.dram("owTI", [DM, DM])
    obI = P.dram("obI", [DM, 1])
    m16I = P.dram("m16I", [128, 128])
    emI = P.dram("emI", [16, 1024])
    identI = P.dram("identI", [128, 128])
    if debug:
        qD = P.dram("qD", [128, NTOK], kind="ExternalOutput")
        x3aD = P.dram("x3aD", [128, NTOK], kind="ExternalOutput")
        ln3aD = P.dram("ln3aD", [128, NTOK], kind="ExternalOutput")
        shpD = P.dram("shpD", [128, NTOK], kind="ExternalOutput")
        mbD = P.dram("mbD", [128, NTOK], kind="ExternalOutput")
        rbD = P.dram("rbD", [128, NTOK], kind="ExternalOutput")

        bD = P.dram("bD", [128, 8192], kind="ExternalOutput")
        qkD = P.dram("qkD", [16, 16384], kind="ExternalOutput")
        pnD = P.dram("pnD", [16, 16384], kind="ExternalOutput")
        xtD = P.dram("xtD", [128, NTOK], kind="ExternalOutput")
        pxD = P.dram("pxD", [128, NTOK], kind="ExternalOutput")
        a3D = P.dram("a3D", [128, NTOK], kind="ExternalOutput")

    LNB = [P.sb(f"LNB{i}", 128, NTOK, BF16) for i in range(8)]
    XT = P.sb("XT", 128, NTOK, BF16)
    wb1 = P.sb("wb1", 128, 8192, BF16)
    wb2 = P.sb("wb2", 128, 8192, BF16)
    wb3 = P.sb("wb3", 128, 8192, BF16)
    CCb = P.sb("CCb", 128, 512, BF16)
    m16b = P.sb("m16b", 128, 128, BF16)
    emb = P.sb("emb", 16, 1024, BF16)
    identbb = P.sb("identbb", 128, 128, BF16)
    PMB = P.sb("PMB", 128, 256, BF16)
    PB = P.sb("PB", 128, 256, BF16)
    ACC = P.sb("ACC", 128, 256)
    obC = P.sb("obC", 128, 8)
    ones_b = P.sb("ones_b", 128, 128, BF16)
    epscol = P.sb("epscol", 128, 1)
    QB = [P.sb(f"QB{i}", 128, NTOK, BF16) for i in range(8)]
    Bb = P.sb("Bb", 128, 8192, BF16)
    QK = P.sb("QK", 16, 16384, BF16)
    EXS = P.sb("EXS", 16, 2048)
    DIAG = P.sb("DIAG", 16, 2048, BF16)
    MXS = P.sb("MXS", 16, 512)
    TMPF = P.sb("TMPF", 128, 512)

    pb = [P.ps(f"pb{i}", 128, 512) for i in range(8)]

    ln_ld = []
    for kc in range(8):
        for b in range(8):
            ln_ld.append(P.dma("gpsimd", LNB[kc][:, 256 * b:256 * (b + 1)],
                               ln2sI[1024 * b + 128 * kc:1024 * b + 128 * (kc + 1), :]))
    for kc in range(8):
        P.dma("gpsimd", wb1[:, 1024 * kc:1024 * (kc + 1)], qwI[128 * kc:128 * (kc + 1), :])
        P.dma("gpsimd", wb2[:, 1024 * kc:1024 * (kc + 1)], qwTI[128 * kc:128 * (kc + 1), :])
        P.dma("gpsimd", wb3[:, 1024 * kc:1024 * (kc + 1)], vwI[128 * kc:128 * (kc + 1), :])
    P.dma("gpsimd", m16b[:, :], m16I[:, :])
    P.dma("gpsimd", emb[:, :], emI[:, :])
    P.dma("gpsimd", identbb[:, :], identI[:, :])
    for t in range(8):
        P.dma("sync", obC[:, t:t + 1], obI[128 * t:128 * (t + 1), :])
    P.op("gpsimd", lambda h: h.memset(ones_b[:, :], 1.0))
    c_eps = P.op("gpsimd", lambda h: h.memset(epscol[:, :], 1e-5))
    wall = P.tok("gpsimd")
    wsync = P.tok("sync")

    qev = None
    for nc4 in range(4):
        nsl = slice(512 * nc4, 512 * (nc4 + 1))
        def q_mm(h, nsl=nsl):
            r = None
            for m in range(8):
                for kc in range(8):
                    r = h.matmul(pb[m][:, 0:512],
                                 wb1[:, 1024 * kc + 128 * m:1024 * kc + 128 * (m + 1)],
                                 LNB[kc][:, nsl], start=(kc == 0), stop=(kc == 7),
                                 skip_group_check=True)
            return r
        t_q = P.op("tensor", q_mm, [wall, qev])
        for m in range(8):
            qev = P.op("vector", lambda h, m=m, nsl=nsl: h.tensor_copy(QB[m][:, nsl],
                                                                       pb[m][:, 0:512]), [t_q])

    def C_mm(h):
        r = None
        for hh in range(16):
            po = 64 * (hh % 2)
            co = 64 * (hh // 2)
            for kc in range(8):
                r = h.matmul(pb[0][po:po + 64, co:co + 64],
                             wb1[:, 1024 * kc + 64 * hh:1024 * kc + 64 * (hh + 1)],
                             wb3[:, 1024 * kc + 64 * hh:1024 * kc + 64 * (hh + 1)],
                             start=(kc == 0 and hh < 2), stop=(kc == 7),
                             skip_group_check=True)
        return r
    t_C = P.op("tensor", C_mm, [wall, qev])
    t_Ce = P.op("scalar", lambda h: h.activation(CCb[:, :], pb[0][:, 0:512], AF.Copy,
                                                 scale=0.125), [t_C])
    bev = None
    for hh in range(16):
        po = 64 * (hh % 2)
        co = 64 * (hh // 2)
        rt = hh // 2
        def Bmm1(h, po=po, co=co, rt=rt):
            r = None
            for mc in range(8):
                r = h.matmul(pb[1 + mc % 4][po:po + 64 if False else 0:128, 128 * (mc // 4):128 * (mc // 4) + 64]
                             if False else pb[1 + mc % 4][:, 128 * (mc // 4):128 * (mc // 4) + 64],
                             wb2[po:po + 64, 1024 * rt + 128 * mc:1024 * rt + 128 * (mc + 1)],
                             CCb[po:po + 64, co:co + 64],
                             start=True, stop=True, skip_group_check=True)
            return r
        t_B = P.op("tensor", Bmm1, [t_Ce, bev])
        def Bev(h, hh=hh):
            r = None
            for mc in range(8):
                r = h.copy(Bb[:, 1024 * mc + 64 * hh:1024 * mc + 64 * (hh + 1)],
                           pb[1 + mc % 4][:, 128 * (mc // 4):128 * (mc // 4) + 64])
            return r
        bev = P.op("scalar", Bev, [t_B])
    b_done = bev

    pm_war = None
    qk_war = None
    for t in range(8):
        for s_ in range(t, 8):
            mm_t = None
            for kc in range(8):
                t_pm = P.op("vector", lambda h, kc=kc, t=t, s_=s_: h.tensor_tensor(
                    PMB[:, :], QB[kc][:, 256 * t:256 * (t + 1)],
                    QB[kc][:, 256 * s_:256 * (s_ + 1)], OP.mult), [pm_war, qev])
                mm_t = P.op("tensor", lambda h, kc=kc: h.matmul(
                    pb[2][0:16, 0:256], m16b[:, 16 * kc:16 * (kc + 1)], PMB[:, :],
                    start=(kc == 0), stop=(kc == 7), skip_group_check=True),
                    [t_pm, qk_war, b_done])
                pm_war = mm_t
            t_qke = P.op("scalar", lambda h, t=t, s_=s_: h.copy(
                QK[:, 256 * (8 * t + s_):256 * (8 * t + s_ + 1)], pb[2][0:16, 0:256]), [mm_t])
            if s_ != t:
                t_qke = P.op("scalar", lambda h, t=t, s_=s_: h.copy(
                    QK[:, 256 * (8 * s_ + t):256 * (8 * s_ + t + 1)],
                    QK[:, 256 * (8 * t + s_):256 * (8 * t + s_ + 1)]), [t_qke])
            qk_war = t_qke
    qk_done = qk_war
    if debug:
        P._d1 = [P.dma("gpsimd", qD[:, :], QB[0][:, :], [qk_done]),
                 P.dma("gpsimd", bD[:, :], Bb[:, :], [qk_done]),
                 P.dma("gpsimd", qkD[:, :], QK[:, :], [qk_done])]

    dsave = []
    for s_ in range(8):
        dsave.append(P.op("vector", lambda h, s_=s_: h.tensor_copy(
            DIAG[:, 256 * s_:256 * (s_ + 1)], QK[:, 256 * (8 * s_ + s_):256 * (8 * s_ + s_ + 1)]),
            [qk_done]))
    diag_done = dsave[-1]
    pn_tok = {}
    for t in range(8):
        lg = []
        for s_ in range(8):
            t1 = P.op("vector", lambda h, t=t, s_=s_: h.scalar_tensor_tensor(
                EXS[:, 256 * s_:256 * (s_ + 1)], QK[:, 256 * (8 * t + s_):256 * (8 * t + s_ + 1)],
                2.0, DIAG[:, 256 * t:256 * (t + 1)], OP.mult, OP.subtract),
                [qk_done, diag_done])
            t2 = P.op("vector", lambda h, t=t, s_=s_: h.tensor_tensor(
                EXS[:, 256 * s_:256 * (s_ + 1)], EXS[:, 256 * s_:256 * (s_ + 1)],
                DIAG[:, 256 * s_:256 * (s_ + 1)], OP.subtract), [t1])
            lg.append(t2)
        t_m = P.op("vector", lambda h: h.tensor_tensor(MXS[:, 0:256], EXS[:, 0:256],
                                                       EXS[:, 256:512], OP.max), [lg[0], lg[1]])
        for s_ in range(2, 8):
            t_m = P.op("vector", lambda h, s_=s_: h.tensor_tensor(
                MXS[:, 0:256], MXS[:, 0:256], EXS[:, 256 * s_:256 * (s_ + 1)], OP.max),
                [t_m, lg[s_]])
        ex = []
        for s_ in range(8):
            t1 = P.op("vector", lambda h, s_=s_: h.tensor_tensor(
                EXS[:, 256 * s_:256 * (s_ + 1)], EXS[:, 256 * s_:256 * (s_ + 1)], MXS[:, 0:256],
                OP.subtract), [t_m])
            t2 = P.op("scalar", lambda h, s_=s_: h.activation(
                EXS[:, 256 * s_:256 * (s_ + 1)], EXS[:, 256 * s_:256 * (s_ + 1)], AF.Exp,
                scale=0.125), [t1])
            ex.append(t2)
        t_sm = P.op("vector", lambda h: h.tensor_tensor(MXS[:, 256:512], EXS[:, 0:256],
                                                        EXS[:, 256:512], OP.add), [ex[0], ex[1]])
        for s_ in range(2, 8):
            t_sm = P.op("vector", lambda h, s_=s_: h.tensor_tensor(
                MXS[:, 256:512], MXS[:, 256:512], EXS[:, 256 * s_:256 * (s_ + 1)], OP.add),
                [t_sm, ex[s_]])
        t_rc = P.op("vector", lambda h: h.reciprocal(MXS[:, 256:512], MXS[:, 256:512]), [t_sm])
        for s_ in range(8):
            pn_tok[(t, s_)] = P.op("vector", lambda h, t=t, s_=s_: h.tensor_tensor(
                QK[:, 256 * (8 * t + s_):256 * (8 * t + s_ + 1)], EXS[:, 256 * s_:256 * (s_ + 1)],
                MXS[:, 256:512], OP.mult), [t_rc])
    pn_done = pn_tok[(7, 7)]
    if debug:
        P._d1.append(P.dma("gpsimd", pnD[:, :], QK[:, :], [pn_done]))

    pb_war = None
    px_tok = []
    xt_ev = None
    for tile in range(8):
        for nc4 in range(4):
            def xmm(h, tile=tile, nc4=nc4):
                r = None
                for kc in range(8):
                    r = h.matmul(pb[4][:, 0:512],
                                 Bb[:, 1024 * kc + 128 * tile:1024 * kc + 128 * (tile + 1)],
                                 LNB[kc][:, 512 * nc4:512 * (nc4 + 1)], start=(kc == 0),
                                 stop=(kc == 7), skip_group_check=True)
                return r
            t_x = P.op("tensor", xmm, [b_done, xt_ev] + ([pn_done] if tile == 0 and nc4 == 0 else []))
            xt_ev = P.op("vector", lambda h, nc4=nc4: h.tensor_copy(
                XT[:, 512 * nc4:512 * (nc4 + 1)], pb[4][:, 0:512]), [t_x])
        ap_last = None
        for t in range(8):
            acc_t = None
            for s_ in range(8):
                t_bc = P.op("tensor", lambda h, t=t, s_=s_, tile=tile: h.matmul(
                    pb[5][:, 0:256], emb[:, 128 * tile:128 * (tile + 1)],
                    QK[:, 256 * (8 * t + s_):256 * (8 * t + s_ + 1)],
                    start=True, stop=True, skip_group_check=True),
                    [pn_tok[(t, s_)], pb_war])
                t_pb = P.op("scalar", lambda h: h.copy(PB[:, :], pb[5][:, 0:256]), [t_bc])
                pb_war = t_pb
                if s_ == 0:
                    acc_t = P.op("vector", lambda h, s_=s_: h.tensor_tensor(
                        ACC[:, :], PB[:, :], XT[:, 256 * s_:256 * (s_ + 1)], OP.mult),
                        [t_pb, xt_ev])
                else:
                    t_p = P.op("vector", lambda h, s_=s_: h.tensor_tensor(
                        TMPF[:, 0:256], PB[:, :], XT[:, 256 * s_:256 * (s_ + 1)], OP.mult),
                        [t_pb])
                    acc_t = P.op("vector", lambda h: h.tensor_tensor(
                        ACC[:, :], ACC[:, :], TMPF[:, 0:256], OP.add), [t_p, acc_t])
            ap_last = P.op("vector", lambda h, t=t, tile=tile: h.tensor_copy(
                QB[tile][:, 256 * t:256 * (t + 1)], ACC[:, :]), [acc_t])
        if debug and tile == 0:
            P._d1.append(P.dma("gpsimd", xtD[:, :], XT[:, :], [ap_last]))
            P._d1.append(P.dma("gpsimd", pxD[:, :], QB[0][:, :], [ap_last]))
        px_tok.append(ap_last)
    px_done = px_tok[-1]


    ow_ld = [P.dma("gpsimd", wb1[:, 1024 * kc:1024 * (kc + 1)], owTI[128 * kc:128 * (kc + 1), :],
                   [P.tok("tensor"), qk_done]) for kc in range(8)]
    ow_all = ow_ld[-1]
    at_ev = None
    for nc4 in [1, 2, 3, 0]:
        nsl = slice(512 * nc4, 512 * (nc4 + 1))
        def o_mm(h, nsl=nsl):
            r = None
            for m in range(8):
                for kc in range(8):
                    r = h.matmul(pb[m][:, 0:512],
                                 wb1[:, 1024 * kc + 128 * m:1024 * kc + 128 * (m + 1)],
                                 QB[kc][:, nsl], start=(kc == 0), stop=(kc == 7),
                                 skip_group_check=True)
            return r
        t_o = P.op("tensor", o_mm, [ow_all, px_done, at_ev, wsync])
        for m in range(8):
            at_ev = P.op("scalar", lambda h, m=m, nsl=nsl: h.activation(
                LNB[m][:, nsl], pb[m][:, 0:512], AF.Identity, bias=obC[:, m:m + 1]), [t_o])
    attn_done = at_ev
    if debug:
        P._d1.append(P.dma("gpsimd", a3D[:, :], LNB[0][:, :], [attn_done]))
        attn_done = P._d1[-1]


    attnO = P.dram("attnO", [DM, NTOK], BF16, kind="ExternalOutput")
    fin = [P.dma("gpsimd", attnO[128 * kc:128 * (kc + 1), :], LNB[kc][:, :], [attn_done])
           for kc in range(8)]
    if debug:
        fin = fin + P._d1
    return P.finish(final_waits=fin)

def host_maps_K2a(inputs, ln2_full):
    qw = inputs["L1_attn_q_weight"].reshape(DM, DM)
    vw = inputs["L1_attn_v_weight"].reshape(DM, DM)
    m16 = np.zeros((128, 128), np.float32)
    for tile in range(8):
        for k in range(128):
            m16[k, 16 * tile + 2 * tile + k // 64] = 1.0
    em = np.zeros((16, 1024), np.float32)
    for tile in range(8):
        for c in range(128):
            em[2 * tile + c // 64, 128 * tile + c] = 1.0
    base = {
        "qwI": np.ascontiguousarray(qw), "qwTI": np.ascontiguousarray(qw.T),
        "vwI": np.ascontiguousarray(vw),
        "owTI": np.ascontiguousarray(inputs["L1_attn_out_w"].T),
        "obI": np.ascontiguousarray(inputs["L1_attn_out_b"].reshape(-1, 1)),
        "m16I": m16, "emI": em, "identI": np.eye(128, dtype=np.float32),
    }
    maps = []
    for k in range(NCORE):
        m = dict(base)
        sl = slice(256 * k, 256 * (k + 1))
        m["ln2sI"] = np.ascontiguousarray(np.concatenate([ln2_full[b][:, sl] for b in range(8)], 0))
        maps.append(m)
    return maps


def host_maps_K2b(inputs, x2_full, attn_parts):
    base = {
        "sw1TI": np.ascontiguousarray(inputs["L1_ffn_sw1"].T),
        "sw2TI": np.ascontiguousarray(inputs["L1_ffn_sw2"].T),
        "sw3TI": np.ascontiguousarray(inputs["L1_ffn_sw3"].T),
        "sb1I": np.ascontiguousarray(inputs["L1_ffn_sb1"].reshape(-1, 1)),
        "sb2I": np.ascontiguousarray(inputs["L1_ffn_sb2"].reshape(-1, 1)),
        "sb3I": np.ascontiguousarray(inputs["L1_ffn_sb3"].reshape(-1, 1)),
        "slngI": np.ascontiguousarray(inputs["L1_ffn_s_ln_g"].reshape(-1, 1)),
        "slnbI": np.ascontiguousarray(inputs["L1_ffn_s_ln_b"].reshape(-1, 1)),
        "identI": np.eye(128, dtype=np.float32),
    }
    maps = []
    for k in range(NCORE):
        m = dict(base)
        sl = slice(256 * k, 256 * (k + 1))
        m["x2sI"] = np.ascontiguousarray(np.concatenate([x2_full[b][:, sl] for b in range(8)], 0))
        m["attnI"] = np.ascontiguousarray(attn_parts[k])
        maps.append(m)
    return maps


def build_launch_K2b(serial=False):
    P = Prog("K2b", serial=serial)
    NTOK = 2048
    attnI = P.dram("attnI", [DM, NTOK], BF16)
    x2sI = P.dram("x2sI", [8192, 256])
    sw1TI = P.dram("sw1TI", [DM, 512])
    sw2TI = P.dram("sw2TI", [DM, 512])
    sw3TI = P.dram("sw3TI", [512, DM])
    sb1I = P.dram("sb1I", [512, 1])
    sb2I = P.dram("sb2I", [512, 1])
    sb3I = P.dram("sb3I", [DM, 1])
    slngI = P.dram("slngI", [512, 1])
    slnbI = P.dram("slnbI", [512, 1])
    identI = P.dram("identI", [128, 128])
    x3O = P.dram("x3O", [NTOK, DM], kind="ExternalOutput")
    ln3O = P.dram("ln3O", [NTOK, DM], BF16, kind="ExternalOutput")
    shO = P.dram("shO", [NTOK, DM], BF16, kind="ExternalOutput")

    LNB = [P.sb(f"LNB{i}", 128, NTOK, BF16) for i in range(8)]
    X3 = [P.sb(f"X3{i}", 128, NTOK) for i in range(8)]
    sw1b = P.sb("sw1b", 128, 4096, BF16)
    sw2b = P.sb("sw2b", 128, 4096, BF16)
    sw3b = P.sb("sw3b", 128, 4096, BF16)
    sb1C = P.sb("sb1C", 128, 4)
    sb2C = P.sb("sb2C", 128, 4)
    sb3C = P.sb("sb3C", 128, 8)
    slngC = P.sb("slngC", 128, 4)
    slnbC = P.sb("slnbC", 128, 4)
    identb = P.sb("identb", 128, 128)
    identbb = P.sb("identbb", 128, 128, BF16)
    ones_b = P.sb("ones_b", 128, 128, BF16)
    epscol = P.sb("epscol", 128, 1)
    mean_b = P.sb("mean_b", 128, NTOK)
    rstd_b = P.sb("rstd_b", 128, NTOK)
    SQ = [P.sb(f"SQ{i}", 128, NTOK, BF16) for i in range(2)]
    TMPF = P.sb("TMPF", 128, NTOK)
    TMPG = TMPF
    x2b = P.sb("x2b", 128, NTOK)
    X1S = [P.sb(f"X1S{i}", 128, NTOK, BF16) for i in range(4)]
    SHB = [P.sb(f"SHB{i}", 128, NTOK, BF16) for i in range(4)]
    TNB = [P.sb(f"TNB{i}", 128, 1024) for i in range(2)]
    TNBb = [P.sb(f"TNBb{i}", 128, 1024, BF16) for i in range(2)]
    T256b = P.sb("T256b", 128, 256, BF16)

    pb = [P.ps(f"pb{i}", 128, 512) for i in range(8)]

    ld_at = [P.dma("sync", LNB[kc][:, :], attnI[128 * kc:128 * (kc + 1), :]) for kc in range(8)]
    for kc in range(8):
        P.dma("gpsimd", sw1b[:, 512 * kc:512 * (kc + 1)], sw1TI[128 * kc:128 * (kc + 1), :])
        P.dma("gpsimd", sw2b[:, 512 * kc:512 * (kc + 1)], sw2TI[128 * kc:128 * (kc + 1), :])
    for kc in range(4):
        P.dma("gpsimd", sw3b[:, 1024 * kc:1024 * (kc + 1)], sw3TI[128 * kc:128 * (kc + 1), :])
    P.dma("gpsimd", identbb[:, :], identI[:, :])
    P.dma("sync", identb[:, :], identI[:, :])
    for t in range(8):
        P.dma("sync", sb3C[:, t:t + 1], sb3I[128 * t:128 * (t + 1), :])
    for t in range(4):
        P.dma("sync", sb1C[:, t:t + 1], sb1I[128 * t:128 * (t + 1), :])
        P.dma("sync", sb2C[:, t:t + 1], sb2I[128 * t:128 * (t + 1), :])
        P.dma("sync", slngC[:, t:t + 1], slngI[128 * t:128 * (t + 1), :])
        P.dma("sync", slnbC[:, t:t + 1], slnbI[128 * t:128 * (t + 1), :])
    P.op("gpsimd", lambda h: h.memset(ones_b[:, :], 1.0))
    c_eps = P.op("gpsimd", lambda h: h.memset(epscol[:, :], 1e-5))
    wall = P.tok("gpsimd")
    wsync = P.tok("sync")

    stat = emit_ln_stats(P, LNB, lambda h, kc, dst: h.activation(dst[:, :], LNB[kc][:, :],
                                                                 AF.Square),
                         mean_b, rstd_b, pb[0:4], pb[4:8], ones_b, 8, NTOK, SQ, TMPF, epscol,
                         waits_per_kc=[ld_at[kc] for kc in range(8)],
                         extra_first=[c_eps, wall, wsync])
    x3_tok = []
    prevx2 = None
    for kc in range(8):
        lds = [P.dma("sync", x2b[:, 256 * b:256 * (b + 1)],
                     x2sI[1024 * b + 128 * kc:1024 * b + 128 * (kc + 1), :],
                     ([prevx2] if b == 0 else [])) for b in range(8)]
        t1 = P.op("vector", lambda h, kc=kc: h.tensor_tensor(TMPG[:, :], LNB[kc][:, :],
                                                             mean_b[:, :], OP.subtract),
                  [stat[-1]])
        t2 = P.op("vector", lambda h: h.tensor_tensor(TMPG[:, :], TMPG[:, :], rstd_b[:, :],
                                                      OP.mult), [t1])
        t3 = P.op("vector", lambda h, kc=kc: h.tensor_tensor(X3[kc][:, :], x2b[:, :], TMPG[:, :],
                                                             OP.add), [t2, lds[-1]])
        prevx2 = t3
        t5 = P.op("vector", lambda h, kc=kc: h.tensor_copy(LNB[kc][:, :], X3[kc][:, :]), [t3])
        x3_tok.append((t3, t5))
    stat2 = emit_ln_stats(P, LNB, lambda h, kc, dst: h.activation(dst[:, :], X3[kc][:, :],
                                                                  AF.Square),
                          mean_b, rstd_b, pb[0:4], pb[4:8], ones_b, 8, NTOK, SQ, TMPF, epscol,
                          waits_per_kc=[x3_tok[kc][1] for kc in range(8)],
                          extra_first=[P.tok("vector"), P.tok("scalar")])
    ln3_tok = []
    for kc in range(8):
        t1 = P.op("vector", lambda h, kc=kc: h.tensor_tensor(TMPG[:, :], X3[kc][:, :],
                                                             mean_b[:, :], OP.subtract),
                  [stat2[-1]])
        t2 = P.op("vector", lambda h, kc=kc: h.tensor_tensor(LNB[kc][:, :], TMPG[:, :],
                                                             rstd_b[:, :], OP.mult), [t1])
        ln3_tok.append(t2)
    ln3_done = ln3_tok[-1]

    tn_toks = []
    tr_war = None
    for tt in range(16):
        tsl = slice(128 * tt, 128 * (tt + 1))
        buf = TNB[tt % 2]
        bufb = TNBb[tt % 2]
        last = None
        war2 = [tn_toks[2 * (tt - 2)], tn_toks[2 * (tt - 2) + 1]] if tt >= 2 else []
        if tt == 0:
            war2 = war2 + [stat2[-1], ln3_done]
        for kc in range(8):
            t_t = P.op("tensor", lambda h, kc=kc, tsl=tsl: h.transpose(
                pb[6][:, 0:128], X3[kc][:, tsl], identb[:, :]),
                [x3_tok[kc][0], tr_war, last] + (war2 if kc == 0 else []))
            last = P.op("vector", lambda h, kc=kc, buf=buf: h.tensor_copy(
                buf[:, 128 * kc:128 * (kc + 1)], pb[6][:, 0:128]), [t_t])
            t_tb = P.op("tensor", lambda h, kc=kc, tsl=tsl: h.transpose(
                pb[7][:, 0:64].bitcast(BF16), LNB[kc][:, tsl], identbb[:, :]),
                [ln3_tok[kc], last])
            last = P.op("vector", lambda h, kc=kc, bufb=bufb: h.tensor_copy(
                bufb[:, 128 * kc:128 * (kc + 1)], pb[7][:, 0:64].bitcast(BF16)), [t_tb])
        tr_war = last
        tn_toks.append(P.dma("sync", x3O[tsl, :], buf[:, :], [last]))
        tn_toks.append(P.dma("gpsimd", ln3O[tsl, :], bufb[:, :], [last]))

    shp_tok = []
    x1ev = None
    for nc4 in range(4):
        nsl = slice(512 * nc4, 512 * (nc4 + 1))
        def s1_mm(h, nsl=nsl):
            r = None
            for m in range(4):
                for kc in range(8):
                    r = h.matmul(pb[m][:, 0:512],
                                 sw1b[:, 512 * kc + 128 * m:512 * kc + 128 * (m + 1)],
                                 LNB[kc][:, nsl], start=(kc == 0), stop=(kc == 7),
                                 skip_group_check=True)
            return r
        t_s1 = P.op("tensor", s1_mm, [wall, ln3_done, x1ev, tn_toks[-1], tn_toks[-2]])
        def s2_mm(h, nsl=nsl):
            r = None
            for m in range(4):
                for kc in range(8):
                    r = h.matmul(pb[4 + m][:, 0:512],
                                 sw2b[:, 512 * kc + 128 * m:512 * kc + 128 * (m + 1)],
                                 LNB[kc][:, nsl], start=(kc == 0), stop=(kc == 7),
                                 skip_group_check=True)
            return r
        t_s2 = P.op("tensor", s2_mm, [wall, ln3_done, x1ev])
        for m in range(4):
            t_si = P.op("scalar", lambda h, m=m, nsl=nsl: h.activation(
                X1S[m][:, nsl], pb[m][:, 0:512], AF.Silu, bias=sb1C[:, m:m + 1]), [t_s1])
            t_b2 = P.op("vector", lambda h, m=m, nsl=nsl: h.tensor_scalar(
                TMPF[:, 0:512], pb[4 + m][:, 0:512], sb2C[:, m:m + 1], None, OP.add),
                [t_s2, t_si])
            x1ev = P.op("vector", lambda h, m=m, nsl=nsl: h.tensor_tensor(
                SHB[m][:, nsl], X1S[m][:, nsl], TMPF[:, 0:512], OP.mult), [t_b2])
        shp_tok.append(x1ev)
    shp_done = x1ev
    statS = emit_ln_stats(P, SHB, lambda h, kc, dst: h.activation(dst[:, :], SHB[kc][:, :],
                                                                  AF.Square),
                          mean_b, rstd_b, pb[0:4], pb[4:8], ones_b, 4, NTOK, SQ, TMPF, epscol,
                          waits_per_kc=[shp_done] * 4,
                          extra_first=[P.tok("vector"), P.tok("scalar")])
    shn_tok = []
    for kc in range(4):
        t1 = P.op("vector", lambda h, kc=kc: h.tensor_tensor(TMPG[:, :], SHB[kc][:, :],
                                                             mean_b[:, :], OP.subtract),
                  [statS[-1]])
        t2 = P.op("vector", lambda h, kc=kc: h.tensor_tensor(TMPG[:, :], TMPG[:, :],
                                                             rstd_b[:, :], OP.mult), [t1])
        t3 = P.op("vector", lambda h, kc=kc: h.tensor_scalar(
            SHB[kc][:, :], TMPG[:, :], slngC[:, kc:kc + 1], slnbC[:, kc:kc + 1],
            OP.mult, OP.add), [t2])
        shn_tok.append(t3)
    shn_done = shn_tok[-1]

    sh_fin = []
    ev_war = None
    for tt in range(16):
        tsl = slice(128 * tt, 128 * (tt + 1))
        bufb = TNBb[tt % 2]
        last = None
        for m in range(8):
            def sh_mm(h, m=m, tsl=tsl):
                r = None
                for kc in range(4):
                    r = h.matmul(pb[2][:, 0:128],
                                 sw3b[:, 1024 * kc + 128 * m:1024 * kc + 128 * (m + 1)],
                                 SHB[kc][:, tsl], start=(kc == 0), stop=(kc == 3),
                                 skip_group_check=True)
                return r
            t_m = P.op("tensor", sh_mm, [shn_done, ev_war, last,
                                         (sh_fin[-2] if len(sh_fin) >= 2 else
                                          (tn_toks[-1] if tn_toks else None))])
            t_e = P.op("vector", lambda h, m=m: h.tensor_scalar(
                T256b[:, 0:128], pb[2][:, 0:128], sb3C[:, m:m + 1], None, OP.add), [t_m])
            t_t = P.op("tensor", lambda h: h.transpose(pb[3][:, 0:64].bitcast(BF16),
                                                       T256b[:, 0:128], identbb[:, :]), [t_e])
            last = P.op("vector", lambda h, m=m, bufb=bufb: h.tensor_copy(
                bufb[:, 128 * m:128 * (m + 1)], pb[3][:, 0:64].bitcast(BF16)), [t_t])
            ev_war = last
        sh_fin.append(P.dma("gpsimd", shO[tsl, :], bufb[:, :], [last]))

    return P.finish(final_waits=tn_toks + sh_fin)





def build_launch_K3(serial=False, stop_after=None):
    P = Prog("K3", serial=serial)
    ln3F = P.dram("ln3F", [NB * NT, DM], BF16)
    shF = P.dram("shF", [NB * NT, DM], BF16)
    idxbI = P.dram("idxbI", [256, 1], I32)
    idxtI = P.dram("idxtI", [256, 1], I32)
    gwI = P.dram("gwI", [DM, 9])
    gb9I = P.dram("gb9I", [1, 9])
    expbI = P.dram("expbI", [1, 8])
    ew1TI = P.dram("ew1TI", [8 * DM, 512])
    ew2TI = P.dram("ew2TI", [8 * DM, 512])
    ew3TI = P.dram("ew3TI", [8 * 512, DM])
    eb1I = P.dram("eb1I", [8 * 512, 1])
    eb2I = P.dram("eb2I", [8 * 512, 1])
    eb3I = P.dram("eb3I", [8 * DM, 1])
    selgI = P.dram("selgI", [512, 1])
    selbI = P.dram("selbI", [512, 1])
    identI = P.dram("identI", [128, 128])
    moeO = P.dram("moeO", [256, DM], kind="ExternalOutput")

    XM = [P.sb(f"XM{g}", 128, DM, BF16) for g in range(2)]
    SHM = [P.sb(f"SHM{g}", 128, DM, BF16) for g in range(2)]
    XMB = P.sb("XMB", 128, 2048, BF16)
    idxc = P.sb("idxc", 128, 8, U32)
    gwb = P.sb("gwb", 128, 72, BF16)
    gb9b = P.sb("gb9b", 128, 16)
    expbb = P.sb("expbb", 128, 8)
    SC = [P.sb(f"SC{g}", 128, 16) for g in range(2)]
    PK = [P.sb(f"PK{g}", 128, 8) for g in range(2)]
    WSH = [P.sb(f"WSH{g}", 128, 2) for g in range(2)]
    SMT = P.sb("SMT", 128, 16)
    ew1b = [P.sb(f"ew1b{i}", 128, 4096, BF16) for i in range(2)]
    ew2b = [P.sb(f"ew2b{i}", 128, 4096, BF16) for i in range(2)]
    ew3b = [P.sb(f"ew3b{i}", 128, 4096, BF16) for i in range(2)]
    eb1C = [P.sb(f"eb1C{i}", 128, 4) for i in range(2)]
    eb2C = [P.sb(f"eb2C{i}", 128, 4) for i in range(2)]
    eb3C = [P.sb(f"eb3C{i}", 128, 8) for i in range(2)]
    selgC = P.sb("selgC", 128, 4)
    selbC = P.sb("selbC", 128, 4)
    S1 = P.sb("S1", 128, 1024, BF16)
    ZB = P.sb("ZB", 128, 1024, BF16)
    OETN = P.sb("OETN", 128, 2048, BF16)
    ACCT = [P.sb(f"ACCT{g}", 128, DM) for g in range(2)]
    TMP = P.sb("TMP", 128, DM)
    mean_b = P.sb("mean_b", 128, 256)
    rstd_b = P.sb("rstd_b", 128, 256)
    SQ = [P.sb(f"SQk{i}", 128, 256, BF16) for i in range(2)]
    TMS = P.sb("TMS", 128, 256)
    identbb = P.sb("identbb", 128, 128, BF16)
    ones_b = P.sb("ones_b", 128, 128, BF16)
    ones1f = P.sb("ones1f", 1, 128)
    onecol = P.sb("onecol", 128, 1)
    epscol = P.sb("epscol", 128, 1)

    pb = [P.ps(f"pb{i}", 128, 512) for i in range(8)]

    P.dma("gpsimd", idxc[:, 0:1], idxbI[0:128, :])
    P.dma("gpsimd", idxc[:, 1:2], idxbI[128:256, :])
    P.dma("gpsimd", idxc[:, 2:3], idxtI[0:128, :])
    P.dma("gpsimd", idxc[:, 3:4], idxtI[128:256, :])
    for kc in range(8):
        P.dma("gpsimd", gwb[:, 9 * kc:9 * (kc + 1)], gwI[128 * kc:128 * (kc + 1), :])
    P.dma("sync", gb9b[0:1, 0:9], gb9I[:, :])
    P.dma("sync", expbb[0:1, 0:8], expbI[:, :])
    P.dma("gpsimd", identbb[:, :], identI[:, :])
    for t in range(4):
        P.dma("sync", selgC[:, t:t + 1], selgI[128 * t:128 * (t + 1), :])
        P.dma("sync", selbC[:, t:t + 1], selbI[128 * t:128 * (t + 1), :])
    P.op("gpsimd", lambda h: h.memset(ones_b[:, :], 1.0))
    P.op("gpsimd", lambda h: h.memset(ones1f[:, :], 1.0))
    P.op("gpsimd", lambda h: h.memset(onecol[:, :], 1.0))
    c_eps = P.op("gpsimd", lambda h: h.memset(epscol[:, :], 1e-5))
    wall = P.tok("gpsimd")
    wsync = P.tok("sync")

    def bc_mm(h):
        h.matmul(pb[0][:, 0:9], ones1f[:, :], gb9b[0:1, 0:9], start=True, stop=True,
                 skip_group_check=True)
        return h.matmul(pb[0][:, 16:24], ones1f[:, :], expbb[0:1, 0:8], start=False, stop=True,
                        skip_group_check=True)
    t_bc = P.op("tensor", bc_mm, [wall, wsync])
    t_gb = P.op("vector", lambda h: h.tensor_copy(gb9b[:, 0:9], pb[0][:, 0:9]), [t_bc])
    t_eb = P.op("vector", lambda h: h.tensor_copy(expbb[:, :], pb[0][:, 16:24]), [t_bc])

    t_i1 = P.op("gpsimd", lambda h: h.tensor_scalar(idxc[:, 0:2], idxc[:, 0:2], 7, None, OP.min),
                [wall])
    t_i2 = P.op("gpsimd", lambda h: h.tensor_scalar(idxc[:, 2:4], idxc[:, 2:4], 2047, None,
                                                    OP.min), [t_i1])
    t_i3 = P.op("gpsimd", lambda h: h.tensor_scalar(idxc[:, 4:6], idxc[:, 0:2], 2048, None,
                                                    OP.mult), [t_i2])
    t_if = P.op("gpsimd", lambda h: h.tensor_tensor(idxc[:, 4:6], idxc[:, 4:6], idxc[:, 2:4],
                                                    OP.add), [t_i3])

    g_tok = []
    for g in range(2):
        t1 = P.op("gpsimd", lambda h, g=g: h.indirect_dma_start(
            XM[g][:, :], None, ln3F[:, :], IndirectOffsetOnAxis(ap=idxc[:, 4 + g:5 + g], axis=0),
            oob_is_err=False), [t_if], step=16)
        t2 = P.op("gpsimd", lambda h, g=g: h.indirect_dma_start(
            SHM[g][:, :], None, shF[:, :], IndirectOffsetOnAxis(ap=idxc[:, 4 + g:5 + g], axis=0),
            oob_is_err=False), [t_if], step=16)
        g_tok += [t1, t2]
    g_done = g_tok[-1]

    tr_tok = []
    last_tr = None
    for g in range(2):
        for kc in range(8):
            t_t = P.op("tensor", lambda h, g=g, kc=kc: h.transpose(
                pb[1][:, 0:64].bitcast(BF16), XM[g][:, 128 * kc:128 * (kc + 1)], identbb[:, :]),
                [g_done, last_tr, wall])
            last_tr = P.op("vector", lambda h, g=g, kc=kc: h.tensor_copy(
                XMB[:, 256 * kc + 128 * g:256 * kc + 128 * (g + 1)],
                pb[1][:, 0:64].bitcast(BF16)), [t_t])
            tr_tok.append(last_tr)
    xmb_done = tr_tok[-1]
    if stop_after == "gather":
        dbg = P.dram("dbgO", [256, DM], BF16, kind="ExternalOutput")
        f1 = P.dma("gpsimd", dbg[0:128, :], XM[0][:, :], [g_done])
        f2 = P.dma("gpsimd", dbg[128:256, :], XM[1][:, :], [g_done, xmb_done])
        return P.finish(final_waits=[f1, f2])

    gs_tok = []
    for g in range(2):
        def gs_mm(h, g=g):
            r = None
            for kc in range(8):
                r = h.matmul(pb[2][:, 0:9], XMB[:, 256 * kc + 128 * g:256 * kc + 128 * (g + 1)],
                             gwb[:, 9 * kc:9 * (kc + 1)], start=(kc == 0), stop=(kc == 7),
                             skip_group_check=True)
            return r
        t_g = P.op("tensor", gs_mm, [xmb_done, (gs_tok[-1] if gs_tok else None)])
        t_a = P.op("vector", lambda h, g=g: h.tensor_tensor(SC[g][:, 0:9], pb[2][:, 0:9],
                                                            gb9b[:, 0:9], OP.add), [t_g, t_gb])
        t_s = P.op("scalar", lambda h, g=g: h.activation(SC[g][:, 0:9], SC[g][:, 0:9],
                                                         AF.Sigmoid), [t_a])
        t_v = P.op("vector", lambda h, g=g: h.tensor_tensor(SMT[:, 0:8], SC[g][:, 1:9],
                                                            expbb[:, :], OP.add), [t_s, t_eb])
        t_mx = P.op("vector", lambda h: h.reduce_max(SMT[:, 8:9], SMT[:, 0:8], axis=AX.X), [t_v])
        t_in = P.op("vector", lambda h: h.tensor_scalar(SMT[:, 0:8], SMT[:, 0:8], SMT[:, 8:9],
                                                        None, OP.is_ge), [t_mx])
        t_nt = P.op("vector", lambda h: h.tensor_copy(SMT[:, 9:10], onecol[:, :]), [t_in, wall])
        prev = t_nt
        for e in range(8):
            p1 = P.op("vector", lambda h, g=g, e=e: h.tensor_tensor(
                PK[g][:, e:e + 1], SMT[:, e:e + 1], SMT[:, 9:10], OP.mult), [prev])
            p2 = P.op("vector", lambda h, e=e: h.tensor_scalar(
                SMT[:, 10:11], SMT[:, e:e + 1], -1.0, 1.0, OP.mult, OP.add), [p1])
            prev = P.op("vector", lambda h: h.tensor_tensor(SMT[:, 9:10], SMT[:, 9:10],
                                                            SMT[:, 10:11], OP.mult), [p2])
        t_ps = P.op("vector", lambda h, g=g: h.tensor_tensor(SMT[:, 0:8], PK[g][:, :],
                                                             SC[g][:, 1:9], OP.mult), [prev])
        t_t1 = P.op("vector", lambda h: h.tensor_reduce(SMT[:, 8:9], SMT[:, 0:8], axis=AX.X,
                                                        op=OP.add), [t_ps])
        t_m2 = P.op("vector", lambda h, g=g: h.tensor_tensor(SMT[:, 10:11], SC[g][:, 0:1],
                                                             SMT[:, 8:9], OP.max), [t_t1])
        t_n1 = P.op("vector", lambda h, g=g: h.tensor_tensor(SMT[:, 11:12], SC[g][:, 0:1],
                                                             SMT[:, 10:11], OP.subtract), [t_m2])
        t_n2 = P.op("vector", lambda h: h.tensor_tensor(SMT[:, 12:13], SMT[:, 8:9],
                                                        SMT[:, 10:11], OP.subtract), [t_m2])
        t_e1 = P.op("scalar", lambda h: h.activation(SMT[:, 11:12], SMT[:, 11:12], AF.Exp), [t_n1])
        t_e2 = P.op("scalar", lambda h: h.activation(SMT[:, 12:13], SMT[:, 12:13], AF.Exp), [t_n2])
        t_sm = P.op("vector", lambda h: h.tensor_tensor(SMT[:, 13:14], SMT[:, 11:12],
                                                        SMT[:, 12:13], OP.add), [t_e1, t_e2])
        t_rc = P.op("vector", lambda h: h.reciprocal(SMT[:, 13:14], SMT[:, 13:14]), [t_sm])
        t_w1 = P.op("vector", lambda h, g=g: h.tensor_tensor(WSH[g][:, 0:1], SMT[:, 11:12],
                                                             SMT[:, 13:14], OP.mult), [t_rc])
        t_w2 = P.op("vector", lambda h, g=g: h.tensor_tensor(WSH[g][:, 1:2], SMT[:, 12:13],
                                                             SMT[:, 13:14], OP.mult), [t_w1])
        gs_tok.append(t_w2)
    gate_done = gs_tok[-1]
    if stop_after == "gating":
        dbg = P.dram("dbgO", [256, 16], kind="ExternalOutput")
        f = []
        for g in range(2):
            f.append(P.dma("gpsimd", dbg[128 * g:128 * (g + 1), 0:8], PK[g][:, :], [gate_done]))
            f.append(P.dma("gpsimd", dbg[128 * g:128 * (g + 1), 8:10], WSH[g][:, :], [gate_done]))
        return P.finish(final_waits=f)

    def ld_expert(e, buf, waits):
        toks = []
        for kc in range(8):
            toks.append(P.dma("gpsimd", ew1b[buf][:, 512 * kc:512 * (kc + 1)],
                              ew1TI[1024 * e + 128 * kc:1024 * e + 128 * (kc + 1), :], waits))
            toks.append(P.dma("gpsimd", ew2b[buf][:, 512 * kc:512 * (kc + 1)],
                              ew2TI[1024 * e + 128 * kc:1024 * e + 128 * (kc + 1), :], waits))
        for kc in range(4):
            toks.append(P.dma("gpsimd", ew3b[buf][:, 1024 * kc:1024 * (kc + 1)],
                              ew3TI[512 * e + 128 * kc:512 * e + 128 * (kc + 1), :], waits))
        for t in range(4):
            toks.append(P.dma("sync", eb1C[buf][:, t:t + 1],
                              eb1I[512 * e + 128 * t:512 * e + 128 * (t + 1), :], waits))
            toks.append(P.dma("sync", eb2C[buf][:, t:t + 1],
                              eb2I[512 * e + 128 * t:512 * e + 128 * (t + 1), :], waits))
        for t in range(8):
            toks.append(P.dma("sync", eb3C[buf][:, t:t + 1],
                              eb3I[1024 * e + 128 * t:1024 * e + 128 * (t + 1), :], waits))
        return toks

    wtoks = {0: ld_expert(0, 0, [])}
    acc_tok = [None, None]
    e_used = {}
    for e in range(8):
        buf = e % 2
        wt = wtoks[e]
        if e + 1 < 8:
            wtoks[e + 1] = ld_expert(e + 1, 1 - buf, [e_used.get(e - 1)])
        def h1_mm(h, buf=buf):
            r = None
            for mi in range(4):
                for kc in range(8):
                    r = h.matmul(pb[mi // 2][:, 256 * (mi % 2):256 * (mi % 2) + 256],
                                 ew1b[buf][:, 512 * kc + 128 * mi:512 * kc + 128 * (mi + 1)],
                                 XMB[:, 256 * kc:256 * (kc + 1)], start=(kc == 0 and mi % 2 == 0),
                                 stop=(kc == 7), skip_group_check=True)
            return r
        t_h1 = P.op("tensor", h1_mm, [xmb_done, wt[-1], acc_tok[0]] +
                    ([gate_done] if e == 0 else []))
        def h2_mm(h, buf=buf):
            r = None
            for mi in range(4):
                for kc in range(8):
                    r = h.matmul(pb[2 + mi // 2][:, 256 * (mi % 2):256 * (mi % 2) + 256],
                                 ew2b[buf][:, 512 * kc + 128 * mi:512 * kc + 128 * (mi + 1)],
                                 XMB[:, 256 * kc:256 * (kc + 1)], start=(kc == 0 and mi % 2 == 0),
                                 stop=(kc == 7), skip_group_check=True)
            return r
        t_h2 = P.op("tensor", h2_mm, [acc_tok[0]] + ([gate_done] if e == 0 else []))
        zp_tok = []
        for mi in range(4):
            bank = pb[mi // 2]
            off = 256 * (mi % 2)
            t_si = P.op("scalar", lambda h, mi=mi, bank=bank, off=off, buf=buf: h.activation(
                S1[:, 256 * mi:256 * (mi + 1)], bank[:, off:off + 256], AF.Silu,
                bias=eb1C[buf][:, mi:mi + 1]), [t_h1, t_h2])
            t_b2 = P.op("vector", lambda h, mi=mi, buf=buf: h.tensor_scalar(
                TMS[:, :], pb[2 + mi // 2][:, 256 * (mi % 2):256 * (mi % 2) + 256],
                eb2C[buf][:, mi:mi + 1], None, OP.add), [t_si])
            t_z = P.op("vector", lambda h, mi=mi: h.tensor_tensor(
                ZB[:, 256 * mi:256 * (mi + 1)], S1[:, 256 * mi:256 * (mi + 1)], TMS[:, :],
                OP.mult), [t_b2])
            zp_tok.append(t_z)
        zp_done = zp_tok[-1]
        if stop_after == "h" and e == 0:
            dbg = P.dram("dbgO", [128, 1024], BF16, kind="ExternalOutput")
            return P.finish(final_waits=[P.dma("gpsimd", dbg[:, :], ZB[:, :], [zp_done])])
        ZT = [ZB[:, 256 * mi:256 * (mi + 1)] for mi in range(4)]
        statz = emit_ln_stats(P, ZT, lambda h, kc, dst: h.activation(
            dst[:, 0:256], ZB[:, 256 * kc:256 * (kc + 1)], AF.Square),
            mean_b, rstd_b, [pb[4]], [pb[5]], ones_b, 4, 256, SQ, TMS, epscol,
            waits_per_kc=[zp_done] * 4, extra_first=[c_eps])
        zn_tok = []
        for mi in range(4):
            t1 = P.op("vector", lambda h, mi=mi: h.tensor_tensor(
                TMS[:, :], ZB[:, 256 * mi:256 * (mi + 1)], mean_b[:, 0:256], OP.subtract),
                [statz[-1]])
            t2 = P.op("vector", lambda h: h.tensor_tensor(TMS[:, :], TMS[:, :], rstd_b[:, 0:256],
                                                          OP.mult), [t1])
            t3 = P.op("vector", lambda h, mi=mi: h.tensor_scalar(
                ZB[:, 256 * mi:256 * (mi + 1)], TMS[:, :], selgC[:, mi:mi + 1],
                selbC[:, mi:mi + 1], OP.mult, OP.add), [t2])
            zn_tok.append(t3)
        zn_done = zn_tok[-1]
        if stop_after == "z" and e == 0:
            dbg = P.dram("dbgO", [128, 1024], BF16, kind="ExternalOutput")
            return P.finish(final_waits=[P.dma("gpsimd", dbg[:, :], ZB[:, :], [zn_done])])
        def oe_mm(h, buf=buf):
            r = None
            for m in range(8):
                bank = pb[4 + m // 2]
                off = 256 * (m % 2)
                for kc in range(4):
                    r = h.matmul(bank[:, off:off + 256],
                                 ew3b[buf][:, 1024 * kc + 128 * m:1024 * kc + 128 * (m + 1)],
                                 ZB[:, 256 * kc:256 * (kc + 1)], start=(kc == 0 and m % 2 == 0),
                                 stop=(kc == 3), skip_group_check=True)
            return r
        t_oe = P.op("tensor", oe_mm, [zn_done, statz[-1]])
        if stop_after == "oemm" and e == 0:
            dbg = P.dram("dbgO", [128, 512], kind="ExternalOutput")
            tz = P.op("vector", lambda h: h.tensor_copy(ACCT[0][:, 0:512], pb[4][:, 0:512]), [t_oe])
            return P.finish(final_waits=[P.dma("gpsimd", dbg[:, :], ACCT[0][:, 0:512], [tz])])
        oe_ev = []
        for m in range(8):
            bank = pb[4 + m // 2]
            off = 256 * (m % 2)
            t_e1 = P.op("scalar", lambda h, m=m, bank=bank, off=off, buf=buf: h.activation(
                S1[:, 256 * (m % 4):256 * (m % 4) + 256], bank[:, off:off + 256], AF.Identity,
                bias=eb3C[buf][:, m:m + 1]), [t_oe] + ([oe_ev[m - 4]] if m >= 4 else []))
            tts = []
            for g in range(2):
                t_t = P.op("tensor", lambda h, m=m, g=g: h.transpose(
                    pb[1][:, 64 * g:64 * (g + 1)].bitcast(BF16),
                    S1[:, 256 * (m % 4) + 128 * g:256 * (m % 4) + 128 * (g + 1)], identbb[:, :]),
                    [t_e1, (tts[-1] if tts else (oe_ev[-1] if oe_ev else None))])
                tts.append(P.op("vector", lambda h, m=m, g=g: h.tensor_copy(
                    OETN[:, 1024 * g + 128 * m:1024 * g + 128 * (m + 1)],
                    pb[1][:, 64 * g:64 * (g + 1)].bitcast(BF16)), [t_t]))
            oe_ev.append(tts[-1])
        oe_done = oe_ev[-1]
        if stop_after == "oe" and e == 0:
            dbg = P.dram("dbgO", [128, 2048], BF16, kind="ExternalOutput")
            return P.finish(final_waits=[P.dma("gpsimd", dbg[:, :], OETN[:, :], [oe_done])])
        for g in range(2):
            if e == 0:
                a1 = P.op("vector", lambda h, g=g, e=e: h.tensor_scalar(
                    ACCT[g][:, :], OETN[:, 1024 * g:1024 * (g + 1)], PK[g][:, e:e + 1], None,
                    OP.mult), [oe_done, gate_done])
            else:
                a0 = P.op("vector", lambda h, g=g, e=e: h.tensor_scalar(
                    TMP[:, :], OETN[:, 1024 * g:1024 * (g + 1)], PK[g][:, e:e + 1], None,
                    OP.mult), [oe_done, acc_tok[g]])
                a1 = P.op("vector", lambda h, g=g: h.tensor_tensor(
                    ACCT[g][:, :], ACCT[g][:, :], TMP[:, :], OP.add), [a0])
            acc_tok[g] = a1
        e_used[e] = t_oe
        if stop_after == "expert1" and e == 0:
            dbg = P.dram("dbgO", [256, DM], kind="ExternalOutput")
            f = [P.dma("gpsimd", dbg[128 * g:128 * (g + 1), :],
                       ACCT[g][:, :], [acc_tok[g]]) for g in range(2)]
            return P.finish(final_waits=f)

    fin = []
    for g in range(2):
        t1 = P.op("vector", lambda h, g=g: h.tensor_scalar(
            ACCT[g][:, :], ACCT[g][:, :], WSH[g][:, 1:2], None, OP.mult), [acc_tok[g]])
        t2 = P.op("vector", lambda h, g=g: h.tensor_scalar(
            TMP[:, :], SHM[g][:, :], WSH[g][:, 0:1], None, OP.mult), [t1, g_done])
        t3 = P.op("vector", lambda h, g=g: h.tensor_tensor(ACCT[g][:, :], ACCT[g][:, :],
                                                           TMP[:, :], OP.add), [t2])
        fin.append(P.dma("sync", moeO[128 * g:128 * (g + 1), :], ACCT[g][:, :], [t3]))
    return P.finish(final_waits=fin)


def host_maps_K3(inputs):
    G = np.concatenate([inputs["L1_ffn_sg_w"].T, inputs["L1_ffn_eg_w"].T], 1).astype(np.float32)
    gb9 = np.concatenate([inputs["L1_ffn_sg_b"], inputs["L1_ffn_eg_b"]]).reshape(1, 9)
    ew1T = np.ascontiguousarray(inputs["L1_ffn_ew1"].transpose(0, 2, 1).reshape(8 * DM, 512))
    ew2T = np.ascontiguousarray(inputs["L1_ffn_ew2"].transpose(0, 2, 1).reshape(8 * DM, 512))
    ew3T = np.ascontiguousarray(inputs["L1_ffn_ew3"].transpose(0, 2, 1).reshape(8 * 512, DM))
    base = {
        "gwI": np.ascontiguousarray(G), "gb9I": np.ascontiguousarray(gb9),
        "expbI": np.ascontiguousarray(inputs["L1_ffn_exp_bias"].reshape(1, 8)),
        "ew1TI": ew1T, "ew2TI": ew2T, "ew3TI": ew3T,
        "eb1I": np.ascontiguousarray(inputs["L1_ffn_eb1"].reshape(-1, 1)),
        "eb2I": np.ascontiguousarray(inputs["L1_ffn_eb2"].reshape(-1, 1)),
        "eb3I": np.ascontiguousarray(inputs["L1_ffn_eb3"].reshape(-1, 1)),
        "selgI": np.ascontiguousarray(inputs["L1_ffn_sel_ln_g"].reshape(-1, 1)),
        "selbI": np.ascontiguousarray(inputs["L1_ffn_sel_ln_b"].reshape(-1, 1)),
        "identI": np.eye(128, dtype=np.float32),
    }
    return base


def assemble_tn_full(parts):
    full = np.zeros((NB * NT, parts[0].shape[1]), parts[0].dtype)
    for k in range(NCORE):
        p = parts[k]
        for b in range(NB):
            full[b * NT + 256 * k: b * NT + 256 * (k + 1)] = p[256 * b:256 * (b + 1)]
    return full





def build_launch_K4(serial=False, debug=False):
    P = Prog("K4", serial=serial)
    x3sI = P.dram("x3sI", [NT, DM])
    shsI = P.dram("shsI", [NT, DM], BF16)
    moeF = P.dram("moeF", [NT, DM])
    idxbF = P.dram("idxbF", [NT, 1], I32)
    idxtF = P.dram("idxtF", [NT, 1], I32)
    kofsI = P.dram("kofsI", [128, 1], I32)
    kofs2I = P.dram("kofs2I", [128, 1], I32)
    x4O = P.dram("x4O", [NT, DM], kind="ExternalOutput")
    mslD = P.dram("mslD", [NT, DM], kind="Internal")
    mskD = P.dram("mskD", [NT, 4], kind="Internal")
    if debug:
        mslX = P.dram("mslX", [NT, DM], kind="ExternalOutput")
        mskX = P.dram("mskX", [NT, 4], kind="ExternalOutput")
        lrX = P.dram("lrX", [128, 16], U32, kind="ExternalOutput")
        shX = P.dram("shX", [128, DM], kind="ExternalOutput")
        ffX = P.dram("ffX", [128, DM], kind="ExternalOutput")
        tmpX = P.dram("tmpX", [128, DM], kind="ExternalOutput")

    idxc = P.sb("idxc", 128, 8, U32)
    kofs = P.sb("kofs", 128, 1, U32)
    bigc = P.sb("bigc", 128, 1, U32)
    ZT = P.sb("ZT", 128, DM)
    ONES4 = P.sb("ONES4", 128, 4)
    MT = [P.sb(f"MT{i}", 128, DM) for i in range(2)]
    MS = P.sb("MS", 128, DM)
    MK = P.sb("MK", 128, 4)
    X3T = P.sb("X3T", 128, DM)
    SHT = P.sb("SHT", 128, DM, BF16)
    TMP = P.sb("TMP", 128, DM)
    FF = P.sb("FF", 128, DM)
    SQT = P.sb("SQT", 128, DM)
    SMC = P.sb("SMC", 128, 8)
    epscol = P.sb("epscol", 128, 1)

    kofs2 = P.sb("kofs2", 128, 1, U32)
    P.dma("gpsimd", kofs[:, :], kofsI[:, :])
    P.dma("gpsimd", kofs2[:, :], kofs2I[:, :])
    P.op("gpsimd", lambda h: h.memset(bigc[:, :], 70000))
    P.op("gpsimd", lambda h: h.memset(ZT[:, :], 0.0))
    P.op("gpsimd", lambda h: h.memset(ONES4[:, :], 1.0))
    c_eps = P.op("gpsimd", lambda h: h.memset(epscol[:, :], 1e-5))
    wall = P.tok("gpsimd")

    z_tok = []
    for tt in range(16):
        z_tok.append(P.dma("sync", mslD[128 * tt:128 * (tt + 1), :], ZT[:, :], [wall]))
        z_tok.append(P.dma("sync", mskD[128 * tt:128 * (tt + 1), :], ONES4[:, :] if False else ZT[:, 0:4], [wall]))
    z_done = z_tok[-1]

    sc_tok = []
    ld_prev = [None, None]
    for g in range(16):
        t_ib = P.dma("gpsimd", idxc[:, 0:1], idxbF[128 * g:128 * (g + 1), :],
                     [sc_tok[-1] if sc_tok else None])
        t_it = P.dma("gpsimd", idxc[:, 1:2], idxtF[128 * g:128 * (g + 1), :], [])
        t_u = P.op("gpsimd", lambda h: h.tensor_tensor(idxc[:, 2:3], idxc[:, 1:2], kofs[:, :],
                                                       OP.subtract), [t_it, wall])
        t_c = P.op("gpsimd", lambda h: h.tensor_scalar(idxc[:, 3:4], idxc[:, 2:3], 256, None,
                                                       OP.is_lt), [t_u])
        t_l = P.op("gpsimd", lambda h: h.tensor_scalar(idxc[:, 4:5], idxc[:, 0:1], 256, None,
                                                       OP.mult), [t_ib, t_c])
        t_l2 = P.op("gpsimd", lambda h: h.tensor_tensor(idxc[:, 4:5], idxc[:, 4:5], idxc[:, 2:3],
                                                        OP.add), [t_l])
        t_sel = P.op("vector", lambda h: h.select(idxc[:, 5:6], idxc[:, 3:4], idxc[:, 4:5],
                                                  bigc[:, :]), [t_l2])
        buf = MT[g % 2]
        t_ld = P.dma("sync", buf[:, :], moeF[128 * g:128 * (g + 1), :], [ld_prev[g % 2]])
        t_s1 = P.op("gpsimd", lambda h, buf=buf: h.indirect_dma_start(
            mslD[:, :], IndirectOffsetOnAxis(ap=idxc[:, 5:6], axis=0), buf[:, :], None,
            oob_is_err=False), [t_sel, t_ld, z_done], step=16)
        t_s2 = P.op("gpsimd", lambda h: h.indirect_dma_start(
            mskD[:, :], IndirectOffsetOnAxis(ap=idxc[:, 5:6], axis=0), ONES4[:, :], None,
            oob_is_err=False), [t_s1], step=16)
        ld_prev[g % 2] = t_s1
        if debug:
            sc_tok.append(P.op("gpsimd", lambda h, g=g: h.dma_start(
                out=lrX[:, g:g + 1], in_=idxc[:, 5:6], allow_slow_non_contiguous=True),
                [t_s2], step=16))
        sc_tok.append(t_s2)
    sc_done = sc_tok[-1]
    if debug:
        for tt in range(16):
            sc_done = P.dma("gpsimd", mslX[128 * tt:128 * (tt + 1), :],
                            mslD[128 * tt:128 * (tt + 1), :], [sc_done])
            sc_done = P.dma("gpsimd", mskX[128 * tt:128 * (tt + 1), :],
                            mskD[128 * tt:128 * (tt + 1), :], [sc_done])

    fin = []
    prev = None
    prev_t1 = None
    prev_t2a = None
    prev_t13 = None
    for tt in range(16):
        tsl = slice(128 * tt, 128 * (tt + 1))
        l1 = P.dma("sync", MS[:, :], mslD[tsl, :], [sc_done, prev])
        l2 = P.dma("sync", MK[:, :], mskD[tsl, :], [sc_done, prev_t1])
        l3 = P.dma("sync", X3T[:, :], x3sI[tsl, :], [prev_t13])
        l4 = P.dma("gpsimd", SHT[:, :], shsI[tsl, :], [prev_t2a])
        t1 = P.op("vector", lambda h: h.tensor_scalar(SMC[:, 0:1], MK[:, 0:1], -1.0, 1.0,
                                                      OP.mult, OP.add), [l2])
        t2a = P.op("vector", lambda h: h.tensor_copy(TMP[:, :], SHT[:, :]), [l4])
        t2 = P.op("vector", lambda h: h.tensor_scalar(TMP[:, :], TMP[:, :], SMC[:, 0:1], None,
                                                      OP.mult), [t1, t2a])
        t3 = P.op("vector", lambda h: h.tensor_tensor(FF[:, :], TMP[:, :], MS[:, :], OP.add),
                  [t2, l1])
        t4 = P.op("vector", lambda h: h.tensor_reduce(SMC[:, 1:2], FF[:, :], axis=AX.X,
                                                      op=OP.add), [t3])
        t5 = P.op("vector", lambda h: h.tensor_scalar(SMC[:, 1:2], SMC[:, 1:2], 1.0 / DM, None,
                                                      OP.mult), [t4])
        t6 = P.op("vector", lambda h: h.tensor_scalar(FF[:, :], FF[:, :], SMC[:, 1:2], None,
                                                      OP.subtract), [t5])
        t7 = P.op("scalar", lambda h: h.activation(SQT[:, :], FF[:, :], AF.Square), [t6])
        t8 = P.op("vector", lambda h: h.tensor_reduce(SMC[:, 2:3], SQT[:, :], axis=AX.X,
                                                      op=OP.add), [t7])
        t9 = P.op("vector", lambda h: h.tensor_scalar(SMC[:, 2:3], SMC[:, 2:3], 1.0 / DM, None,
                                                      OP.mult), [t8])
        t10 = P.op("scalar", lambda h: h.activation(SMC[:, 2:3], SMC[:, 2:3], AF.Sqrt,
                                                    bias=epscol[:, 0:1]), [t9, c_eps])
        t11 = P.op("vector", lambda h: h.reciprocal(SMC[:, 2:3], SMC[:, 2:3]), [t10])
        t12 = P.op("vector", lambda h: h.tensor_scalar(FF[:, :], FF[:, :], SMC[:, 2:3], None,
                                                       OP.mult), [t11])
        if debug and tt == 0:
            d1 = P.op("vector", lambda h: h.tensor_copy(MT[0][:, :], SHT[:, :]), [l4, t2])
            P._dk = [P.dma("gpsimd", shX[:, :], MT[0][:, :], [d1]),
                     P.dma("gpsimd", tmpX[:, :], TMP[:, :], [t2]),
                     P.dma("gpsimd", ffX[:, :], FF[:, :], [t3])]
            t3b = P._dk[-1]
        t13 = P.op("vector", lambda h: h.tensor_tensor(FF[:, :], FF[:, :], X3T[:, :], OP.add),
                   [t12, l3] + ([t3b] if debug and tt == 0 else []))
        prev = P.dma("sync", x4O[tsl, :], FF[:, :], [t13])
        prev_t1 = t1
        prev_t2a = t2a
        prev_t13 = t13
        fin.append(prev)
    return P.finish(final_waits=fin)


def host_maps_K4(inputs, x3_parts, sh_parts, moe_full):
    ib = inputs["L1_ffn_index_b"].astype(np.int32).reshape(-1, 1)
    it = inputs["L1_ffn_index_t"].astype(np.int32).reshape(-1, 1)
    maps = []
    for k in range(NCORE):
        maps.append({
            "x3sI": np.ascontiguousarray(x3_parts[k]),
            "shsI": np.ascontiguousarray(sh_parts[k]),
            "moeF": np.ascontiguousarray(moe_full),
            "idxbF": ib, "idxtF": it,
            "kofsI": np.full((128, 1), 256 * k, np.int32),
            "kofs2I": np.full((128, 1), 256 * (k + 1), np.int32),
        })
    return maps


_NC_CACHE = {}


def _get_nc(name, builder):
    if name not in _NC_CACHE:
        _NC_CACHE[name] = builder()
    return _NC_CACHE[name]


def kernel(**inputs):
    ncA = _get_nc("A", build_launch_A)
    resA = run_bass_kernel_spmd(ncA, host_maps_A(inputs), core_ids=list(range(NCORE)))
    x1 = [np.asarray(resA.results[c]["x1T"]) for c in range(NCORE)]
    y = [np.asarray(resA.results[c]["y0T"]) for c in range(NCORE)]

    ncBa = _get_nc("Ba", build_launch_Battn)
    ncBf = _get_nc("Bf", build_launch_Bffn)
    for d in range(6):
        r1 = run_bass_kernel_spmd(ncBa, host_maps_Battn(inputs, y, d), core_ids=list(range(NCORE)))
        y = [np.asarray(r1.results[c]["youtT"]) for c in range(NCORE)]
        r2 = run_bass_kernel_spmd(ncBf, host_maps_Bffn(inputs, y, d), core_ids=list(range(NCORE)))
        y = [np.asarray(r2.results[c]["youtT"]) for c in range(NCORE)]

    ncC = _get_nc("C", build_launch_C)
    resC = run_bass_kernel_spmd(ncC, host_maps_C(x1, y), core_ids=list(range(NCORE)))
    x2 = [np.asarray(resC.results[c]["o1T"]) for c in range(NCORE)]
    ln2 = [np.asarray(resC.results[c]["o2T"]) for c in range(NCORE)]

    ncK2a = _get_nc("K2a", build_launch_K2a)
    r2a = run_bass_kernel_spmd(ncK2a, host_maps_K2a(inputs, ln2), core_ids=list(range(NCORE)))
    attn_parts = [np.asarray(r2a.results[k]["attnO"]) for k in range(NCORE)]

    ncK2b = _get_nc("K2b", build_launch_K2b)
    r2b = run_bass_kernel_spmd(ncK2b, host_maps_K2b(inputs, x2, attn_parts),
                               core_ids=list(range(NCORE)))
    x3_parts = [np.asarray(r2b.results[k]["x3O"]) for k in range(NCORE)]
    ln3_parts = [np.asarray(r2b.results[k]["ln3O"]) for k in range(NCORE)]
    sh_parts = [np.asarray(r2b.results[k]["shO"]) for k in range(NCORE)]

    ln3F = assemble_tn_full(ln3_parts)
    shF = assemble_tn_full(sh_parts)
    ib = inputs["L1_ffn_index_b"].astype(np.int32)
    it = inputs["L1_ffn_index_t"].astype(np.int32)
    ncK3 = _get_nc("K3", build_launch_K3)
    base3 = host_maps_K3(inputs)
    maps3 = []
    for k in range(NCORE):
        m = dict(base3)
        m["ln3F"] = ln3F
        m["shF"] = shF
        m["idxbI"] = np.ascontiguousarray(ib[256 * k:256 * (k + 1)].reshape(-1, 1))
        m["idxtI"] = np.ascontiguousarray(it[256 * k:256 * (k + 1)].reshape(-1, 1))
        maps3.append(m)
    r3 = run_bass_kernel_spmd(ncK3, maps3, core_ids=list(range(NCORE)))
    moe_full = np.concatenate([np.asarray(r3.results[k]["moeO"]) for k in range(NCORE)], 0)

    ncK4 = _get_nc("K4", build_launch_K4)
    r4 = run_bass_kernel_spmd(ncK4, host_maps_K4(inputs, x3_parts, sh_parts, moe_full),
                              core_ids=list(range(NCORE)))
    out = np.zeros((NB, NT, DM), np.float32)
    for k in range(NCORE):
        part = np.asarray(r4.results[k]["x4O"])
        for b in range(NB):
            out[b, 256 * k:256 * (k + 1)] = part[256 * b:256 * (b + 1)]
    return out










# version 02 (final)
# speedup vs baseline: 1.3132x; 1.3132x over previous
import numpy as np
from contextlib import ExitStack
import concourse.bass as bass
import concourse.mybir as mybir
from concourse.bass_utils import run_bass_kernel_spmd
from concourse.alu_op_type import AluOpType as OP
from bass_rust import IndirectOffsetOnAxis

F32 = mybir.dt.float32
BF16 = mybir.dt.bfloat16
U32 = mybir.dt.uint32
I32 = mybir.dt.int32
AF = mybir.ActivationFunctionType
AX = mybir.AxisListType

NB, NT, DM = 8, 2048, 1024
NCORE = 8

ENGS = ("sync", "gpsimd", "vector", "scalar", "tensor")


class Prog:

    def __init__(self, name="k", serial=False):
        self.nc = bass.Bass("TRN2", target_bir_lowering=False)
        self.st = ExitStack()
        self.q = {e: [] for e in ENGS}
        self.sem = {}
        self.cnt = {}
        self.serial = serial
        self.last = None
        for e in ENGS:
            self.sem[e] = self.st.enter_context(self.nc.semaphore("ms_" + e))
            self.cnt[e] = 0

    def dram(self, name, shape, dt=F32, kind="ExternalInput", **kw):
        return self.nc.dram_tensor(name, list(shape), dt, kind=kind, **kw).ap()

    def sb(self, name, p, f, dt=F32, scope=None, side=None):
        kw = {}
        if side is not None:
            kw["side"] = side
        return (scope or self.st).enter_context(self.nc.sbuf_tensor(name, [p, f], dt, **kw))

    def ps(self, name, p, f, dt=F32):
        return self.st.enter_context(self.nc.psum_tensor(name, [p, f], dt))

    def op(self, eng, fn, waits=(), step=1):
        wl = [w for w in waits if w is not None]
        if self.serial and self.last is not None and self.last[0] is not self.sem[eng]:
            wl = wl + [self.last]
        sem = self.sem[eng]
        if step:
            self.cnt[eng] += step
        tokval = self.cnt[eng]

        def emit(h, fn=fn, wl=wl, step=step, sem=sem):
            for (ws, wv) in wl:
                h.wait_ge(ws, wv)
            inst = fn(h)
            if step:
                inst.then_inc(sem, step)
        self.q[eng].append(emit)
        if step:
            self.last = (sem, tokval)
        return (sem, tokval) if step else None

    def dma(self, eng, out, in_, waits=()):
        return self.op(eng, lambda h: h.dma_start(out=out, in_=in_), waits, step=16)

    def tok(self, eng):
        return (self.sem[eng], self.cnt[eng]) if self.cnt[eng] else None

    def finish(self, final_waits=()):
        blk = self.st.enter_context(self.nc.Block())
        for e in ENGS:
            ops = self.q[e]
            if e == "sync":
                def body_sync(h, ops=ops):
                    for f in ops:
                        f(h)
                    for (ws, wv) in final_waits:
                        h.wait_ge(ws, wv)
                blk.sync(body_sync)
            else:
                if not ops:
                    continue
                def body(h, ops=ops):
                    for f in ops:
                        f(h)
                getattr(blk, e)(body)
        self.st.close()
        return self.nc


def _cdiv(a, b):
    return (a + b - 1) // b



def emit_ln_stats(P, rhs_tiles, sq_fn, mean_b, rstd_b, ps_sum, ps_sq, ones_b,
                  nch, ntok, sq_bufs, tmp, epscol, waits_per_kc=None, extra_first=()):
    nt = _cdiv(ntok, 512)
    inv = 1.0 / (nch * 128)
    single_sq = sq_bufs[0] is sq_bufs[1]
    sq_war = [None, None]
    mm_tok = None
    for kc in range(nch):
        b = kc % 2
        w = [mm_tok if single_sq else sq_war[b]]
        if waits_per_kc is not None:
            w.append(waits_per_kc[kc])
        t_sq = P.op("scalar", lambda h, kc=kc, b=b: sq_fn(h, kc, sq_bufs[b]), w)

        def mm(h, kc=kc, b=b):
            last = None
            for j in range(nt):
                c0, c1 = j * 512, min((j + 1) * 512, ntok)
                last = h.matmul(ps_sum[j][:, 0:c1 - c0], ones_b[:, :], rhs_tiles[kc][:, c0:c1],
                                start=(kc == 0), stop=(kc == nch - 1), skip_group_check=True)
                last = h.matmul(ps_sq[j][:, 0:c1 - c0], ones_b[:, :], sq_bufs[b][:, c0:c1],
                                start=(kc == 0), stop=(kc == nch - 1), skip_group_check=True)
            return last
        mm_tok = P.op("tensor", mm, [t_sq] + (list(extra_first) if kc == 0 else []))
        sq_war[b] = mm_tok
    toks = []
    for j in range(nt):
        c0, c1 = j * 512, min((j + 1) * 512, ntok)
        t1 = P.op("scalar", lambda h, j=j, c0=c0, c1=c1: h.activation(
            mean_b[:, c0:c1], ps_sum[j][:, 0:c1 - c0], AF.Copy, scale=inv), [mm_tok])
        t2 = P.op("scalar", lambda h, j=j, c0=c0, c1=c1: h.activation(
            rstd_b[:, c0:c1], ps_sq[j][:, 0:c1 - c0], AF.Copy, scale=inv), [mm_tok])
        t3 = P.op("vector", lambda h, c0=c0, c1=c1: h.tensor_tensor(
            tmp[:, c0:c1], mean_b[:, c0:c1], mean_b[:, c0:c1], OP.mult), [t1])
        t4 = P.op("vector", lambda h, c0=c0, c1=c1: h.tensor_tensor(
            rstd_b[:, c0:c1], rstd_b[:, c0:c1], tmp[:, c0:c1], OP.subtract), [t2, t3])
        t5 = P.op("scalar", lambda h, c0=c0, c1=c1: h.activation(
            rstd_b[:, c0:c1], rstd_b[:, c0:c1], AF.Sqrt, bias=epscol[:, 0:1]), [t4])
        t6 = P.op("vector", lambda h, c0=c0, c1=c1: h.reciprocal(
            rstd_b[:, c0:c1], rstd_b[:, c0:c1]), [t5])
        toks.append(t6)
    return toks



def build_launch_A():
    P = Prog("A")
    C = 128
    NCH = NT // C

    x0T = P.dram("x0T", [DM, NT])
    wqT = P.dram("wqT", [DM, 256])
    wkT = P.dram("wkT", [DM, 256])
    wvT = P.dram("wvT", [DM, 256])
    woT = P.dram("woT", [256, DM])
    memI = P.dram("memI", [256, 256])
    mnI = P.dram("mnI", [256, 1])
    gateI = P.dram("gateI", [1, 1])
    triI = P.dram("triI", [128, 128])
    identI = P.dram("identI", [128, 128])
    x1T = P.dram("x1T", [DM, NT], kind="ExternalOutput")
    y0T = P.dram("y0T", [DM, NT], kind="ExternalOutput")

    X = [P.sb(f"X{i}", 128, NT) for i in range(8)]
    LB = [P.sb(f"LB{i}", 128, NT, BF16) for i in range(8)]
    ATT = [P.sb(f"ATT{i}", 128, NT, BF16) for i in range(8)]
    SQ = [P.sb(f"SQ{i}", 128, NT, BF16) for i in range(2)]
    TMPF = P.sb("TMPF", 128, NT)
    TMPG = P.sb("TMPG", 128, NT)
    mean_b = P.sb("mean_b", 128, NT)
    rstd_b = P.sb("rstd_b", 128, NT)
    wqb = P.sb("wqb", 128, 2048, BF16)
    wkb = P.sb("wkb", 128, 2048, BF16)
    wvb = P.sb("wvb", 128, 2048, BF16)
    wob = P.sb("wob", 128, 2048, BF16)
    memb = P.sb("memb", 128, 512, BF16)
    mnb = P.sb("mnb", 128, 2, BF16)
    mnf = P.sb("mnf", 128, 2)
    trib = P.sb("trib", 128, 128, BF16)
    identb = P.sb("identb", 128, 128, BF16)
    ones_b = P.sb("ones_b", 128, 128, BF16)
    ones1f = P.sb("ones1f", 1, 128)
    onescol = P.sb("onescol", 128, 1, BF16)
    onescolf = P.sb("onescolf", 128, 1)
    epscol = P.sb("epscol", 128, 1)
    gf = P.sb("gf", 1, 4)
    gcol = P.sb("gcol", 128, 2)
    sgq = P.sb("sgq", 128, 256, BF16)
    sgk = P.sb("sgk", 128, 256, BF16)
    sgkT = P.sb("sgkT", 128, 256, BF16)
    vTb = P.sb("vTb", 128, 256, BF16)
    Amb = P.sb("Amb", 128, 128, BF16)
    Ssb = P.sb("Ssb", 128, 512, BF16)
    zb = P.sb("zb", 128, 2, BF16)
    dsc = P.sb("dsc", 1, 512)
    rb_sb = P.sb("rb_sb", 128, 128)
    rmn_sb = P.sb("rmn_sb", 128, 128)
    LOC = P.sb("LOC", 128, 256)
    MON = P.sb("MON", 128, 256)
    combb = P.sb("combb", 128, 256, BF16)
    TS = P.sb("TS", 128, 256)
    TS2 = P.sb("TS2", 128, 256)

    pb = [P.ps(f"pb{i}", 128, 512) for i in range(8)]

    ldx = [P.dma("sync", X[i][:, :], x0T[128 * i:128 * (i + 1), :]) for i in range(8)]
    for kc in range(8):
        P.dma("gpsimd", wqb[:, 256 * kc:256 * (kc + 1)], wqT[128 * kc:128 * (kc + 1), :])
        P.dma("gpsimd", wkb[:, 256 * kc:256 * (kc + 1)], wkT[128 * kc:128 * (kc + 1), :])
        P.dma("gpsimd", wvb[:, 256 * kc:256 * (kc + 1)], wvT[128 * kc:128 * (kc + 1), :])
    for ec in range(2):
        P.dma("gpsimd", wob[:, 1024 * ec:1024 * (ec + 1)], woT[128 * ec:128 * (ec + 1), :])
        P.dma("gpsimd", memb[:, 256 * ec:256 * (ec + 1)], memI[128 * ec:128 * (ec + 1), :])
        P.dma("gpsimd", mnb[:, ec:ec + 1], mnI[128 * ec:128 * (ec + 1), :])
        P.dma("gpsimd", mnf[:, ec:ec + 1], mnI[128 * ec:128 * (ec + 1), :])
    P.dma("gpsimd", trib[:, :], triI[:, :])
    P.dma("gpsimd", identb[:, :], identI[:, :])
    wall = P.dma("gpsimd", gf[0:1, 0:1], gateI[:, :])
    P.op("gpsimd", lambda h: h.memset(ones_b[:, :], 1.0))
    P.op("gpsimd", lambda h: h.memset(ones1f[:, :], 1.0))
    P.op("gpsimd", lambda h: h.memset(onescol[:, :], 1.0))
    P.op("gpsimd", lambda h: h.memset(onescolf[:, :], 1.0))
    call = P.op("gpsimd", lambda h: h.memset(epscol[:, :], 1e-5))

    t_g = P.op("scalar", lambda h: h.activation(gf[0:1, 0:1], gf[0:1, 0:1], AF.Sigmoid),
               [wall, call])
    t_1mg = P.op("vector", lambda h: h.tensor_scalar(gf[0:1, 1:2], gf[0:1, 0:1], -1.0, 1.0,
                                                     OP.mult, OP.add), [t_g])

    def ms_mm(h):
        h.matmul(pb[2][0:1, 0:1], mnf[:, 0:1], onescolf[:, :], start=True, stop=False,
                 skip_group_check=True)
        return h.matmul(pb[2][0:1, 0:1], mnf[:, 1:2], onescolf[:, :], start=False, stop=True,
                        skip_group_check=True)
    t_ms = P.op("tensor", ms_mm, [wall, call])
    t_msc = P.op("vector", lambda h: h.tensor_scalar(gf[0:1, 2:3], pb[2][0:1, 0:1], 1e-6, None,
                                                     OP.is_ge), [t_ms])
    t_gf = P.op("vector", lambda h: h.tensor_tensor(gf[0:1, 3:4], gf[0:1, 0:1], gf[0:1, 2:3],
                                                    OP.mult), [t_msc])

    def bc_mm(h):
        h.matmul(pb[3][:, 0:1], ones1f[:, :], gf[0:1, 1:2], start=True, stop=True,
                 skip_group_check=True)
        return h.matmul(pb[3][:, 1:2], ones1f[:, :], gf[0:1, 3:4], start=False, stop=True,
                        skip_group_check=True)
    t_bc = P.op("tensor", bc_mm, [t_1mg, t_gf])
    t_gcol = P.op("vector", lambda h: h.tensor_copy(gcol[:, :], pb[3][:, 0:2]), [t_bc])

    cast_tok = [P.op("vector", lambda h, kc=kc: h.tensor_copy(LB[kc][:, :], X[kc][:, :]),
                     [ldx[kc]]) for kc in range(8)]

    stat = emit_ln_stats(P, LB, lambda h, kc, dst: h.activation(dst[:, :], X[kc][:, :], AF.Square),
                         mean_b, rstd_b, pb[0:4], pb[4:8], ones_b, 8, NT, SQ, TMPF, epscol,
                         waits_per_kc=[cast_tok[kc] for kc in range(8)], extra_first=[call])
    lnall = None
    for kc in range(8):
        t1 = P.op("vector", lambda h, kc=kc: h.tensor_tensor(TMPG[:, :], X[kc][:, :], mean_b[:, :],
                                                             OP.subtract), [stat[-1]])
        lnall = P.op("vector", lambda h, kc=kc: h.tensor_tensor(LB[kc][:, :], TMPG[:, :],
                                                                rstd_b[:, :], OP.mult), [t1])

    prev = {"q_bank": t_gcol, "k_bank": None, "vT_bank": None, "den_read": None,
            "Ssb": None, "zb": t_msc, "tr_bank": None, "rb_bank": None,
            "out47_bank": None}
    for ci in range(NCH):
        sl = slice(ci * C, ci * C + C)
        first = (ci == 0)
        last = (ci == NCH - 1)

        def q_mm(h, sl=sl):
            r = None
            for m in range(2):
                for kc in range(8):
                    r = h.matmul(pb[3][:, 128 * m:128 * (m + 1)],
                                 wqb[:, 256 * kc + 128 * m:256 * kc + 128 * (m + 1)],
                                 LB[kc][:, sl], start=(kc == 0 and m == 0),
                                 stop=(kc == 7 and m == 1), skip_group_check=True)
            return r
        t_q = P.op("tensor", q_mm, [lnall, prev["q_bank"]])

        def k_mm(h, sl=sl):
            r = None
            for m in range(2):
                for kc in range(8):
                    r = h.matmul(pb[5][:, 128 * m:128 * (m + 1)],
                                 wkb[:, 256 * kc + 128 * m:256 * kc + 128 * (m + 1)],
                                 LB[kc][:, sl], start=(kc == 0 and m == 0),
                                 stop=(kc == 7 and m == 1), skip_group_check=True)
            return r
        t_k = P.op("tensor", k_mm, [prev["k_bank"]])

        def vT_mm(h, sl=sl):
            r = None
            for kc in range(8):
                r = h.matmul(pb[4][:, 0:256], LB[kc][:, sl], wvb[:, 256 * kc:256 * (kc + 1)],
                             start=(kc == 0), stop=(kc == 7), skip_group_check=True)
            return r
        t_vT = P.op("tensor", vT_mm, [prev["vT_bank"]])
        t_vTe = P.op("vector", lambda h: h.tensor_copy(vTb[:, :], pb[4][:, 0:256]), [t_vT])

        t1 = P.op("vector", lambda h: h.tensor_scalar(TS[:, :], pb[3][:, 0:256], 0.0, None,
                                                      OP.min), [t_q])
        t2 = P.op("scalar", lambda h: h.activation(TS[:, :], TS[:, :], AF.Exp), [t1])
        t3 = P.op("scalar", lambda h: h.activation(TS2[:, :], pb[3][:, 0:256], AF.Relu), [t_q])
        t_sgq = P.op("vector", lambda h: h.tensor_tensor(sgq[:, :], TS[:, :], TS2[:, :], OP.add),
                     [t2, t3])
        t1 = P.op("vector", lambda h: h.tensor_scalar(TS[:, :], pb[5][:, 0:256], 0.0, None,
                                                      OP.min), [t_k, t_sgq])
        t2 = P.op("scalar", lambda h: h.activation(TS[:, :], TS[:, :], AF.Exp), [t1])
        t3 = P.op("scalar", lambda h: h.activation(TS2[:, :], pb[5][:, 0:256], AF.Relu), [t_k])
        t_sgk = P.op("vector", lambda h: h.tensor_tensor(sgk[:, :], TS[:, :], TS2[:, :], OP.add),
                     [t2, t3])

        def tr_mm(h):
            r = None
            for dc in range(2):
                r = h.transpose(pb[6][:, 64 * dc:64 * (dc + 1)].bitcast(BF16),
                                sgk[:, 128 * dc:128 * (dc + 1)], identb[:, :])
            return r
        t_tr = P.op("tensor", tr_mm, [t_sgk, prev["tr_bank"]])
        t_sgkTe = P.op("vector", lambda h: h.tensor_copy(sgkT[:, :], pb[6][:, 0:128].bitcast(BF16)),
                       [t_tr])

        def A_mm(h):
            r = None
            for dc in range(2):
                r = h.matmul(pb[6][:, 256:384], sgk[:, 128 * dc:128 * (dc + 1)],
                             sgq[:, 128 * dc:128 * (dc + 1)], start=(dc == 0), stop=(dc == 1),
                             skip_group_check=True)
            return r
        t_A = P.op("tensor", A_mm, [t_sgq, t_sgk, t_sgkTe])
        t_Am = P.op("vector", lambda h: h.tensor_tensor(Amb[:, :], pb[6][:, 256:384], trib[:, :],
                                                        OP.mult), [t_A])

        def den_mm(h, first=first):
            r = h.matmul(pb[4][0:1, 256:384], onescol[:, :], Amb[:, :], start=True, stop=False,
                         skip_group_check=True)
            if not first:
                for dc in range(2):
                    r = h.matmul(pb[4][0:1, 256:384], zb[:, dc:dc + 1],
                                 sgq[:, 128 * dc:128 * (dc + 1)], start=False, stop=False,
                                 skip_group_check=True)
            for dc in range(2):
                r = h.matmul(pb[4][0:1, 384:512], mnb[:, dc:dc + 1],
                             sgq[:, 128 * dc:128 * (dc + 1)], start=(dc == 0), stop=(dc == 1),
                             skip_group_check=True)
            return r
        t_den = P.op("tensor", den_mm, [t_Am, t_vTe, prev["den_read"], prev["zb"]])

        def num_mm(h, first=first):
            r = None
            for m in range(2):
                st = True
                if not first:
                    for dc in range(2):
                        r = h.matmul(pb[6][:, 128 * m:128 * (m + 1)],
                                     Ssb[:, 256 * dc + 128 * m:256 * dc + 128 * (m + 1)],
                                     sgq[:, 128 * dc:128 * (dc + 1)], start=st, stop=False,
                                     skip_group_check=True)
                        st = False
                r = h.matmul(pb[6][:, 128 * m:128 * (m + 1)], vTb[:, 128 * m:128 * (m + 1)],
                             Amb[:, :], start=st, stop=True, skip_group_check=True)
            return r
        t_num = P.op("tensor", num_mm, [t_sgkTe, t_Am, t_vTe, prev["Ssb"]])

        def mo_mm(h):
            r = None
            for m in range(2):
                for dc in range(2):
                    r = h.matmul(pb[5][:, 128 * m:128 * (m + 1)],
                                 memb[:, 256 * dc + 128 * m:256 * dc + 128 * (m + 1)],
                                 sgq[:, 128 * dc:128 * (dc + 1)],
                                 start=(dc == 0 and m == 0), stop=(dc == 1 and m == 1),
                                 skip_group_check=True)
            return r
        t_mo = P.op("tensor", mo_mm, [t_sgq, t_sgk])

        def S_mm(h, first=first, last=last):
            r = None
            for dc in range(2):
                r = h.matmul(pb[dc][:, 0:256], sgkT[:, 128 * dc:128 * (dc + 1)], vTb[:, :],
                             start=first, stop=last, skip_group_check=True)
            for dc in range(2):
                r = h.matmul(pb[2][:, 128 * dc:128 * dc + 1], sgkT[:, 128 * dc:128 * (dc + 1)],
                             onescol[:, :], start=first, stop=last, skip_group_check=True)
            return r
        t_S = P.op("tensor", S_mm, [t_sgkTe, t_vTe, prev["Ssb"], prev["zb"]])
        t_Ssb = P.op("vector", lambda h: (h.tensor_copy(Ssb[:, 0:256], pb[0][:, 0:256]),
                                          h.tensor_copy(Ssb[:, 256:512], pb[1][:, 0:256]))[-1],
                     [t_S])
        t_zb = P.op("vector", lambda h: (h.tensor_copy(zb[:, 0:1], pb[2][:, 0:1]),
                                         h.tensor_copy(zb[:, 1:2], pb[2][:, 128:129]))[-1], [t_S])

        t_d1 = P.op("vector", lambda h: h.tensor_scalar(dsc[0:1, 0:128], pb[4][0:1, 256:384],
                                                        1e-6, None, OP.max), [t_den])
        t_d2 = P.op("vector", lambda h: h.reciprocal(dsc[0:1, 0:128], dsc[0:1, 0:128]), [t_d1])
        t_m1 = P.op("vector", lambda h: h.tensor_scalar(dsc[0:1, 128:256], pb[4][0:1, 384:512],
                                                        1e-6, None, OP.max), [t_den])
        t_m2 = P.op("vector", lambda h: h.reciprocal(dsc[0:1, 128:256], dsc[0:1, 128:256]), [t_m1])

        def rb_mm(h):
            h.matmul(pb[7][:, 0:128], ones1f[:, :], dsc[0:1, 0:128], start=True, stop=True,
                     skip_group_check=True)
            return h.matmul(pb[7][:, 128:256], ones1f[:, :], dsc[0:1, 128:256], start=False,
                            stop=True, skip_group_check=True)
        t_rb = P.op("tensor", rb_mm, [t_d2, t_m2, prev["rb_bank"]])
        t_rbe = P.op("vector", lambda h: h.tensor_copy(rb_sb[:, :], pb[7][:, 0:128]), [t_rb])
        t_rmne = P.op("vector", lambda h: h.tensor_copy(rmn_sb[:, :], pb[7][:, 128:256]), [t_rb])

        t_loc = P.op("vector", lambda h: (h.tensor_tensor(LOC[:, 0:128], pb[6][:, 0:128],
                                                          rb_sb[:, :], OP.mult),
                                          h.tensor_tensor(LOC[:, 128:256], pb[6][:, 128:256],
                                                          rb_sb[:, :], OP.mult))[-1],
                     [t_num, t_rbe])
        t_mon = P.op("vector", lambda h: (h.tensor_tensor(MON[:, 0:128], pb[5][:, 0:128],
                                                          rmn_sb[:, :], OP.mult),
                                          h.tensor_tensor(MON[:, 128:256], pb[5][:, 128:256],
                                                          rmn_sb[:, :], OP.mult))[-1],
                     [t_mo, t_rmne])
        t_c1 = P.op("vector", lambda h: h.tensor_scalar(LOC[:, :], LOC[:, :], gcol[:, 0:1], None,
                                                        OP.mult), [t_loc, t_gcol])
        t_c2 = P.op("vector", lambda h: h.tensor_scalar(MON[:, :], MON[:, :], gcol[:, 1:2], None,
                                                        OP.mult), [t_mon])
        t_comb = P.op("vector", lambda h: h.tensor_tensor(combb[:, :], LOC[:, :], MON[:, :],
                                                          OP.add), [t_c1, t_c2])

        def out_mm(h):
            r = None
            for m in range(8):
                bank = pb[3] if m < 4 else pb[7]
                off = (m % 4) * 128
                for ec in range(2):
                    r = h.matmul(bank[:, off:off + 128],
                                 wob[:, 1024 * ec + 128 * m:1024 * ec + 128 * (m + 1)],
                                 combb[:, 128 * ec:128 * (ec + 1)],
                                 start=(ec == 0 and m % 4 == 0), stop=(ec == 1),
                                 skip_group_check=True)
            return r
        t_out = P.op("tensor", out_mm, [t_comb, t_rbe, t_rmne])

        def ev03(h, sl=sl):
            r = None
            for m in range(4):
                r = h.copy(ATT[m][:, sl], pb[3][:, 128 * m:128 * (m + 1)])
            return r
        def ev47(h, sl=sl):
            r = None
            for m in range(4):
                r = h.copy(ATT[4 + m][:, sl], pb[7][:, 128 * m:128 * (m + 1)])
            return r
        t_e03 = P.op("scalar", ev03, [t_out])
        t_e47 = P.op("scalar", ev47, [t_out])

        prev = {"q_bank": t_e03, "k_bank": t_mon, "vT_bank": t_m1, "den_read": t_m2,
                "Ssb": t_Ssb, "zb": t_zb, "tr_bank": t_loc, "rb_bank": t_e47,
                "out47_bank": t_e47}

    scan_tail_v = P.tok("vector")
    scan_tail_s = P.tok("scalar")

    stat2 = emit_ln_stats(P, ATT, lambda h, kc, dst: h.activation(dst[:, :], ATT[kc][:, :],
                                                                  AF.Square),
                          mean_b, rstd_b, pb[0:4], pb[4:8], ones_b, 8, NT, SQ, TMPF, epscol,
                          waits_per_kc=[scan_tail_v if kc == 0 else None for kc in range(8)],
                          extra_first=[scan_tail_v, scan_tail_s])
    x1_tok = []
    for kc in range(8):
        t1 = P.op("vector", lambda h, kc=kc: h.tensor_tensor(TMPG[:, :], ATT[kc][:, :],
                                                             mean_b[:, :], OP.subtract),
                  [stat2[-1]])
        t2 = P.op("vector", lambda h, kc=kc: h.tensor_tensor(TMPG[:, :], TMPG[:, :], rstd_b[:, :],
                                                             OP.mult), [t1])
        t3 = P.op("vector", lambda h, kc=kc: h.tensor_tensor(X[kc][:, :], X[kc][:, :], TMPG[:, :],
                                                             OP.add), [t2])
        t4 = P.dma("sync", x1T[128 * kc:128 * (kc + 1), :], X[kc][:, :], [t3])
        t5 = P.op("vector", lambda h, kc=kc: h.tensor_copy(LB[kc][:, :], X[kc][:, :]), [t3])
        x1_tok.append((t3, t4, t5))

    stat3 = emit_ln_stats(P, LB, lambda h, kc, dst: h.activation(dst[:, :], X[kc][:, :],
                                                                 AF.Square),
                          mean_b, rstd_b, pb[0:4], pb[4:8], ones_b, 8, NT, SQ, TMPF, epscol,
                          waits_per_kc=[x1_tok[kc][2] for kc in range(8)],
                          extra_first=[P.tok("vector"), P.tok("scalar")])
    fin = []
    dma_prev = [None, None]
    for kc in range(8):
        buf = TMPG if kc % 2 == 0 else TMPF
        t1 = P.op("vector", lambda h, kc=kc, buf=buf: h.tensor_tensor(
            buf[:, :], X[kc][:, :], mean_b[:, :], OP.subtract),
            [stat3[-1], dma_prev[kc % 2]])
        t2 = P.op("vector", lambda h, kc=kc, buf=buf: h.tensor_tensor(
            buf[:, :], buf[:, :], rstd_b[:, :], OP.mult), [t1])
        t3 = P.dma("sync", y0T[128 * kc:128 * (kc + 1), :], buf[:, :], [t2])
        dma_prev[kc % 2] = t3
        fin.append(t3)
    final = fin + [t[1] for t in x1_tok]
    return P.finish(final_waits=final)


def host_maps_A(inputs):
    t_idx = np.arange(128)
    tri = np.zeros((128, 128), np.float32)
    tri[t_idx[:, None] <= t_idx[None, :]] = 1.0
    base = {
        "wqT": np.ascontiguousarray(inputs["L0_attn_w_q"].T),
        "wkT": np.ascontiguousarray(inputs["L0_attn_w_k"].T),
        "wvT": np.ascontiguousarray(inputs["L0_attn_w_v"].T),
        "woT": np.ascontiguousarray(inputs["L0_attn_w_o"].T),
        "memI": np.ascontiguousarray(inputs["L0_attn_memory"]),
        "mnI": np.ascontiguousarray(inputs["L0_attn_memory_norm"].reshape(256, 1)),
        "gateI": np.ascontiguousarray(inputs["L0_attn_gate"].reshape(1, 1)),
        "triI": tri,
        "identI": np.eye(128, dtype=np.float32),
    }
    x0 = inputs["L0_attn_hidden_states"]
    maps = []
    for c in range(NCORE):
        m = dict(base)
        m["x0T"] = np.ascontiguousarray(x0[c].T)
        maps.append(m)
    return maps





def build_launch_Battn(debug_w=None, serial=False):
    P = Prog("Ba", serial=serial)
    W = 256
    NW = NT // W

    yT = P.dram("yT", [DM, NT])
    qkvI = P.dram("qkvI", [DM, 3 * DM])
    woI = P.dram("woI", [DM, DM])
    aqI = P.dram("aqI", [128, NT])
    bqI = P.dram("bqI", [128, NT])
    akI = P.dram("akI", [128, NT])
    bkI = P.dram("bkI", [128, NT])
    qsI = P.dram("qsI", [128, 2])
    ksI = P.dram("ksI", [128, 2])
    maskI = P.dram("maskI", [512, 512])
    identI = P.dram("identI", [128, 128])
    permI = P.dram("permI", [128, 128])
    youtT = P.dram("youtT", [DM, NT], kind="ExternalOutput")
    if debug_w is not None:
        qrD = P.dram("qrD", [128, 2048], kind="ExternalOutput")
        krD = P.dram("krD", [128, 2048], kind="ExternalOutput")
        vtD = P.dram("vtD", [128, 2048], kind="ExternalOutput")
        aoD = P.dram("aoD", [128, 2048], kind="ExternalOutput")
        ybD = P.dram("ybD", [128, 2056], kind="ExternalOutput")
        simD = P.dram("simD", [128, 512], kind="ExternalOutput")
        avD = P.dram("avD", [64, 128], kind="ExternalOutput")
        prbD = P.dram("prbD", [128, 512], kind="ExternalOutput")
        ptrD = P.dram("ptrD", [128, 512], kind="ExternalOutput")

    Y = [P.sb(f"Y{i}", 128, NT + 1) for i in range(8)]
    qkvb = P.sb("qkvb", 128, 24576, BF16)
    wo2b = P.sb("wo2b", 128, 8192, BF16)
    AQ = P.sb("AQ", 128, NT, BF16)
    BQ = P.sb("BQ", 128, NT, BF16)
    AK = P.sb("AK", 128, NT, BF16)
    BK = P.sb("BK", 128, NT, BF16)
    qsc = P.sb("qsc", 128, 2)
    ksc = P.sb("ksc", 128, 2)
    identb = P.sb("identb", 128, 128, BF16)
    permb = P.sb("permb", 128, 128, BF16)
    maskb = P.sb("maskb", 128, 2048, BF16)
    hmaskb = P.sb("hmaskb", 128, 128, BF16)
    YBc = [P.sb(f"YBc{i}", 128, 8 * 257, BF16) for i in range(2)]
    QR = P.sb("QR", 128, 2048, BF16)
    KR = [P.sb(f"KR{i}", 128, 2048, BF16) for i in range(2)]
    VT = [P.sb(f"VT{i}", 128, 2048, BF16) for i in range(2)]
    AO = P.sb("AO", 128, 2048, BF16)
    PRB2 = [P.sb(f"PRB{i}", 128, 512, BF16) for i in range(4)]
    RQN = P.sb("RQN", 128, 256)
    QN = P.sb("QN", 128, 256, BF16)
    SWP = P.sb("SWP", 128, 256, BF16)
    T256 = P.sb("T256", 128, 256, BF16)
    nbias = P.sb("nbias", 128, 4)
    rsum = P.sb("rsum", 128, 4)

    pb = [P.ps(f"pb{i}", 128, 512) for i in range(8)]

    ldy = [P.dma("sync", Y[i][:, 1:NT + 1], yT[128 * i:128 * (i + 1), :]) for i in range(8)]
    zpad = [P.op("gpsimd", lambda h, i=i: h.memset(Y[i][:, 0:1], 0.0)) for i in range(8)]
    for kc in range(8):
        P.dma("gpsimd", qkvb[:, 3072 * kc:3072 * (kc + 1)], qkvI[128 * kc:128 * (kc + 1), :])
        P.dma("gpsimd", wo2b[:, 1024 * kc:1024 * (kc + 1)], woI[128 * kc:128 * (kc + 1), :])
    P.dma("gpsimd", AQ[:, :], aqI[:, :])
    P.dma("gpsimd", BQ[:, :], bqI[:, :])
    P.dma("gpsimd", AK[:, :], akI[:, :])
    P.dma("gpsimd", BK[:, :], bkI[:, :])
    P.dma("sync", qsc[:, :], qsI[:, :])
    P.dma("sync", ksc[:, :], ksI[:, :])
    P.dma("gpsimd", identb[:, :], identI[:, :])
    P.dma("gpsimd", permb[:, :], permI[:, :])
    for t in range(4):
        P.dma("gpsimd", maskb[:, 512 * t:512 * (t + 1)], maskI[128 * t:128 * (t + 1), :])
    wall = P.tok("gpsimd")
    wsync = P.tok("sync")

    P.op("gpsimd", lambda h: h.memset(hmaskb[:, :], 0.0))
    P.op("gpsimd", lambda h: h.memset(hmaskb[0:64, 0:64], 1.0))
    hm_tok = P.op("gpsimd", lambda h: h.memset(hmaskb[64:128, 64:128], 1.0))

    f1 = P.op("vector", lambda h: h.tensor_scalar(AQ[:, :], AQ[:, :], qsc[:, 0:1], None, OP.mult),
              [wall, wsync])
    f2 = P.op("vector", lambda h: h.tensor_scalar(BQ[:, :], BQ[:, :], qsc[:, 1:2], None, OP.mult), [f1])
    f3 = P.op("vector", lambda h: h.tensor_scalar(AK[:, :], AK[:, :], ksc[:, 0:1], None, OP.mult), [f2])
    tabs = P.op("vector", lambda h: h.tensor_scalar(BK[:, :], BK[:, :], ksc[:, 1:2], None, OP.mult), [f3])

    def cast_win(w, slot, extra=()):
        toks = []
        for kc in range(8):
            lo = 256 * w
            t = P.op("scalar", lambda h, kc=kc, lo=lo, slot=slot: h.copy(
                YBc[slot][:, 257 * kc:257 * (kc + 1)], Y[kc][:, lo:lo + 257]),
                list(extra) + ([ldy[kc], zpad[kc]] if w == 0 else []))
            toks.append(t)
        return toks

    cast_tok = {0: cast_win(0, 0, [wall])}

    prev_resid = [None] * 8
    prev_banks = {"qk": None, "vT": None, "ssq": None, "rb": None, "sim": [None, None],
                  "ptr": None, "av": None, "out": None}
    kr_war = [None, None]

    for w in range(NW):
        slot = w % 2
        c0 = 256 * w
        csl = slice(c0, c0 + 256)
        if w + 1 < NW:
            cast_tok[w + 1] = cast_win(w + 1, (w + 1) % 2,
                                       [prev_resid[kc] for kc in range(8)])
        ycast = cast_tok[w]

        def q_mm(h, slot=slot):
            r = None
            for m in range(8):
                bank, off = divmod(m, 2)
                for kc in range(8):
                    shift = 0 if kc < 4 else 1
                    rhs = YBc[slot][:, 257 * kc + 1 - shift:257 * kc + 257 - shift]
                    r = h.matmul(pb[bank][:, 256 * off:256 * off + 256],
                                 qkvb[:, 3072 * kc + 128 * m:3072 * kc + 128 * (m + 1)],
                                 rhs, start=(kc == 0 and off == 0), stop=(kc == 7),
                                 skip_group_check=True)
            return r
        t_qp = P.op("tensor", q_mm, ycast + [prev_banks["qk"], tabs])

        ssq_tok = None
        qr_tok = []
        for mt in range(8):
            bank, off = divmod(mt, 2)
            qsl = (pb[bank], 256 * off)
            t_s = P.op("scalar", lambda h, qsl=qsl: h.activation(
                T256[:, :], qsl[0][:, qsl[1]:qsl[1] + 256], AF.Square), [t_qp, ssq_tok])
            t_mm = P.op("tensor", lambda h: h.matmul(pb[4][:, 0:256], hmaskb[:, :], T256[:, :],
                                                     start=True, stop=True, skip_group_check=True),
                        [t_s, hm_tok, prev_banks["ssq"]])
            t_r1 = P.op("scalar", lambda h: h.activation(RQN[:, :], pb[4][:, 0:256], AF.Sqrt),
                        [t_mm])
            t_r2 = P.op("vector", lambda h: h.tensor_scalar(RQN[:, :], RQN[:, :], 1e-12, None,
                                                            OP.max), [t_r1])
            t_r3 = P.op("vector", lambda h: h.reciprocal(RQN[:, :], RQN[:, :]), [t_r2])
            t_qn = P.op("vector", lambda h, qsl=qsl: h.tensor_tensor(
                QN[:, :], qsl[0][:, qsl[1]:qsl[1] + 256], RQN[:, :], OP.mult), [t_r3])
            t_sw = P.op("tensor", lambda h: h.matmul(pb[4][:, 256:512], permb[:, :],
                                                     QN[:, :], start=True, stop=True,
                                                     skip_group_check=True),
                        [t_qn, prev_banks["ssq"]])
            t_a = P.op("vector", lambda h, mt=mt, csl=csl: h.tensor_tensor(
                T256[:, :], QN[:, :], AQ[:, csl], OP.mult), [t_qn])
            t_b = P.op("vector", lambda h, mt=mt, csl=csl: h.tensor_tensor(
                SWP[:, :], pb[4][:, 256:512], BQ[:, csl], OP.mult), [t_sw])
            t_qr = P.op("vector", lambda h, mt=mt: h.tensor_tensor(
                QR[:, 256 * mt:256 * (mt + 1)], T256[:, :], SWP[:, :], OP.add), [t_a, t_b])
            ssq_tok = t_qr
            qr_tok.append(t_qr)

        def k_mm(h, slot=slot):
            r = None
            for m in range(8):
                bank, off = divmod(m, 2)
                for kc in range(8):
                    shift = 0 if kc < 4 else 1
                    rhs = YBc[slot][:, 257 * kc + 1 - shift:257 * kc + 257 - shift]
                    r = h.matmul(pb[bank][:, 256 * off:256 * off + 256],
                                 qkvb[:, 3072 * kc + 1024 + 128 * m:3072 * kc + 1024 + 128 * (m + 1)],
                                 rhs, start=(kc == 0 and off == 0), stop=(kc == 7),
                                 skip_group_check=True)
            return r
        t_kp = P.op("tensor", k_mm, [qr_tok[-1]])

        kr_tok = []
        ssq_tok2 = None
        for mt in range(8):
            bank, off = divmod(mt, 2)
            qsl = (pb[bank], 256 * off)
            t_s = P.op("scalar", lambda h, qsl=qsl: h.activation(
                T256[:, :], qsl[0][:, qsl[1]:qsl[1] + 256], AF.Square), [t_kp, ssq_tok2])
            t_mm = P.op("tensor", lambda h: h.matmul(pb[4][:, 0:256], hmaskb[:, :], T256[:, :],
                                                     start=True, stop=True, skip_group_check=True),
                        [t_s, prev_banks["ssq"]])
            t_r1 = P.op("scalar", lambda h: h.activation(RQN[:, :], pb[4][:, 0:256], AF.Sqrt), [t_mm])
            t_r2 = P.op("vector", lambda h: h.tensor_scalar(RQN[:, :], RQN[:, :], 1e-12, None,
                                                            OP.max), [t_r1])
            t_r3 = P.op("vector", lambda h: h.reciprocal(RQN[:, :], RQN[:, :]), [t_r2])
            t_qn = P.op("vector", lambda h, qsl=qsl: h.tensor_tensor(
                QN[:, :], qsl[0][:, qsl[1]:qsl[1] + 256], RQN[:, :], OP.mult), [t_r3])
            t_sw = P.op("tensor", lambda h: h.matmul(pb[4][:, 256:512], permb[:, :],
                                                     QN[:, :], start=True, stop=True,
                                                     skip_group_check=True), [t_qn])
            t_a = P.op("vector", lambda h, csl=csl: h.tensor_tensor(
                T256[:, :], QN[:, :], AK[:, csl], OP.mult), [t_qn])
            t_b = P.op("vector", lambda h, csl=csl: h.tensor_tensor(
                SWP[:, :], pb[4][:, 256:512], BK[:, csl], OP.mult), [t_sw])
            t_kr = P.op("vector", lambda h, mt=mt, slot=slot: h.tensor_tensor(
                KR[slot][:, 256 * mt:256 * (mt + 1)], T256[:, :], SWP[:, :], OP.add),
                [t_a, t_b, kr_war[slot]])
            ssq_tok2 = t_kr
            kr_tok.append(t_kr)

        def vT_mm(h, slot=slot):
            r = None
            for ts_ in range(2):
                for nh in range(2):
                    bank = 2 * ts_ + nh
                    for kc in range(8):
                        shift = 0 if kc < 4 else 1
                        rhs = qkvb[:, 3072 * kc + 2048 + 512 * nh:3072 * kc + 2048 + 512 * (nh + 1)]
                        lhsT = YBc[slot][:, 257 * kc + 1 - shift + 128 * ts_:257 * kc + 1 - shift + 128 * (ts_ + 1)]
                        r = h.matmul(pb[bank][:, 0:512], lhsT, rhs, start=(kc == 0), stop=(kc == 7),
                                     skip_group_check=True)
            return r
        t_vp = P.op("tensor", vT_mm, [kr_tok[-1]])
        t_ve = []
        for ts_ in range(2):
            for nh in range(2):
                bank = 2 * ts_ + nh
                t = P.op("vector", lambda h, bank=bank, ts_=ts_, nh=nh, slot=slot: h.tensor_copy(
                    VT[slot][:, 1024 * ts_ + 512 * nh:1024 * ts_ + 512 * (nh + 1)],
                    pb[bank][:, 0:512]), [t_vp, kr_war[slot]])
                t_ve.append(t)
        vT_done = t_ve[-1]

        ao_tok = []
        simbk = [pb[5], pb[6], pb[2], pb[3]]
        pbk = [pb[7], pb[4], pb[0], pb[1]]
        sim_war = [prev_banks["sim"][0], prev_banks["sim"][1], vT_done, vT_done]
        ptr_war = [prev_banks["ptr"], kr_tok[-1], vT_done, vT_done]
        av_war = [prev_banks["av"], kr_tok[-1], vT_done, vT_done]
        av_war_pe = [None, None, None, None]
        NU = 32
        U = []
        for hh in range(16):
            for qs_ in range(2):
                k = 2 * hh + qs_
                U.append(dict(p=k % 4, qs_=qs_, hh=hh, tile=hh // 2, pofs=64 * (hh % 2),
                              bank=simbk[k % 4], pbank=pbk[k % 4], PRB=PRB2[k % 4]))

        def st_sim(u, w=w, slot=slot):
            tile = u["tile"]; pofs = u["pofs"]; qs_ = u["qs_"]; bank = u["bank"]; p = u["p"]
            mvar = 2 if w == 0 else 0
            def sim_mm(h, tile=tile, pofs=pofs, qs_=qs_, bank=bank, mvar=mvar, slot=slot, w=w):
                r = h.matmul(bank[:, 0:512], identb[:, :],
                             maskb[:, 512 * (mvar + qs_):512 * (mvar + qs_ + 1)],
                             start=True, stop=False, skip_group_check=True)
                lhsT = QR[pofs:pofs + 64, 256 * tile + 128 * qs_:256 * tile + 128 * (qs_ + 1)]
                if w > 0:
                    r = h.matmul(bank[:, 0:256], lhsT,
                                 KR[1 - slot][pofs:pofs + 64, 256 * tile:256 * (tile + 1)],
                                 start=False, stop=False, skip_group_check=True)
                r = h.matmul(bank[:, 256:512], lhsT,
                             KR[slot][pofs:pofs + 64, 256 * tile:256 * (tile + 1)],
                             start=False, stop=True, skip_group_check=True)
                return r
            u["t_sim"] = P.op("tensor", sim_mm,
                              [qr_tok[tile], kr_tok[tile], sim_war[p], wall])
            sim_war[p] = None

        def st_mx(u):
            bank = u["bank"]; p = u["p"]
            u["t_mx"] = P.op("vector", lambda h, bank=bank, p=p: h.reduce_max(
                nbias[:, p:p + 1], bank[:, 0:512], axis=AX.X, negate=True), [u["t_sim"]])

        def st_ex(u):
            bank = u["bank"]; p = u["p"]; PRB = u["PRB"]
            u["t_ex"] = P.op("scalar", lambda h, bank=bank, p=p, PRB=PRB: h.activation(
                PRB[:, :], bank[:, 0:512], AF.Exp, bias=nbias[:, p:p + 1], scale=1.0,
                accum_out=rsum[:, p:p + 1]), [u["t_mx"], av_war_pe[p]])
            sim_war[p] = u["t_ex"]

        def st_rsn(u):
            p = u["p"]; PRB = u["PRB"]
            t_rs = P.op("vector", lambda h, p=p: h.reciprocal(
                rsum[:, p:p + 1], rsum[:, p:p + 1]), [u["t_ex"]])
            u["t_pn"] = P.op("vector", lambda h, p=p, PRB=PRB: h.tensor_scalar(
                PRB[:, :], PRB[:, :], rsum[:, p:p + 1], None, OP.mult), [t_rs])

        def st_tr(u):
            p = u["p"]; pbank = u["pbank"]; PRB = u["PRB"]
            def ptr_mm(h, pbank=pbank, PRB=PRB):
                r = None
                for j in range(4):
                    r = h.transpose(pbank[:, 64 * j:64 * (j + 1)].bitcast(BF16),
                                    PRB[:, 128 * j:128 * (j + 1)], identb[:, :])
                return r
            u["t_tr"] = P.op("tensor", ptr_mm, [u["t_pn"], ptr_war[p], av_war[p]])

        def st_pte(u):
            p = u["p"]; pbank = u["pbank"]; PRB = u["PRB"]
            u["t_pte"] = P.op("vector", lambda h, pbank=pbank, PRB=PRB: h.tensor_copy(
                PRB[:, 0:512], pbank[:, 0:256].bitcast(BF16)), [u["t_tr"]])
            ptr_war[p] = u["t_pte"]

        def st_av(u, w=w, slot=slot):
            p = u["p"]; hh = u["hh"]; pbank = u["pbank"]; PRB = u["PRB"]
            def av_mm(h, slot=slot, w=w, hh=hh, pbank=pbank, PRB=PRB):
                r = None
                first = True
                for kc4 in range(4):
                    vslot = (1 - slot) if kc4 < 2 else slot
                    if w == 0 and kc4 < 2:
                        continue
                    ts_ = kc4 % 2
                    lhsT = VT[vslot][:, 1024 * ts_ + 64 * hh:1024 * ts_ + 64 * (hh + 1)]
                    rhs = PRB[:, 128 * kc4:128 * (kc4 + 1)]
                    r = h.matmul(pbank[0:64, 256:384], lhsT, rhs, start=first, stop=(kc4 == 3),
                                 skip_group_check=True)
                    first = False
                return r
            u["t_av"] = P.op("tensor", av_mm, [u["t_pte"], vT_done, av_war[p]])
            av_war_pe[p] = u["t_av"]

        def st_ao(u):
            p = u["p"]; tile = u["tile"]; pofs = u["pofs"]; qs_ = u["qs_"]; pbank = u["pbank"]
            u["t_ao"] = P.op("scalar", lambda h, tile=tile, pofs=pofs, qs_=qs_, pbank=pbank: h.copy(
                AO[pofs:pofs + 64, 256 * tile + 128 * qs_:256 * tile + 128 * (qs_ + 1)],
                pbank[0:64, 256:384]), [u["t_av"]])
            av_war[p] = u["t_ao"]
            ao_tok.append(u["t_ao"])

        STAGES = [(st_ao, 7), (st_av, 6), (st_pte, 5), (st_tr, 4),
                  (st_rsn, 3), (st_ex, 2), (st_mx, 1), (st_sim, 0)]
        for i in range(NU + 7):
            for fn, lag in STAGES:
                k = i - lag
                if 0 <= k < NU:
                    fn(U[k])

        if debug_w is not None and w == debug_w:
            dbg_toks = []
            dbg_toks.append(P.dma("gpsimd", qrD[:, :], QR[:, :], ao_tok))
            dbg_toks.append(P.dma("gpsimd", krD[:, :], KR[slot][:, :], ao_tok))
            dbg_toks.append(P.dma("gpsimd", vtD[:, :], VT[slot][:, :], ao_tok))
            dbg_toks.append(P.dma("gpsimd", aoD[:, :], AO[:, :], ao_tok))
            dbg_toks.append(P.dma("gpsimd", ybD[:, :], YBc[slot][:, :], ao_tok))
            P._dbg = dbg_toks

        def out_mm(h):
            r = None
            for m in range(8):
                bank, off = divmod(m, 2)
                for kc in range(8):
                    r = h.matmul(pb[bank][:, 256 * off:256 * off + 256],
                                 wo2b[:, 1024 * kc + 128 * m:1024 * kc + 128 * (m + 1)],
                                 AO[:, 256 * kc:256 * (kc + 1)],
                                 start=(kc == 0 and off == 0), stop=(kc == 7),
                                 skip_group_check=True)
            return r
        t_om = P.op("tensor", out_mm, ao_tok + [vT_done])
        res_tok = []
        for m in range(8):
            bank, off = divmod(m, 2)
            t = P.op("vector", lambda h, m=m, bank=bank, off=off, c0=c0: h.tensor_tensor(
                Y[m][:, c0 + 1:c0 + 257], Y[m][:, c0 + 1:c0 + 257],
                pb[bank][:, 256 * off:256 * off + 256], OP.add),
                [t_om] + ([cast_tok[w + 1][m]] if w + 1 < NW else []))
            res_tok.append(t)
        prev_resid = res_tok

        prev_banks = {"qk": res_tok[-1], "vT": res_tok[-1], "ssq": t_kr,
                      "rb": None, "sim": [sim_war[0], sim_war[1]], "ptr": ptr_war[0],
                      "av": av_war[0], "out": res_tok[-1], "res": res_tok[-1]}
        kr_war[0] = t_om
        kr_war[1] = t_om

    fin = []
    for kc in range(8):
        fin.append(P.dma("sync", youtT[128 * kc:128 * (kc + 1), :], Y[kc][:, 1:NT + 1],
                         [prev_resid[kc]]))
    if debug_w is not None:
        fin = fin + P._dbg + getattr(P, "_dbg2", [])
    return P.finish(final_waits=fin)


def _rotary_tables_host():
    DH, WIN = 64, 256
    half = np.arange(0, DH, 2, dtype=np.float64)
    inv_freq = 1.0 / 10000.0 ** (half / DH)
    t = np.arange(NT, dtype=np.float64)
    fh = t[:, None] * inv_freq[None, :]
    freqs = np.concatenate([fh, fh], -1)
    base = (half + 0.4 * DH) / (1.4 * DH)
    power = (t - NT // 2) / (WIN // 2)
    sc = base[None, :] ** power[:, None]
    scale = np.concatenate([sc, sc], -1)
    cos, sin = np.cos(freqs), np.sin(freqs)
    sign = np.where(np.arange(DH) < 32, -1.0, 1.0)[None, :]
    AQt = (cos * scale * 8.0).T
    BQt = (sign * sin * scale * 8.0).T
    AKt = (cos / scale).T
    BKt = (sign * sin / scale).T
    def dup(a):
        return np.ascontiguousarray(np.concatenate([a, a], 0).astype(np.float32))
    return dup(AQt), dup(BQt), dup(AKt), dup(BKt)


def _attn_masks_host():
    m = np.zeros((512, 512), np.float32)
    NEG = -8e9
    for qs_ in range(2):
        for i in range(128):
            iloc = qs_ * 128 + i
            for j in range(256):
                if j < iloc:
                    m[qs_ * 128 + i, j] = NEG
                if j > iloc:
                    m[qs_ * 128 + i, 256 + j] = NEG
            m[256 + qs_ * 128 + i, 0:256] = NEG
            m[256 + qs_ * 128 + i, 256:512] = m[qs_ * 128 + i, 256:512]
    return m


def _perm_swap_host():
    p = np.zeros((128, 128), np.float32)
    for i in range(128):
        j = i + 32 if (i % 64) < 32 else i - 32
        p[j, i] = 1.0
    return p


def host_maps_Battn(inputs, y_cur, depth):
    AQt, BQt, AKt, BKt = _rotary_tables_host()
    qs = inputs["L0_ffn_q_scale"][depth]
    ks = inputs["L0_ffn_k_scale"][depth]
    qs128 = np.tile(qs, 2).astype(np.float32)
    ks128 = np.tile(ks, 2).astype(np.float32)
    swap = np.concatenate([np.arange(32, 64), np.arange(0, 32)])
    qs_sw = np.tile(qs[swap], 2).astype(np.float32)
    ks_sw = np.tile(ks[swap], 2).astype(np.float32)
    base = {
        "qkvI": np.ascontiguousarray(inputs["L0_ffn_Wqkv"][depth]),
        "woI": np.ascontiguousarray(inputs["L0_ffn_Wo"][depth]),
        "aqI": AQt, "bqI": BQt, "akI": AKt, "bkI": BKt,
        "qsI": np.stack([qs128, qs_sw], 1).copy(),
        "ksI": np.stack([ks128, ks_sw], 1).copy(),
        "maskI": _attn_masks_host(),
        "identI": np.eye(128, dtype=np.float32),
        "permI": _perm_swap_host(),
    }
    maps = []
    for c in range(NCORE):
        m = dict(base)
        m["yT"] = np.ascontiguousarray(y_cur[c])
        maps.append(m)
    return maps





FB = [512, 512, 512, 512, 512, 170]




def build_launch_Bffn(debug=False, serial=False):
    P = Prog("Bf", serial=serial)
    CHT = 1024

    yT = P.dram("yT", [DM, NT])
    w1I = P.dram("w1I", [DM, 5460])
    w2I = P.dram("w2I", [2730, DM])
    b1aI = P.dram("b1aI", [2730, 1])
    b1gI = P.dram("b1gI", [2730, 1])
    b2I = P.dram("b2I", [DM, 1])
    lngI = P.dram("lngI", [DM, 1])
    lnbI = P.dram("lnbI", [DM, 1])
    youtT = P.dram("youtT", [DM, NT], kind="ExternalOutput")
    if debug:
        xnbD = P.dram("xnbD", [DM, CHT], kind="ExternalOutput")
        outD = P.dram("outD", [DM, CHT], kind="ExternalOutput")
        hbD = P.dram("hbD", [128, 4096], kind="ExternalOutput")

    Y = [P.sb(f"Y{i}", 128, NT + 1) for i in range(8)]
    YC = [P.sb(f"YC{i}", 128, 8 * (CHT + 1), BF16) for i in range(2)]
    XNB = [P.sb(f"XNB{i}", 128, CHT, BF16) for i in range(8)]
    OUT = [P.sb(f"OUT{i}", 128, CHT) for i in range(8)]
    Hb = P.sb("Hb", 128, 4096, BF16)
    GEL = P.sb("GEL", 128, 512, BF16)
    w1aB = P.sb("w1aB", 128, 4096, BF16)
    w1gB = P.sb("w1gB", 128, 4096, BF16)
    w2B = P.sb("w2B", 128, 4096, BF16)
    b1aC = P.sb("b1aC", 128, 22)
    b1gC = P.sb("b1gC", 128, 22)
    b2C = P.sb("b2C", 128, 8)
    lngC = P.sb("lngC", 128, 8)
    lnbC = P.sb("lnbC", 128, 8)
    ones_b = P.sb("ones_b", 128, 128, BF16)
    epscol = P.sb("epscol", 128, 1)
    mean_b = P.sb("mean_b", 128, CHT)
    rstd_b = P.sb("rstd_b", 128, CHT)
    SQ = [P.sb(f"SQ{i}", 128, CHT, BF16) for i in range(2)]
    TMPF = P.sb("TMPF", 128, CHT)
    TMPG = P.sb("TMPG", 128, CHT)

    pb = [P.ps(f"pb{i}", 128, 512) for i in range(8)]

    ldy = [P.dma("sync", Y[i][:, 1:NT + 1], yT[128 * i:128 * (i + 1), :]) for i in range(8)]
    zpad = [P.op("gpsimd", lambda h, i=i: h.memset(Y[i][:, 0:1], 0.0)) for i in range(8)]
    for t in range(22):
        p0 = 128 * t
        p1 = min(p0 + 128, 2730)
        P.dma("sync", b1aC[0:p1 - p0, t:t + 1], b1aI[p0:p1, :])
        P.dma("sync", b1gC[0:p1 - p0, t:t + 1], b1gI[p0:p1, :])
    for t in range(8):
        P.dma("sync", b2C[:, t:t + 1], b2I[128 * t:128 * (t + 1), :])
        P.dma("sync", lngC[:, t:t + 1], lngI[128 * t:128 * (t + 1), :])
        P.dma("sync", lnbC[:, t:t + 1], lnbI[128 * t:128 * (t + 1), :])
    P.op("gpsimd", lambda h: h.memset(ones_b[:, :], 1.0))
    c_eps = P.op("gpsimd", lambda h: h.memset(epscol[:, :], 1e-5))
    wsync = P.tok("sync")
    wg = P.tok("gpsimd")

    cast_tok = {}
    for j in range(2):
        toks = []
        for kc in range(8):
            lo = CHT * j
            t = P.op("scalar", lambda h, kc=kc, lo=lo, j=j: h.copy(
                YC[j][:, (CHT + 1) * kc:(CHT + 1) * (kc + 1)], Y[kc][:, lo:lo + CHT + 1]),
                [ldy[kc], zpad[kc]])
            toks.append(t)
        cast_tok[j] = toks

    def ycs(j, kc):
        base = (CHT + 1) * kc
        sh = 0 if kc < 4 else 1
        return YC[j][:, base + 1 - sh:base + 1 - sh + CHT]

    prev_w2_dma_war = None
    prev_a_read = None
    prev_g_read = None
    prev_w2_read = None
    prev_wmm = None
    stat_war = [None]
    for j in range(2):
        csl = slice(CHT * j, CHT * (j + 1))
        rhs_tiles = [ycs(j, kc) for kc in range(8)]
        stat = emit_ln_stats(P, rhs_tiles,
                             lambda h, kc, dst, j=j: h.activation(dst[:, :], ycs(j, kc), AF.Square),
                             mean_b, rstd_b, pb[0:2], pb[2:4], ones_b,
                             8, CHT, SQ, TMPF, epscol,
                             waits_per_kc=[cast_tok[j][kc] for kc in range(8)],
                             extra_first=[wg, wsync, c_eps] + ([prev_a_read, prev_g_read,
                                                                prev_w2_read] if j else []))
        xnb_tok = []
        for kc in range(8):
            t1 = P.op("vector", lambda h, kc=kc, j=j: h.tensor_tensor(
                TMPF[:, :], ycs(j, kc), mean_b[:, :], OP.subtract), [stat[-1]])
            t2 = P.op("vector", lambda h, kc=kc: h.tensor_tensor(
                TMPG[:, :], TMPF[:, :], rstd_b[:, :], OP.mult), [t1])
            t3 = P.op("vector", lambda h, kc=kc: h.tensor_scalar(
                XNB[kc][:, :], TMPG[:, :], lngC[:, kc:kc + 1], lnbC[:, kc:kc + 1],
                OP.mult, OP.add), [t2])
            xnb_tok.append(t3)
        xnb_all = xnb_tok[-1]

        for b, BW in enumerate(FB):
            NMI = _cdiv(BW, 128)
            wd = []
            for kc in range(8):
                wd.append(P.dma("gpsimd", w1aB[:, 512 * kc:512 * kc + BW],
                                w1I[128 * kc:128 * (kc + 1), 512 * b:512 * b + BW],
                                [prev_wmm]))
                wd.append(P.dma("gpsimd", w1gB[:, 512 * kc:512 * kc + BW],
                                w1I[128 * kc:128 * (kc + 1), 2730 + 512 * b:2730 + 512 * b + BW],
                                [prev_wmm]))
            for ks in range(NMI):
                p0 = 512 * b + 128 * ks
                p1 = min(p0 + 128, 2730)
                wd.append(P.dma("gpsimd", w2B[0:p1 - p0, 1024 * ks:1024 * (ks + 1)],
                                w2I[p0:p1, :], [prev_wmm]))
            wtok = wd[-1]

            for nt in range(2):
                nsl = slice(512 * nt, 512 * (nt + 1))
                def a_mm(h, b=b, BW=BW, nt=nt, NMI=NMI):
                    r = None
                    for mi in range(NMI):
                        mw = min(128, BW - 128 * mi)
                        for kc in range(8):
                            r = h.matmul(pb[mi][0:mw, 0:512],
                                         w1aB[:, 512 * kc + 128 * mi:512 * kc + 128 * mi + mw],
                                         XNB[kc][:, 512 * nt:512 * (nt + 1)],
                                         start=(kc == 0), stop=(kc == 7), skip_group_check=True)
                    return r
                t_amm = P.op("tensor", a_mm, [xnb_all, wtok, prev_a_read])

                def g_mm(h, b=b, BW=BW, nt=nt, NMI=NMI):
                    r = None
                    for mi in range(NMI):
                        mw = min(128, BW - 128 * mi)
                        for kc in range(8):
                            r = h.matmul(pb[4 + mi][0:mw, 0:512],
                                         w1gB[:, 512 * kc + 128 * mi:512 * kc + 128 * mi + mw],
                                         XNB[kc][:, 512 * nt:512 * (nt + 1)],
                                         start=(kc == 0), stop=(kc == 7), skip_group_check=True)
                    return r
                t_gmm = P.op("tensor", g_mm, [xnb_all, wtok, prev_g_read])

                h_tok = []
                prev_th = None
                for mi in range(NMI):
                    mw = min(128, BW - 128 * mi)
                    ct = 4 * b + mi
                    t_gl = P.op("scalar", lambda h, mi=mi, mw=mw, ct=ct: h.activation(
                        GEL[0:mw, :], pb[4 + mi][0:mw, 0:512], AF.Gelu,
                        bias=b1gC[0:mw, ct:ct + 1]), [t_gmm, prev_th])
                    t_ad = P.op("vector", lambda h, mi=mi, mw=mw, ct=ct: h.tensor_scalar(
                        TMPF[0:mw, 0:512], pb[mi][0:mw, 0:512], b1aC[0:mw, ct:ct + 1], None,
                        OP.add), [t_amm, t_gl])
                    t_h = P.op("vector", lambda h, mi=mi, mw=mw: h.tensor_tensor(
                        Hb[0:mw, 1024 * mi + 512 * 0:1024 * mi + 512], TMPF[0:mw, 0:512],
                        GEL[0:mw, :], OP.mult), [t_ad])
                    prev_th = t_h
                    h_tok.append(t_h)
                h_all = h_tok[-1]
                prev_a_read = h_all
                prev_g_read = h_all

                def w2_mm(h, b=b, BW=BW, NMI=NMI):
                    r = None
                    for m in range(8):
                        for ks in range(NMI):
                            kw = min(128, BW - 128 * ks)
                            r = h.matmul(pb[m][:, 0:512],
                                         w2B[0:kw, 1024 * ks + 128 * m:1024 * ks + 128 * (m + 1)],
                                         Hb[0:kw, 1024 * ks:1024 * ks + 512],
                                         start=(ks == 0), stop=(ks == NMI - 1),
                                         skip_group_check=True)
                    return r
                t_w2 = P.op("tensor", w2_mm, [h_all, prev_w2_read, prev_a_read])
                prev_wmm = t_w2

                acc_tok = []
                for m in range(8):
                    if b == 0:
                        t = P.op("vector", lambda h, m=m, nsl=nsl: h.tensor_copy(
                            OUT[m][:, nsl], pb[m][:, 0:512]), [t_w2])
                    else:
                        t = P.op("vector", lambda h, m=m, nsl=nsl: h.tensor_tensor(
                            OUT[m][:, nsl], OUT[m][:, nsl], pb[m][:, 0:512], OP.add), [t_w2])
                    acc_tok.append(t)
                prev_w2_read = acc_tok[-1]
                prev_a_read = acc_tok[-1]
                prev_g_read = acc_tok[-1]

        if debug and j == 0:
            P._dbgf = [P.dma("gpsimd", xnbD[128 * kc:128 * (kc + 1), :], XNB[kc][:, :], [xnb_all])
                       for kc in range(8)]
            P._dbgf += [P.dma("gpsimd", outD[128 * m:128 * (m + 1), :], OUT[m][:, :],
                              [prev_w2_read]) for m in range(8)]
            P._dbgf.append(P.dma("gpsimd", hbD[:, :], Hb[:, :], [prev_w2_read]))

        res_tok = []
        for m in range(8):
            t1 = P.op("vector", lambda h, m=m: h.tensor_scalar(
                OUT[m][:, :], OUT[m][:, :], b2C[:, m:m + 1], None, OP.add), [prev_w2_read])
            t2 = P.op("vector", lambda h, m=m, j=j: h.tensor_tensor(
                Y[m][:, CHT * j + 1:CHT * (j + 1) + 1], Y[m][:, CHT * j + 1:CHT * (j + 1) + 1],
                OUT[m][:, :], OP.add), [t1] + (cast_tok[1] if j == 0 else []))
            res_tok.append(t2)
        prev_a_read = res_tok[-1]
        prev_g_read = res_tok[-1]
        prev_w2_read = res_tok[-1]

    fin = [P.dma("sync", youtT[128 * kc:128 * (kc + 1), :], Y[kc][:, 1:NT + 1],
                 [res_tok[kc]]) for kc in range(8)]
    if debug:
        fin = fin + P._dbgf
    return P.finish(final_waits=fin)


def host_maps_Bffn(inputs, y_cur, depth):
    base = {
        "w1I": np.ascontiguousarray(inputs["L0_ffn_W1"][depth]),
        "w2I": np.ascontiguousarray(inputs["L0_ffn_W2"][depth]),
        "b1aI": np.ascontiguousarray(inputs["L0_ffn_b1"][depth][:2730].reshape(-1, 1)),
        "b1gI": np.ascontiguousarray(inputs["L0_ffn_b1"][depth][2730:].reshape(-1, 1)),
        "b2I": np.ascontiguousarray(inputs["L0_ffn_b2"][depth].reshape(-1, 1)),
        "lngI": np.ascontiguousarray(inputs["L0_ffn_ln_g"][depth].reshape(-1, 1)),
        "lnbI": np.ascontiguousarray(inputs["L0_ffn_ln_b"][depth].reshape(-1, 1)),
    }
    maps = []
    for c in range(NCORE):
        m = dict(base)
        m["yT"] = np.ascontiguousarray(y_cur[c])
        maps.append(m)
    return maps





def build_launch_C():
    P = Prog("C")
    aT = P.dram("aT", [DM, NT])
    bT = P.dram("bT", [DM, NT])
    o1T = P.dram("o1T", [DM, NT], kind="ExternalOutput")
    o2T = P.dram("o2T", [DM, NT], kind="ExternalOutput")

    A_ = [P.sb(f"A{i}", 128, NT) for i in range(8)]
    B_ = [P.sb(f"B{i}", 128, NT, BF16) for i in range(8)]
    Bf = P.sb("Bf", 128, NT)
    SQ = [P.sb(f"SQ{i}", 128, NT, BF16) for i in range(2)]
    TMPF = P.sb("TMPF", 128, NT)
    TMPG = P.sb("TMPG", 128, NT)
    mean_b = P.sb("mean_b", 128, NT)
    rstd_b = P.sb("rstd_b", 128, NT)
    ones_b = P.sb("ones_b", 128, 128, BF16)
    epscol = P.sb("epscol", 128, 1)
    pb = [P.ps(f"pb{i}", 128, 512) for i in range(8)]

    lda = [P.dma("sync", A_[i][:, :], aT[128 * i:128 * (i + 1), :]) for i in range(8)]
    ldb = [P.dma("gpsimd", B_[i][:, :], bT[128 * i:128 * (i + 1), :]) for i in range(8)]
    P.op("gpsimd", lambda h: h.memset(ones_b[:, :], 1.0))
    c_eps = P.op("gpsimd", lambda h: h.memset(epscol[:, :], 1e-5))

    stat = emit_ln_stats(P, B_, lambda h, kc, dst: h.activation(dst[:, :], B_[kc][:, :], AF.Square),
                         mean_b, rstd_b, pb[0:4], pb[4:8], ones_b, 8, NT, SQ, TMPF, epscol,
                         waits_per_kc=[ldb[kc] for kc in range(8)], extra_first=[c_eps])
    x_tok = []
    for kc in range(8):
        t1 = P.op("vector", lambda h, kc=kc: h.tensor_tensor(TMPG[:, :], B_[kc][:, :],
                                                             mean_b[:, :], OP.subtract), [stat[-1]])
        t2 = P.op("vector", lambda h, kc=kc: h.tensor_tensor(TMPG[:, :], TMPG[:, :],
                                                             rstd_b[:, :], OP.mult), [t1])
        t3 = P.op("vector", lambda h, kc=kc: h.tensor_tensor(A_[kc][:, :], A_[kc][:, :],
                                                             TMPG[:, :], OP.add), [t2, lda[kc]])
        t4 = P.dma("sync", o1T[128 * kc:128 * (kc + 1), :], A_[kc][:, :], [t3])
        t5 = P.op("vector", lambda h, kc=kc: h.tensor_copy(B_[kc][:, :], A_[kc][:, :]), [t3])
        x_tok.append((t3, t4, t5))

    stat2 = emit_ln_stats(P, B_, lambda h, kc, dst: h.activation(dst[:, :], A_[kc][:, :], AF.Square),
                          mean_b, rstd_b, pb[0:4], pb[4:8], ones_b, 8, NT, SQ, TMPF, epscol,
                          waits_per_kc=[x_tok[kc][2] for kc in range(8)],
                          extra_first=[P.tok("vector"), P.tok("scalar")])
    fin = []
    dma_prev = [None, None]
    for kc in range(8):
        buf = TMPG if kc % 2 == 0 else TMPF
        t1 = P.op("vector", lambda h, kc=kc, buf=buf: h.tensor_tensor(
            buf[:, :], A_[kc][:, :], mean_b[:, :], OP.subtract), [stat2[-1], dma_prev[kc % 2]])
        t2 = P.op("vector", lambda h, kc=kc, buf=buf: h.tensor_tensor(
            buf[:, :], buf[:, :], rstd_b[:, :], OP.mult), [t1])
        t3 = P.dma("sync", o2T[128 * kc:128 * (kc + 1), :], buf[:, :], [t2])
        dma_prev[kc % 2] = t3
        fin.append(t3)
    return P.finish(final_waits=fin + [t[1] for t in x_tok])


def host_maps_C(a_cur, b_cur):
    return [{"aT": np.ascontiguousarray(a_cur[c]), "bT": np.ascontiguousarray(b_cur[c])}
            for c in range(NCORE)]







def build_launch_K2a(serial=False, debug=False):
    P = Prog("K2", serial=serial)
    NTOK = 2048
    ln2sI = P.dram("ln2sI", [8192, 256])
    qwI = P.dram("qwI", [DM, DM])
    qwTI = P.dram("qwTI", [DM, DM])
    vwI = P.dram("vwI", [DM, DM])
    owTI = P.dram("owTI", [DM, DM])
    obI = P.dram("obI", [DM, 1])
    m16I = P.dram("m16I", [128, 128])
    emI = P.dram("emI", [16, 1024])
    identI = P.dram("identI", [128, 128])
    if debug:
        qD = P.dram("qD", [128, NTOK], kind="ExternalOutput")
        x3aD = P.dram("x3aD", [128, NTOK], kind="ExternalOutput")
        ln3aD = P.dram("ln3aD", [128, NTOK], kind="ExternalOutput")
        shpD = P.dram("shpD", [128, NTOK], kind="ExternalOutput")
        mbD = P.dram("mbD", [128, NTOK], kind="ExternalOutput")
        rbD = P.dram("rbD", [128, NTOK], kind="ExternalOutput")

        bD = P.dram("bD", [128, 8192], kind="ExternalOutput")
        qkD = P.dram("qkD", [16, 16384], kind="ExternalOutput")
        pnD = P.dram("pnD", [16, 16384], kind="ExternalOutput")
        xtD = P.dram("xtD", [128, NTOK], kind="ExternalOutput")
        pxD = P.dram("pxD", [128, NTOK], kind="ExternalOutput")
        a3D = P.dram("a3D", [128, NTOK], kind="ExternalOutput")

    LNB = [P.sb(f"LNB{i}", 128, NTOK, BF16) for i in range(8)]
    XT = P.sb("XT", 128, NTOK, BF16)
    wb1 = P.sb("wb1", 128, 8192, BF16)
    wb2 = P.sb("wb2", 128, 8192, BF16)
    wb3 = P.sb("wb3", 128, 8192, BF16)
    CCb = P.sb("CCb", 128, 512, BF16)
    m16b = P.sb("m16b", 128, 128, BF16)
    emb = P.sb("emb", 16, 1024, BF16)
    identbb = P.sb("identbb", 128, 128, BF16)
    PMB = P.sb("PMB", 128, 256, BF16)
    PB = P.sb("PB", 128, 256, BF16)
    ACC = P.sb("ACC", 128, 256)
    obC = P.sb("obC", 128, 8)
    ones_b = P.sb("ones_b", 128, 128, BF16)
    epscol = P.sb("epscol", 128, 1)
    QB = [P.sb(f"QB{i}", 128, NTOK, BF16) for i in range(8)]
    Bb = P.sb("Bb", 128, 8192, BF16)
    QK = P.sb("QK", 16, 16384, BF16)
    EXS = P.sb("EXS", 16, 2048)
    DIAG = P.sb("DIAG", 16, 2048, BF16)
    MXS = P.sb("MXS", 16, 512)
    TMPF = P.sb("TMPF", 128, 512)

    pb = [P.ps(f"pb{i}", 128, 512) for i in range(8)]

    ln_ld = []
    for kc in range(8):
        for b in range(8):
            ln_ld.append(P.dma("gpsimd", LNB[kc][:, 256 * b:256 * (b + 1)],
                               ln2sI[1024 * b + 128 * kc:1024 * b + 128 * (kc + 1), :]))
    for kc in range(8):
        P.dma("gpsimd", wb1[:, 1024 * kc:1024 * (kc + 1)], qwI[128 * kc:128 * (kc + 1), :])
        P.dma("gpsimd", wb2[:, 1024 * kc:1024 * (kc + 1)], qwTI[128 * kc:128 * (kc + 1), :])
        P.dma("gpsimd", wb3[:, 1024 * kc:1024 * (kc + 1)], vwI[128 * kc:128 * (kc + 1), :])
    P.dma("gpsimd", m16b[:, :], m16I[:, :])
    P.dma("gpsimd", emb[:, :], emI[:, :])
    P.dma("gpsimd", identbb[:, :], identI[:, :])
    for t in range(8):
        P.dma("sync", obC[:, t:t + 1], obI[128 * t:128 * (t + 1), :])
    P.op("gpsimd", lambda h: h.memset(ones_b[:, :], 1.0))
    c_eps = P.op("gpsimd", lambda h: h.memset(epscol[:, :], 1e-5))
    wall = P.tok("gpsimd")
    wsync = P.tok("sync")

    qev = None
    for nc4 in range(4):
        nsl = slice(512 * nc4, 512 * (nc4 + 1))
        def q_mm(h, nsl=nsl):
            r = None
            for m in range(8):
                for kc in range(8):
                    r = h.matmul(pb[m][:, 0:512],
                                 wb1[:, 1024 * kc + 128 * m:1024 * kc + 128 * (m + 1)],
                                 LNB[kc][:, nsl], start=(kc == 0), stop=(kc == 7),
                                 skip_group_check=True)
            return r
        t_q = P.op("tensor", q_mm, [wall, qev])
        for m in range(8):
            qev = P.op("vector", lambda h, m=m, nsl=nsl: h.tensor_copy(QB[m][:, nsl],
                                                                       pb[m][:, 0:512]), [t_q])

    def C_mm(h):
        r = None
        for hh in range(16):
            po = 64 * (hh % 2)
            co = 64 * (hh // 2)
            for kc in range(8):
                r = h.matmul(pb[0][po:po + 64, co:co + 64],
                             wb1[:, 1024 * kc + 64 * hh:1024 * kc + 64 * (hh + 1)],
                             wb3[:, 1024 * kc + 64 * hh:1024 * kc + 64 * (hh + 1)],
                             start=(kc == 0 and hh < 2), stop=(kc == 7),
                             skip_group_check=True)
        return r
    t_C = P.op("tensor", C_mm, [wall, qev])
    t_Ce = P.op("scalar", lambda h: h.activation(CCb[:, :], pb[0][:, 0:512], AF.Copy,
                                                 scale=0.125), [t_C])
    bev = None
    for hh in range(16):
        po = 64 * (hh % 2)
        co = 64 * (hh // 2)
        rt = hh // 2
        def Bmm1(h, po=po, co=co, rt=rt):
            r = None
            for mc in range(8):
                r = h.matmul(pb[1 + mc % 4][po:po + 64 if False else 0:128, 128 * (mc // 4):128 * (mc // 4) + 64]
                             if False else pb[1 + mc % 4][:, 128 * (mc // 4):128 * (mc // 4) + 64],
                             wb2[po:po + 64, 1024 * rt + 128 * mc:1024 * rt + 128 * (mc + 1)],
                             CCb[po:po + 64, co:co + 64],
                             start=True, stop=True, skip_group_check=True)
            return r
        t_B = P.op("tensor", Bmm1, [t_Ce, bev])
        def Bev(h, hh=hh):
            r = None
            for mc in range(8):
                r = h.copy(Bb[:, 1024 * mc + 64 * hh:1024 * mc + 64 * (hh + 1)],
                           pb[1 + mc % 4][:, 128 * (mc // 4):128 * (mc // 4) + 64])
            return r
        bev = P.op("scalar", Bev, [t_B])
    b_done = bev

    pm_war = None
    qk_war = None
    for t in range(8):
        for s_ in range(t, 8):
            mm_t = None
            for kc in range(8):
                t_pm = P.op("vector", lambda h, kc=kc, t=t, s_=s_: h.tensor_tensor(
                    PMB[:, :], QB[kc][:, 256 * t:256 * (t + 1)],
                    QB[kc][:, 256 * s_:256 * (s_ + 1)], OP.mult), [pm_war, qev])
                mm_t = P.op("tensor", lambda h, kc=kc: h.matmul(
                    pb[2][0:16, 0:256], m16b[:, 16 * kc:16 * (kc + 1)], PMB[:, :],
                    start=(kc == 0), stop=(kc == 7), skip_group_check=True),
                    [t_pm, qk_war, b_done])
                pm_war = mm_t
            t_qke = P.op("scalar", lambda h, t=t, s_=s_: h.copy(
                QK[:, 256 * (8 * t + s_):256 * (8 * t + s_ + 1)], pb[2][0:16, 0:256]), [mm_t])
            if s_ != t:
                t_qke = P.op("scalar", lambda h, t=t, s_=s_: h.copy(
                    QK[:, 256 * (8 * s_ + t):256 * (8 * s_ + t + 1)],
                    QK[:, 256 * (8 * t + s_):256 * (8 * t + s_ + 1)]), [t_qke])
            qk_war = t_qke
    qk_done = qk_war
    if debug:
        P._d1 = [P.dma("gpsimd", qD[:, :], QB[0][:, :], [qk_done]),
                 P.dma("gpsimd", bD[:, :], Bb[:, :], [qk_done]),
                 P.dma("gpsimd", qkD[:, :], QK[:, :], [qk_done])]

    dsave = []
    for s_ in range(8):
        dsave.append(P.op("vector", lambda h, s_=s_: h.tensor_copy(
            DIAG[:, 256 * s_:256 * (s_ + 1)], QK[:, 256 * (8 * s_ + s_):256 * (8 * s_ + s_ + 1)]),
            [qk_done]))
    diag_done = dsave[-1]
    pn_tok = {}
    for t in range(8):
        lg = []
        for s_ in range(8):
            t1 = P.op("vector", lambda h, t=t, s_=s_: h.scalar_tensor_tensor(
                EXS[:, 256 * s_:256 * (s_ + 1)], QK[:, 256 * (8 * t + s_):256 * (8 * t + s_ + 1)],
                2.0, DIAG[:, 256 * t:256 * (t + 1)], OP.mult, OP.subtract),
                [qk_done, diag_done])
            t2 = P.op("vector", lambda h, t=t, s_=s_: h.tensor_tensor(
                EXS[:, 256 * s_:256 * (s_ + 1)], EXS[:, 256 * s_:256 * (s_ + 1)],
                DIAG[:, 256 * s_:256 * (s_ + 1)], OP.subtract), [t1])
            lg.append(t2)
        t_m = P.op("vector", lambda h: h.tensor_tensor(MXS[:, 0:256], EXS[:, 0:256],
                                                       EXS[:, 256:512], OP.max), [lg[0], lg[1]])
        for s_ in range(2, 8):
            t_m = P.op("vector", lambda h, s_=s_: h.tensor_tensor(
                MXS[:, 0:256], MXS[:, 0:256], EXS[:, 256 * s_:256 * (s_ + 1)], OP.max),
                [t_m, lg[s_]])
        ex = []
        for s_ in range(8):
            t1 = P.op("vector", lambda h, s_=s_: h.tensor_tensor(
                EXS[:, 256 * s_:256 * (s_ + 1)], EXS[:, 256 * s_:256 * (s_ + 1)], MXS[:, 0:256],
                OP.subtract), [t_m])
            t2 = P.op("scalar", lambda h, s_=s_: h.activation(
                EXS[:, 256 * s_:256 * (s_ + 1)], EXS[:, 256 * s_:256 * (s_ + 1)], AF.Exp,
                scale=0.125), [t1])
            ex.append(t2)
        t_sm = P.op("vector", lambda h: h.tensor_tensor(MXS[:, 256:512], EXS[:, 0:256],
                                                        EXS[:, 256:512], OP.add), [ex[0], ex[1]])
        for s_ in range(2, 8):
            t_sm = P.op("vector", lambda h, s_=s_: h.tensor_tensor(
                MXS[:, 256:512], MXS[:, 256:512], EXS[:, 256 * s_:256 * (s_ + 1)], OP.add),
                [t_sm, ex[s_]])
        t_rc = P.op("vector", lambda h: h.reciprocal(MXS[:, 256:512], MXS[:, 256:512]), [t_sm])
        for s_ in range(8):
            pn_tok[(t, s_)] = P.op("vector", lambda h, t=t, s_=s_: h.tensor_tensor(
                QK[:, 256 * (8 * t + s_):256 * (8 * t + s_ + 1)], EXS[:, 256 * s_:256 * (s_ + 1)],
                MXS[:, 256:512], OP.mult), [t_rc])
    pn_done = pn_tok[(7, 7)]
    if debug:
        P._d1.append(P.dma("gpsimd", pnD[:, :], QK[:, :], [pn_done]))

    pb_war = None
    px_tok = []
    xt_ev = None
    for tile in range(8):
        for nc4 in range(4):
            def xmm(h, tile=tile, nc4=nc4):
                r = None
                for kc in range(8):
                    r = h.matmul(pb[4][:, 0:512],
                                 Bb[:, 1024 * kc + 128 * tile:1024 * kc + 128 * (tile + 1)],
                                 LNB[kc][:, 512 * nc4:512 * (nc4 + 1)], start=(kc == 0),
                                 stop=(kc == 7), skip_group_check=True)
                return r
            t_x = P.op("tensor", xmm, [b_done, xt_ev] + ([pn_done] if tile == 0 and nc4 == 0 else []))
            xt_ev = P.op("vector", lambda h, nc4=nc4: h.tensor_copy(
                XT[:, 512 * nc4:512 * (nc4 + 1)], pb[4][:, 0:512]), [t_x])
        ap_last = None
        for t in range(8):
            acc_t = None
            for s_ in range(8):
                t_bc = P.op("tensor", lambda h, t=t, s_=s_, tile=tile: h.matmul(
                    pb[5][:, 0:256], emb[:, 128 * tile:128 * (tile + 1)],
                    QK[:, 256 * (8 * t + s_):256 * (8 * t + s_ + 1)],
                    start=True, stop=True, skip_group_check=True),
                    [pn_tok[(t, s_)], pb_war])
                t_pb = P.op("scalar", lambda h: h.copy(PB[:, :], pb[5][:, 0:256]), [t_bc])
                pb_war = t_pb
                if s_ == 0:
                    acc_t = P.op("vector", lambda h, s_=s_: h.tensor_tensor(
                        ACC[:, :], PB[:, :], XT[:, 256 * s_:256 * (s_ + 1)], OP.mult),
                        [t_pb, xt_ev])
                else:
                    t_p = P.op("vector", lambda h, s_=s_: h.tensor_tensor(
                        TMPF[:, 0:256], PB[:, :], XT[:, 256 * s_:256 * (s_ + 1)], OP.mult),
                        [t_pb])
                    acc_t = P.op("vector", lambda h: h.tensor_tensor(
                        ACC[:, :], ACC[:, :], TMPF[:, 0:256], OP.add), [t_p, acc_t])
            ap_last = P.op("vector", lambda h, t=t, tile=tile: h.tensor_copy(
                QB[tile][:, 256 * t:256 * (t + 1)], ACC[:, :]), [acc_t])
        if debug and tile == 0:
            P._d1.append(P.dma("gpsimd", xtD[:, :], XT[:, :], [ap_last]))
            P._d1.append(P.dma("gpsimd", pxD[:, :], QB[0][:, :], [ap_last]))
        px_tok.append(ap_last)
    px_done = px_tok[-1]


    ow_ld = [P.dma("gpsimd", wb1[:, 1024 * kc:1024 * (kc + 1)], owTI[128 * kc:128 * (kc + 1), :],
                   [P.tok("tensor"), qk_done]) for kc in range(8)]
    ow_all = ow_ld[-1]
    at_ev = None
    for nc4 in [1, 2, 3, 0]:
        nsl = slice(512 * nc4, 512 * (nc4 + 1))
        def o_mm(h, nsl=nsl):
            r = None
            for m in range(8):
                for kc in range(8):
                    r = h.matmul(pb[m][:, 0:512],
                                 wb1[:, 1024 * kc + 128 * m:1024 * kc + 128 * (m + 1)],
                                 QB[kc][:, nsl], start=(kc == 0), stop=(kc == 7),
                                 skip_group_check=True)
            return r
        t_o = P.op("tensor", o_mm, [ow_all, px_done, at_ev, wsync])
        for m in range(8):
            at_ev = P.op("scalar", lambda h, m=m, nsl=nsl: h.activation(
                LNB[m][:, nsl], pb[m][:, 0:512], AF.Identity, bias=obC[:, m:m + 1]), [t_o])
    attn_done = at_ev
    if debug:
        P._d1.append(P.dma("gpsimd", a3D[:, :], LNB[0][:, :], [attn_done]))
        attn_done = P._d1[-1]


    attnO = P.dram("attnO", [DM, NTOK], BF16, kind="ExternalOutput")
    fin = [P.dma("gpsimd", attnO[128 * kc:128 * (kc + 1), :], LNB[kc][:, :], [attn_done])
           for kc in range(8)]
    if debug:
        fin = fin + P._d1
    return P.finish(final_waits=fin)

def host_maps_K2a(inputs, ln2_full):
    qw = inputs["L1_attn_q_weight"].reshape(DM, DM)
    vw = inputs["L1_attn_v_weight"].reshape(DM, DM)
    m16 = np.zeros((128, 128), np.float32)
    for tile in range(8):
        for k in range(128):
            m16[k, 16 * tile + 2 * tile + k // 64] = 1.0
    em = np.zeros((16, 1024), np.float32)
    for tile in range(8):
        for c in range(128):
            em[2 * tile + c // 64, 128 * tile + c] = 1.0
    base = {
        "qwI": np.ascontiguousarray(qw), "qwTI": np.ascontiguousarray(qw.T),
        "vwI": np.ascontiguousarray(vw),
        "owTI": np.ascontiguousarray(inputs["L1_attn_out_w"].T),
        "obI": np.ascontiguousarray(inputs["L1_attn_out_b"].reshape(-1, 1)),
        "m16I": m16, "emI": em, "identI": np.eye(128, dtype=np.float32),
    }
    maps = []
    for k in range(NCORE):
        m = dict(base)
        sl = slice(256 * k, 256 * (k + 1))
        m["ln2sI"] = np.ascontiguousarray(np.concatenate([ln2_full[b][:, sl] for b in range(8)], 0))
        maps.append(m)
    return maps


def host_maps_K2b(inputs, x2_full, attn_parts):
    base = {
        "sw1TI": np.ascontiguousarray(inputs["L1_ffn_sw1"].T),
        "sw2TI": np.ascontiguousarray(inputs["L1_ffn_sw2"].T),
        "sw3TI": np.ascontiguousarray(inputs["L1_ffn_sw3"].T),
        "sb1I": np.ascontiguousarray(inputs["L1_ffn_sb1"].reshape(-1, 1)),
        "sb2I": np.ascontiguousarray(inputs["L1_ffn_sb2"].reshape(-1, 1)),
        "sb3I": np.ascontiguousarray(inputs["L1_ffn_sb3"].reshape(-1, 1)),
        "slngI": np.ascontiguousarray(inputs["L1_ffn_s_ln_g"].reshape(-1, 1)),
        "slnbI": np.ascontiguousarray(inputs["L1_ffn_s_ln_b"].reshape(-1, 1)),
        "identI": np.eye(128, dtype=np.float32),
    }
    maps = []
    for k in range(NCORE):
        m = dict(base)
        sl = slice(256 * k, 256 * (k + 1))
        m["x2sI"] = np.ascontiguousarray(np.concatenate([x2_full[b][:, sl] for b in range(8)], 0))
        m["attnI"] = np.ascontiguousarray(attn_parts[k])
        maps.append(m)
    return maps


def build_launch_K2b(serial=False):
    P = Prog("K2b", serial=serial)
    NTOK = 2048
    attnI = P.dram("attnI", [DM, NTOK], BF16)
    x2sI = P.dram("x2sI", [8192, 256])
    sw1TI = P.dram("sw1TI", [DM, 512])
    sw2TI = P.dram("sw2TI", [DM, 512])
    sw3TI = P.dram("sw3TI", [512, DM])
    sb1I = P.dram("sb1I", [512, 1])
    sb2I = P.dram("sb2I", [512, 1])
    sb3I = P.dram("sb3I", [DM, 1])
    slngI = P.dram("slngI", [512, 1])
    slnbI = P.dram("slnbI", [512, 1])
    identI = P.dram("identI", [128, 128])
    x3O = P.dram("x3O", [NTOK, DM], kind="ExternalOutput")
    ln3O = P.dram("ln3O", [NTOK, DM], BF16, kind="ExternalOutput")
    shO = P.dram("shO", [NTOK, DM], BF16, kind="ExternalOutput")

    LNB = [P.sb(f"LNB{i}", 128, NTOK, BF16) for i in range(8)]
    X3 = [P.sb(f"X3{i}", 128, NTOK) for i in range(8)]
    sw1b = P.sb("sw1b", 128, 4096, BF16)
    sw2b = P.sb("sw2b", 128, 4096, BF16)
    sw3b = P.sb("sw3b", 128, 4096, BF16)
    sb1C = P.sb("sb1C", 128, 4)
    sb2C = P.sb("sb2C", 128, 4)
    sb3C = P.sb("sb3C", 128, 8)
    slngC = P.sb("slngC", 128, 4)
    slnbC = P.sb("slnbC", 128, 4)
    identb = P.sb("identb", 128, 128)
    identbb = P.sb("identbb", 128, 128, BF16)
    ones_b = P.sb("ones_b", 128, 128, BF16)
    epscol = P.sb("epscol", 128, 1)
    mean_b = P.sb("mean_b", 128, NTOK)
    rstd_b = P.sb("rstd_b", 128, NTOK)
    SQ = [P.sb(f"SQ{i}", 128, NTOK, BF16) for i in range(2)]
    TMPF = P.sb("TMPF", 128, NTOK)
    TMPG = TMPF
    x2b = P.sb("x2b", 128, NTOK)
    X1S = [P.sb(f"X1S{i}", 128, NTOK, BF16) for i in range(4)]
    SHB = [P.sb(f"SHB{i}", 128, NTOK, BF16) for i in range(4)]
    TNB = [P.sb(f"TNB{i}", 128, 1024) for i in range(2)]
    TNBb = [P.sb(f"TNBb{i}", 128, 1024, BF16) for i in range(2)]
    T256b = P.sb("T256b", 128, 256, BF16)

    pb = [P.ps(f"pb{i}", 128, 512) for i in range(8)]

    ld_at = [P.dma("sync", LNB[kc][:, :], attnI[128 * kc:128 * (kc + 1), :]) for kc in range(8)]
    for kc in range(8):
        P.dma("gpsimd", sw1b[:, 512 * kc:512 * (kc + 1)], sw1TI[128 * kc:128 * (kc + 1), :])
        P.dma("gpsimd", sw2b[:, 512 * kc:512 * (kc + 1)], sw2TI[128 * kc:128 * (kc + 1), :])
    for kc in range(4):
        P.dma("gpsimd", sw3b[:, 1024 * kc:1024 * (kc + 1)], sw3TI[128 * kc:128 * (kc + 1), :])
    P.dma("gpsimd", identbb[:, :], identI[:, :])
    P.dma("sync", identb[:, :], identI[:, :])
    for t in range(8):
        P.dma("sync", sb3C[:, t:t + 1], sb3I[128 * t:128 * (t + 1), :])
    for t in range(4):
        P.dma("sync", sb1C[:, t:t + 1], sb1I[128 * t:128 * (t + 1), :])
        P.dma("sync", sb2C[:, t:t + 1], sb2I[128 * t:128 * (t + 1), :])
        P.dma("sync", slngC[:, t:t + 1], slngI[128 * t:128 * (t + 1), :])
        P.dma("sync", slnbC[:, t:t + 1], slnbI[128 * t:128 * (t + 1), :])
    P.op("gpsimd", lambda h: h.memset(ones_b[:, :], 1.0))
    c_eps = P.op("gpsimd", lambda h: h.memset(epscol[:, :], 1e-5))
    wall = P.tok("gpsimd")
    wsync = P.tok("sync")

    stat = emit_ln_stats(P, LNB, lambda h, kc, dst: h.activation(dst[:, :], LNB[kc][:, :],
                                                                 AF.Square),
                         mean_b, rstd_b, pb[0:4], pb[4:8], ones_b, 8, NTOK, SQ, TMPF, epscol,
                         waits_per_kc=[ld_at[kc] for kc in range(8)],
                         extra_first=[c_eps, wall, wsync])
    x3_tok = []
    prevx2 = None
    for kc in range(8):
        lds = [P.dma("sync", x2b[:, 256 * b:256 * (b + 1)],
                     x2sI[1024 * b + 128 * kc:1024 * b + 128 * (kc + 1), :],
                     ([prevx2] if b == 0 else [])) for b in range(8)]
        t1 = P.op("vector", lambda h, kc=kc: h.tensor_tensor(TMPG[:, :], LNB[kc][:, :],
                                                             mean_b[:, :], OP.subtract),
                  [stat[-1]])
        t2 = P.op("vector", lambda h: h.tensor_tensor(TMPG[:, :], TMPG[:, :], rstd_b[:, :],
                                                      OP.mult), [t1])
        t3 = P.op("vector", lambda h, kc=kc: h.tensor_tensor(X3[kc][:, :], x2b[:, :], TMPG[:, :],
                                                             OP.add), [t2, lds[-1]])
        prevx2 = t3
        t5 = P.op("vector", lambda h, kc=kc: h.tensor_copy(LNB[kc][:, :], X3[kc][:, :]), [t3])
        x3_tok.append((t3, t5))
    stat2 = emit_ln_stats(P, LNB, lambda h, kc, dst: h.activation(dst[:, :], X3[kc][:, :],
                                                                  AF.Square),
                          mean_b, rstd_b, pb[0:4], pb[4:8], ones_b, 8, NTOK, SQ, TMPF, epscol,
                          waits_per_kc=[x3_tok[kc][1] for kc in range(8)],
                          extra_first=[P.tok("vector"), P.tok("scalar")])
    ln3_tok = []
    for kc in range(8):
        t1 = P.op("vector", lambda h, kc=kc: h.tensor_tensor(TMPG[:, :], X3[kc][:, :],
                                                             mean_b[:, :], OP.subtract),
                  [stat2[-1]])
        t2 = P.op("vector", lambda h, kc=kc: h.tensor_tensor(LNB[kc][:, :], TMPG[:, :],
                                                             rstd_b[:, :], OP.mult), [t1])
        ln3_tok.append(t2)
    ln3_done = ln3_tok[-1]

    tn_toks = []
    tr_war = None
    for tt in range(16):
        tsl = slice(128 * tt, 128 * (tt + 1))
        buf = TNB[tt % 2]
        bufb = TNBb[tt % 2]
        last = None
        war2 = [tn_toks[2 * (tt - 2)], tn_toks[2 * (tt - 2) + 1]] if tt >= 2 else []
        if tt == 0:
            war2 = war2 + [stat2[-1], ln3_done]
        for kc in range(8):
            t_t = P.op("tensor", lambda h, kc=kc, tsl=tsl: h.transpose(
                pb[6][:, 0:128], X3[kc][:, tsl], identb[:, :]),
                [x3_tok[kc][0], tr_war, last] + (war2 if kc == 0 else []))
            last = P.op("vector", lambda h, kc=kc, buf=buf: h.tensor_copy(
                buf[:, 128 * kc:128 * (kc + 1)], pb[6][:, 0:128]), [t_t])
            t_tb = P.op("tensor", lambda h, kc=kc, tsl=tsl: h.transpose(
                pb[7][:, 0:64].bitcast(BF16), LNB[kc][:, tsl], identbb[:, :]),
                [ln3_tok[kc], last])
            last = P.op("vector", lambda h, kc=kc, bufb=bufb: h.tensor_copy(
                bufb[:, 128 * kc:128 * (kc + 1)], pb[7][:, 0:64].bitcast(BF16)), [t_tb])
        tr_war = last
        tn_toks.append(P.dma("sync", x3O[tsl, :], buf[:, :], [last]))
        tn_toks.append(P.dma("gpsimd", ln3O[tsl, :], bufb[:, :], [last]))

    shp_tok = []
    x1ev = None
    for nc4 in range(4):
        nsl = slice(512 * nc4, 512 * (nc4 + 1))
        def s1_mm(h, nsl=nsl):
            r = None
            for m in range(4):
                for kc in range(8):
                    r = h.matmul(pb[m][:, 0:512],
                                 sw1b[:, 512 * kc + 128 * m:512 * kc + 128 * (m + 1)],
                                 LNB[kc][:, nsl], start=(kc == 0), stop=(kc == 7),
                                 skip_group_check=True)
            return r
        t_s1 = P.op("tensor", s1_mm, [wall, ln3_done, x1ev, tn_toks[-1], tn_toks[-2]])
        def s2_mm(h, nsl=nsl):
            r = None
            for m in range(4):
                for kc in range(8):
                    r = h.matmul(pb[4 + m][:, 0:512],
                                 sw2b[:, 512 * kc + 128 * m:512 * kc + 128 * (m + 1)],
                                 LNB[kc][:, nsl], start=(kc == 0), stop=(kc == 7),
                                 skip_group_check=True)
            return r
        t_s2 = P.op("tensor", s2_mm, [wall, ln3_done, x1ev])
        for m in range(4):
            t_si = P.op("scalar", lambda h, m=m, nsl=nsl: h.activation(
                X1S[m][:, nsl], pb[m][:, 0:512], AF.Silu, bias=sb1C[:, m:m + 1]), [t_s1])
            t_b2 = P.op("vector", lambda h, m=m, nsl=nsl: h.tensor_scalar(
                TMPF[:, 0:512], pb[4 + m][:, 0:512], sb2C[:, m:m + 1], None, OP.add),
                [t_s2, t_si])
            x1ev = P.op("vector", lambda h, m=m, nsl=nsl: h.tensor_tensor(
                SHB[m][:, nsl], X1S[m][:, nsl], TMPF[:, 0:512], OP.mult), [t_b2])
        shp_tok.append(x1ev)
    shp_done = x1ev
    statS = emit_ln_stats(P, SHB, lambda h, kc, dst: h.activation(dst[:, :], SHB[kc][:, :],
                                                                  AF.Square),
                          mean_b, rstd_b, pb[0:4], pb[4:8], ones_b, 4, NTOK, SQ, TMPF, epscol,
                          waits_per_kc=[shp_done] * 4,
                          extra_first=[P.tok("vector"), P.tok("scalar")])
    shn_tok = []
    for kc in range(4):
        t1 = P.op("vector", lambda h, kc=kc: h.tensor_tensor(TMPG[:, :], SHB[kc][:, :],
                                                             mean_b[:, :], OP.subtract),
                  [statS[-1]])
        t2 = P.op("vector", lambda h, kc=kc: h.tensor_tensor(TMPG[:, :], TMPG[:, :],
                                                             rstd_b[:, :], OP.mult), [t1])
        t3 = P.op("vector", lambda h, kc=kc: h.tensor_scalar(
            SHB[kc][:, :], TMPG[:, :], slngC[:, kc:kc + 1], slnbC[:, kc:kc + 1],
            OP.mult, OP.add), [t2])
        shn_tok.append(t3)
    shn_done = shn_tok[-1]

    sh_fin = []
    ev_war = None
    for tt in range(16):
        tsl = slice(128 * tt, 128 * (tt + 1))
        bufb = TNBb[tt % 2]
        last = None
        for m in range(8):
            def sh_mm(h, m=m, tsl=tsl):
                r = None
                for kc in range(4):
                    r = h.matmul(pb[2][:, 0:128],
                                 sw3b[:, 1024 * kc + 128 * m:1024 * kc + 128 * (m + 1)],
                                 SHB[kc][:, tsl], start=(kc == 0), stop=(kc == 3),
                                 skip_group_check=True)
                return r
            t_m = P.op("tensor", sh_mm, [shn_done, ev_war, last,
                                         (sh_fin[-2] if len(sh_fin) >= 2 else
                                          (tn_toks[-1] if tn_toks else None))])
            t_e = P.op("vector", lambda h, m=m: h.tensor_scalar(
                T256b[:, 0:128], pb[2][:, 0:128], sb3C[:, m:m + 1], None, OP.add), [t_m])
            t_t = P.op("tensor", lambda h: h.transpose(pb[3][:, 0:64].bitcast(BF16),
                                                       T256b[:, 0:128], identbb[:, :]), [t_e])
            last = P.op("vector", lambda h, m=m, bufb=bufb: h.tensor_copy(
                bufb[:, 128 * m:128 * (m + 1)], pb[3][:, 0:64].bitcast(BF16)), [t_t])
            ev_war = last
        sh_fin.append(P.dma("gpsimd", shO[tsl, :], bufb[:, :], [last]))

    return P.finish(final_waits=tn_toks + sh_fin)





def build_launch_K3(serial=False, stop_after=None):
    P = Prog("K3", serial=serial)
    ln3F = P.dram("ln3F", [NB * NT, DM], BF16)
    shF = P.dram("shF", [NB * NT, DM], BF16)
    idxbI = P.dram("idxbI", [256, 1], I32)
    idxtI = P.dram("idxtI", [256, 1], I32)
    gwI = P.dram("gwI", [DM, 9])
    gb9I = P.dram("gb9I", [1, 9])
    expbI = P.dram("expbI", [1, 8])
    ew1TI = P.dram("ew1TI", [8 * DM, 512])
    ew2TI = P.dram("ew2TI", [8 * DM, 512])
    ew3TI = P.dram("ew3TI", [8 * 512, DM])
    eb1I = P.dram("eb1I", [8 * 512, 1])
    eb2I = P.dram("eb2I", [8 * 512, 1])
    eb3I = P.dram("eb3I", [8 * DM, 1])
    selgI = P.dram("selgI", [512, 1])
    selbI = P.dram("selbI", [512, 1])
    identI = P.dram("identI", [128, 128])
    moeO = P.dram("moeO", [256, DM], kind="ExternalOutput")

    XM = [P.sb(f"XM{g}", 128, DM, BF16) for g in range(2)]
    SHM = [P.sb(f"SHM{g}", 128, DM, BF16) for g in range(2)]
    XMB = P.sb("XMB", 128, 2048, BF16)
    idxc = P.sb("idxc", 128, 8, U32)
    gwb = P.sb("gwb", 128, 72, BF16)
    gb9b = P.sb("gb9b", 128, 16)
    expbb = P.sb("expbb", 128, 8)
    SC = [P.sb(f"SC{g}", 128, 16) for g in range(2)]
    PK = [P.sb(f"PK{g}", 128, 8) for g in range(2)]
    WSH = [P.sb(f"WSH{g}", 128, 2) for g in range(2)]
    SMT = P.sb("SMT", 128, 16)
    ew1b = [P.sb(f"ew1b{i}", 128, 4096, BF16) for i in range(2)]
    ew2b = [P.sb(f"ew2b{i}", 128, 4096, BF16) for i in range(2)]
    ew3b = [P.sb(f"ew3b{i}", 128, 4096, BF16) for i in range(2)]
    eb1C = [P.sb(f"eb1C{i}", 128, 4) for i in range(2)]
    eb2C = [P.sb(f"eb2C{i}", 128, 4) for i in range(2)]
    eb3C = [P.sb(f"eb3C{i}", 128, 8) for i in range(2)]
    selgC = P.sb("selgC", 128, 4)
    selbC = P.sb("selbC", 128, 4)
    S1 = P.sb("S1", 128, 1024, BF16)
    ZB = P.sb("ZB", 128, 1024, BF16)
    OETN = P.sb("OETN", 128, 2048, BF16)
    ACCT = [P.sb(f"ACCT{g}", 128, DM) for g in range(2)]
    TMP = P.sb("TMP", 128, DM)
    mean_b = P.sb("mean_b", 128, 256)
    rstd_b = P.sb("rstd_b", 128, 256)
    SQ = [P.sb(f"SQk{i}", 128, 256, BF16) for i in range(2)]
    TMS = P.sb("TMS", 128, 256)
    identbb = P.sb("identbb", 128, 128, BF16)
    ones_b = P.sb("ones_b", 128, 128, BF16)
    ones1f = P.sb("ones1f", 1, 128)
    onecol = P.sb("onecol", 128, 1)
    epscol = P.sb("epscol", 128, 1)

    pb = [P.ps(f"pb{i}", 128, 512) for i in range(8)]

    P.dma("gpsimd", idxc[:, 0:1], idxbI[0:128, :])
    P.dma("gpsimd", idxc[:, 1:2], idxbI[128:256, :])
    P.dma("gpsimd", idxc[:, 2:3], idxtI[0:128, :])
    P.dma("gpsimd", idxc[:, 3:4], idxtI[128:256, :])
    for kc in range(8):
        P.dma("gpsimd", gwb[:, 9 * kc:9 * (kc + 1)], gwI[128 * kc:128 * (kc + 1), :])
    P.dma("sync", gb9b[0:1, 0:9], gb9I[:, :])
    P.dma("sync", expbb[0:1, 0:8], expbI[:, :])
    P.dma("gpsimd", identbb[:, :], identI[:, :])
    for t in range(4):
        P.dma("sync", selgC[:, t:t + 1], selgI[128 * t:128 * (t + 1), :])
        P.dma("sync", selbC[:, t:t + 1], selbI[128 * t:128 * (t + 1), :])
    P.op("gpsimd", lambda h: h.memset(ones_b[:, :], 1.0))
    P.op("gpsimd", lambda h: h.memset(ones1f[:, :], 1.0))
    P.op("gpsimd", lambda h: h.memset(onecol[:, :], 1.0))
    c_eps = P.op("gpsimd", lambda h: h.memset(epscol[:, :], 1e-5))
    wall = P.tok("gpsimd")
    wsync = P.tok("sync")

    def bc_mm(h):
        h.matmul(pb[0][:, 0:9], ones1f[:, :], gb9b[0:1, 0:9], start=True, stop=True,
                 skip_group_check=True)
        return h.matmul(pb[0][:, 16:24], ones1f[:, :], expbb[0:1, 0:8], start=False, stop=True,
                        skip_group_check=True)
    t_bc = P.op("tensor", bc_mm, [wall, wsync])
    t_gb = P.op("vector", lambda h: h.tensor_copy(gb9b[:, 0:9], pb[0][:, 0:9]), [t_bc])
    t_eb = P.op("vector", lambda h: h.tensor_copy(expbb[:, :], pb[0][:, 16:24]), [t_bc])

    t_i1 = P.op("gpsimd", lambda h: h.tensor_scalar(idxc[:, 0:2], idxc[:, 0:2], 7, None, OP.min),
                [wall])
    t_i2 = P.op("gpsimd", lambda h: h.tensor_scalar(idxc[:, 2:4], idxc[:, 2:4], 2047, None,
                                                    OP.min), [t_i1])
    t_i3 = P.op("gpsimd", lambda h: h.tensor_scalar(idxc[:, 4:6], idxc[:, 0:2], 2048, None,
                                                    OP.mult), [t_i2])
    t_if = P.op("gpsimd", lambda h: h.tensor_tensor(idxc[:, 4:6], idxc[:, 4:6], idxc[:, 2:4],
                                                    OP.add), [t_i3])

    g_tok = []
    for g in range(2):
        t1 = P.op("gpsimd", lambda h, g=g: h.indirect_dma_start(
            XM[g][:, :], None, ln3F[:, :], IndirectOffsetOnAxis(ap=idxc[:, 4 + g:5 + g], axis=0),
            oob_is_err=False), [t_if], step=16)
        t2 = P.op("gpsimd", lambda h, g=g: h.indirect_dma_start(
            SHM[g][:, :], None, shF[:, :], IndirectOffsetOnAxis(ap=idxc[:, 4 + g:5 + g], axis=0),
            oob_is_err=False), [t_if], step=16)
        g_tok += [t1, t2]
    g_done = g_tok[-1]

    tr_tok = []
    last_tr = None
    for g in range(2):
        for kc in range(8):
            t_t = P.op("tensor", lambda h, g=g, kc=kc: h.transpose(
                pb[1][:, 0:64].bitcast(BF16), XM[g][:, 128 * kc:128 * (kc + 1)], identbb[:, :]),
                [g_done, last_tr, wall])
            last_tr = P.op("vector", lambda h, g=g, kc=kc: h.tensor_copy(
                XMB[:, 256 * kc + 128 * g:256 * kc + 128 * (g + 1)],
                pb[1][:, 0:64].bitcast(BF16)), [t_t])
            tr_tok.append(last_tr)
    xmb_done = tr_tok[-1]
    if stop_after == "gather":
        dbg = P.dram("dbgO", [256, DM], BF16, kind="ExternalOutput")
        f1 = P.dma("gpsimd", dbg[0:128, :], XM[0][:, :], [g_done])
        f2 = P.dma("gpsimd", dbg[128:256, :], XM[1][:, :], [g_done, xmb_done])
        return P.finish(final_waits=[f1, f2])

    gs_tok = []
    for g in range(2):
        def gs_mm(h, g=g):
            r = None
            for kc in range(8):
                r = h.matmul(pb[2][:, 0:9], XMB[:, 256 * kc + 128 * g:256 * kc + 128 * (g + 1)],
                             gwb[:, 9 * kc:9 * (kc + 1)], start=(kc == 0), stop=(kc == 7),
                             skip_group_check=True)
            return r
        t_g = P.op("tensor", gs_mm, [xmb_done, (gs_tok[-1] if gs_tok else None)])
        t_a = P.op("vector", lambda h, g=g: h.tensor_tensor(SC[g][:, 0:9], pb[2][:, 0:9],
                                                            gb9b[:, 0:9], OP.add), [t_g, t_gb])
        t_s = P.op("scalar", lambda h, g=g: h.activation(SC[g][:, 0:9], SC[g][:, 0:9],
                                                         AF.Sigmoid), [t_a])
        t_v = P.op("vector", lambda h, g=g: h.tensor_tensor(SMT[:, 0:8], SC[g][:, 1:9],
                                                            expbb[:, :], OP.add), [t_s, t_eb])
        t_mx = P.op("vector", lambda h: h.reduce_max(SMT[:, 8:9], SMT[:, 0:8], axis=AX.X), [t_v])
        t_in = P.op("vector", lambda h: h.tensor_scalar(SMT[:, 0:8], SMT[:, 0:8], SMT[:, 8:9],
                                                        None, OP.is_ge), [t_mx])
        t_nt = P.op("vector", lambda h: h.tensor_copy(SMT[:, 9:10], onecol[:, :]), [t_in, wall])
        prev = t_nt
        for e in range(8):
            p1 = P.op("vector", lambda h, g=g, e=e: h.tensor_tensor(
                PK[g][:, e:e + 1], SMT[:, e:e + 1], SMT[:, 9:10], OP.mult), [prev])
            p2 = P.op("vector", lambda h, e=e: h.tensor_scalar(
                SMT[:, 10:11], SMT[:, e:e + 1], -1.0, 1.0, OP.mult, OP.add), [p1])
            prev = P.op("vector", lambda h: h.tensor_tensor(SMT[:, 9:10], SMT[:, 9:10],
                                                            SMT[:, 10:11], OP.mult), [p2])
        t_ps = P.op("vector", lambda h, g=g: h.tensor_tensor(SMT[:, 0:8], PK[g][:, :],
                                                             SC[g][:, 1:9], OP.mult), [prev])
        t_t1 = P.op("vector", lambda h: h.tensor_reduce(SMT[:, 8:9], SMT[:, 0:8], axis=AX.X,
                                                        op=OP.add), [t_ps])
        t_m2 = P.op("vector", lambda h, g=g: h.tensor_tensor(SMT[:, 10:11], SC[g][:, 0:1],
                                                             SMT[:, 8:9], OP.max), [t_t1])
        t_n1 = P.op("vector", lambda h, g=g: h.tensor_tensor(SMT[:, 11:12], SC[g][:, 0:1],
                                                             SMT[:, 10:11], OP.subtract), [t_m2])
        t_n2 = P.op("vector", lambda h: h.tensor_tensor(SMT[:, 12:13], SMT[:, 8:9],
                                                        SMT[:, 10:11], OP.subtract), [t_m2])
        t_e1 = P.op("scalar", lambda h: h.activation(SMT[:, 11:12], SMT[:, 11:12], AF.Exp), [t_n1])
        t_e2 = P.op("scalar", lambda h: h.activation(SMT[:, 12:13], SMT[:, 12:13], AF.Exp), [t_n2])
        t_sm = P.op("vector", lambda h: h.tensor_tensor(SMT[:, 13:14], SMT[:, 11:12],
                                                        SMT[:, 12:13], OP.add), [t_e1, t_e2])
        t_rc = P.op("vector", lambda h: h.reciprocal(SMT[:, 13:14], SMT[:, 13:14]), [t_sm])
        t_w1 = P.op("vector", lambda h, g=g: h.tensor_tensor(WSH[g][:, 0:1], SMT[:, 11:12],
                                                             SMT[:, 13:14], OP.mult), [t_rc])
        t_w2 = P.op("vector", lambda h, g=g: h.tensor_tensor(WSH[g][:, 1:2], SMT[:, 12:13],
                                                             SMT[:, 13:14], OP.mult), [t_w1])
        gs_tok.append(t_w2)
    gate_done = gs_tok[-1]
    if stop_after == "gating":
        dbg = P.dram("dbgO", [256, 16], kind="ExternalOutput")
        f = []
        for g in range(2):
            f.append(P.dma("gpsimd", dbg[128 * g:128 * (g + 1), 0:8], PK[g][:, :], [gate_done]))
            f.append(P.dma("gpsimd", dbg[128 * g:128 * (g + 1), 8:10], WSH[g][:, :], [gate_done]))
        return P.finish(final_waits=f)

    def ld_expert(e, buf, waits):
        toks = []
        for kc in range(8):
            toks.append(P.dma("gpsimd", ew1b[buf][:, 512 * kc:512 * (kc + 1)],
                              ew1TI[1024 * e + 128 * kc:1024 * e + 128 * (kc + 1), :], waits))
            toks.append(P.dma("gpsimd", ew2b[buf][:, 512 * kc:512 * (kc + 1)],
                              ew2TI[1024 * e + 128 * kc:1024 * e + 128 * (kc + 1), :], waits))
        for kc in range(4):
            toks.append(P.dma("gpsimd", ew3b[buf][:, 1024 * kc:1024 * (kc + 1)],
                              ew3TI[512 * e + 128 * kc:512 * e + 128 * (kc + 1), :], waits))
        for t in range(4):
            toks.append(P.dma("sync", eb1C[buf][:, t:t + 1],
                              eb1I[512 * e + 128 * t:512 * e + 128 * (t + 1), :], waits))
            toks.append(P.dma("sync", eb2C[buf][:, t:t + 1],
                              eb2I[512 * e + 128 * t:512 * e + 128 * (t + 1), :], waits))
        for t in range(8):
            toks.append(P.dma("sync", eb3C[buf][:, t:t + 1],
                              eb3I[1024 * e + 128 * t:1024 * e + 128 * (t + 1), :], waits))
        return toks

    wtoks = {0: ld_expert(0, 0, [])}
    acc_tok = [None, None]
    e_used = {}
    for e in range(8):
        buf = e % 2
        wt = wtoks[e]
        if e + 1 < 8:
            wtoks[e + 1] = ld_expert(e + 1, 1 - buf, [e_used.get(e - 1)])
        def h1_mm(h, buf=buf):
            r = None
            for mi in range(4):
                for kc in range(8):
                    r = h.matmul(pb[mi // 2][:, 256 * (mi % 2):256 * (mi % 2) + 256],
                                 ew1b[buf][:, 512 * kc + 128 * mi:512 * kc + 128 * (mi + 1)],
                                 XMB[:, 256 * kc:256 * (kc + 1)], start=(kc == 0 and mi % 2 == 0),
                                 stop=(kc == 7), skip_group_check=True)
            return r
        t_h1 = P.op("tensor", h1_mm, [xmb_done, wt[-1], acc_tok[0]] +
                    ([gate_done] if e == 0 else []))
        def h2_mm(h, buf=buf):
            r = None
            for mi in range(4):
                for kc in range(8):
                    r = h.matmul(pb[2 + mi // 2][:, 256 * (mi % 2):256 * (mi % 2) + 256],
                                 ew2b[buf][:, 512 * kc + 128 * mi:512 * kc + 128 * (mi + 1)],
                                 XMB[:, 256 * kc:256 * (kc + 1)], start=(kc == 0 and mi % 2 == 0),
                                 stop=(kc == 7), skip_group_check=True)
            return r
        t_h2 = P.op("tensor", h2_mm, [acc_tok[0]] + ([gate_done] if e == 0 else []))
        zp_tok = []
        for mi in range(4):
            bank = pb[mi // 2]
            off = 256 * (mi % 2)
            t_si = P.op("scalar", lambda h, mi=mi, bank=bank, off=off, buf=buf: h.activation(
                S1[:, 256 * mi:256 * (mi + 1)], bank[:, off:off + 256], AF.Silu,
                bias=eb1C[buf][:, mi:mi + 1]), [t_h1, t_h2])
            t_b2 = P.op("vector", lambda h, mi=mi, buf=buf: h.tensor_scalar(
                TMS[:, :], pb[2 + mi // 2][:, 256 * (mi % 2):256 * (mi % 2) + 256],
                eb2C[buf][:, mi:mi + 1], None, OP.add), [t_si])
            t_z = P.op("vector", lambda h, mi=mi: h.tensor_tensor(
                ZB[:, 256 * mi:256 * (mi + 1)], S1[:, 256 * mi:256 * (mi + 1)], TMS[:, :],
                OP.mult), [t_b2])
            zp_tok.append(t_z)
        zp_done = zp_tok[-1]
        if stop_after == "h" and e == 0:
            dbg = P.dram("dbgO", [128, 1024], BF16, kind="ExternalOutput")
            return P.finish(final_waits=[P.dma("gpsimd", dbg[:, :], ZB[:, :], [zp_done])])
        ZT = [ZB[:, 256 * mi:256 * (mi + 1)] for mi in range(4)]
        statz = emit_ln_stats(P, ZT, lambda h, kc, dst: h.activation(
            dst[:, 0:256], ZB[:, 256 * kc:256 * (kc + 1)], AF.Square),
            mean_b, rstd_b, [pb[4]], [pb[5]], ones_b, 4, 256, SQ, TMS, epscol,
            waits_per_kc=[zp_done] * 4, extra_first=[c_eps])
        zn_tok = []
        for mi in range(4):
            t1 = P.op("vector", lambda h, mi=mi: h.tensor_tensor(
                TMS[:, :], ZB[:, 256 * mi:256 * (mi + 1)], mean_b[:, 0:256], OP.subtract),
                [statz[-1]])
            t2 = P.op("vector", lambda h: h.tensor_tensor(TMS[:, :], TMS[:, :], rstd_b[:, 0:256],
                                                          OP.mult), [t1])
            t3 = P.op("vector", lambda h, mi=mi: h.tensor_scalar(
                ZB[:, 256 * mi:256 * (mi + 1)], TMS[:, :], selgC[:, mi:mi + 1],
                selbC[:, mi:mi + 1], OP.mult, OP.add), [t2])
            zn_tok.append(t3)
        zn_done = zn_tok[-1]
        if stop_after == "z" and e == 0:
            dbg = P.dram("dbgO", [128, 1024], BF16, kind="ExternalOutput")
            return P.finish(final_waits=[P.dma("gpsimd", dbg[:, :], ZB[:, :], [zn_done])])
        def oe_mm(h, buf=buf):
            r = None
            for m in range(8):
                bank = pb[4 + m // 2]
                off = 256 * (m % 2)
                for kc in range(4):
                    r = h.matmul(bank[:, off:off + 256],
                                 ew3b[buf][:, 1024 * kc + 128 * m:1024 * kc + 128 * (m + 1)],
                                 ZB[:, 256 * kc:256 * (kc + 1)], start=(kc == 0 and m % 2 == 0),
                                 stop=(kc == 3), skip_group_check=True)
            return r
        t_oe = P.op("tensor", oe_mm, [zn_done, statz[-1]])
        if stop_after == "oemm" and e == 0:
            dbg = P.dram("dbgO", [128, 512], kind="ExternalOutput")
            tz = P.op("vector", lambda h: h.tensor_copy(ACCT[0][:, 0:512], pb[4][:, 0:512]), [t_oe])
            return P.finish(final_waits=[P.dma("gpsimd", dbg[:, :], ACCT[0][:, 0:512], [tz])])
        oe_ev = []
        for m in range(8):
            bank = pb[4 + m // 2]
            off = 256 * (m % 2)
            t_e1 = P.op("scalar", lambda h, m=m, bank=bank, off=off, buf=buf: h.activation(
                S1[:, 256 * (m % 4):256 * (m % 4) + 256], bank[:, off:off + 256], AF.Identity,
                bias=eb3C[buf][:, m:m + 1]), [t_oe] + ([oe_ev[m - 4]] if m >= 4 else []))
            tts = []
            for g in range(2):
                t_t = P.op("tensor", lambda h, m=m, g=g: h.transpose(
                    pb[1][:, 64 * g:64 * (g + 1)].bitcast(BF16),
                    S1[:, 256 * (m % 4) + 128 * g:256 * (m % 4) + 128 * (g + 1)], identbb[:, :]),
                    [t_e1, (tts[-1] if tts else (oe_ev[-1] if oe_ev else None))])
                tts.append(P.op("vector", lambda h, m=m, g=g: h.tensor_copy(
                    OETN[:, 1024 * g + 128 * m:1024 * g + 128 * (m + 1)],
                    pb[1][:, 64 * g:64 * (g + 1)].bitcast(BF16)), [t_t]))
            oe_ev.append(tts[-1])
        oe_done = oe_ev[-1]
        if stop_after == "oe" and e == 0:
            dbg = P.dram("dbgO", [128, 2048], BF16, kind="ExternalOutput")
            return P.finish(final_waits=[P.dma("gpsimd", dbg[:, :], OETN[:, :], [oe_done])])
        for g in range(2):
            if e == 0:
                a1 = P.op("vector", lambda h, g=g, e=e: h.tensor_scalar(
                    ACCT[g][:, :], OETN[:, 1024 * g:1024 * (g + 1)], PK[g][:, e:e + 1], None,
                    OP.mult), [oe_done, gate_done])
            else:
                a0 = P.op("vector", lambda h, g=g, e=e: h.tensor_scalar(
                    TMP[:, :], OETN[:, 1024 * g:1024 * (g + 1)], PK[g][:, e:e + 1], None,
                    OP.mult), [oe_done, acc_tok[g]])
                a1 = P.op("vector", lambda h, g=g: h.tensor_tensor(
                    ACCT[g][:, :], ACCT[g][:, :], TMP[:, :], OP.add), [a0])
            acc_tok[g] = a1
        e_used[e] = t_oe
        if stop_after == "expert1" and e == 0:
            dbg = P.dram("dbgO", [256, DM], kind="ExternalOutput")
            f = [P.dma("gpsimd", dbg[128 * g:128 * (g + 1), :],
                       ACCT[g][:, :], [acc_tok[g]]) for g in range(2)]
            return P.finish(final_waits=f)

    fin = []
    for g in range(2):
        t1 = P.op("vector", lambda h, g=g: h.tensor_scalar(
            ACCT[g][:, :], ACCT[g][:, :], WSH[g][:, 1:2], None, OP.mult), [acc_tok[g]])
        t2 = P.op("vector", lambda h, g=g: h.tensor_scalar(
            TMP[:, :], SHM[g][:, :], WSH[g][:, 0:1], None, OP.mult), [t1, g_done])
        t3 = P.op("vector", lambda h, g=g: h.tensor_tensor(ACCT[g][:, :], ACCT[g][:, :],
                                                           TMP[:, :], OP.add), [t2])
        fin.append(P.dma("sync", moeO[128 * g:128 * (g + 1), :], ACCT[g][:, :], [t3]))
    return P.finish(final_waits=fin)


def host_maps_K3(inputs):
    G = np.concatenate([inputs["L1_ffn_sg_w"].T, inputs["L1_ffn_eg_w"].T], 1).astype(np.float32)
    gb9 = np.concatenate([inputs["L1_ffn_sg_b"], inputs["L1_ffn_eg_b"]]).reshape(1, 9)
    ew1T = np.ascontiguousarray(inputs["L1_ffn_ew1"].transpose(0, 2, 1).reshape(8 * DM, 512))
    ew2T = np.ascontiguousarray(inputs["L1_ffn_ew2"].transpose(0, 2, 1).reshape(8 * DM, 512))
    ew3T = np.ascontiguousarray(inputs["L1_ffn_ew3"].transpose(0, 2, 1).reshape(8 * 512, DM))
    base = {
        "gwI": np.ascontiguousarray(G), "gb9I": np.ascontiguousarray(gb9),
        "expbI": np.ascontiguousarray(inputs["L1_ffn_exp_bias"].reshape(1, 8)),
        "ew1TI": ew1T, "ew2TI": ew2T, "ew3TI": ew3T,
        "eb1I": np.ascontiguousarray(inputs["L1_ffn_eb1"].reshape(-1, 1)),
        "eb2I": np.ascontiguousarray(inputs["L1_ffn_eb2"].reshape(-1, 1)),
        "eb3I": np.ascontiguousarray(inputs["L1_ffn_eb3"].reshape(-1, 1)),
        "selgI": np.ascontiguousarray(inputs["L1_ffn_sel_ln_g"].reshape(-1, 1)),
        "selbI": np.ascontiguousarray(inputs["L1_ffn_sel_ln_b"].reshape(-1, 1)),
        "identI": np.eye(128, dtype=np.float32),
    }
    return base


def assemble_tn_full(parts):
    full = np.zeros((NB * NT, parts[0].shape[1]), parts[0].dtype)
    for k in range(NCORE):
        p = parts[k]
        for b in range(NB):
            full[b * NT + 256 * k: b * NT + 256 * (k + 1)] = p[256 * b:256 * (b + 1)]
    return full





def build_launch_K4(serial=False, debug=False):
    P = Prog("K4", serial=serial)
    x3sI = P.dram("x3sI", [NT, DM])
    shsI = P.dram("shsI", [NT, DM], BF16)
    moeF = P.dram("moeF", [NT, DM])
    idxbF = P.dram("idxbF", [NT, 1], I32)
    idxtF = P.dram("idxtF", [NT, 1], I32)
    kofsI = P.dram("kofsI", [128, 1], I32)
    kofs2I = P.dram("kofs2I", [128, 1], I32)
    x4O = P.dram("x4O", [NT, DM], kind="ExternalOutput")
    mslD = P.dram("mslD", [NT, DM], kind="Internal")
    mskD = P.dram("mskD", [NT, 4], kind="Internal")
    if debug:
        mslX = P.dram("mslX", [NT, DM], kind="ExternalOutput")
        mskX = P.dram("mskX", [NT, 4], kind="ExternalOutput")
        lrX = P.dram("lrX", [128, 16], U32, kind="ExternalOutput")
        shX = P.dram("shX", [128, DM], kind="ExternalOutput")
        ffX = P.dram("ffX", [128, DM], kind="ExternalOutput")
        tmpX = P.dram("tmpX", [128, DM], kind="ExternalOutput")

    idxc = P.sb("idxc", 128, 8, U32)
    kofs = P.sb("kofs", 128, 1, U32)
    bigc = P.sb("bigc", 128, 1, U32)
    ZT = P.sb("ZT", 128, DM)
    ONES4 = P.sb("ONES4", 128, 4)
    MT = [P.sb(f"MT{i}", 128, DM) for i in range(2)]
    MS = P.sb("MS", 128, DM)
    MK = P.sb("MK", 128, 4)
    X3T = P.sb("X3T", 128, DM)
    SHT = P.sb("SHT", 128, DM, BF16)
    TMP = P.sb("TMP", 128, DM)
    FF = P.sb("FF", 128, DM)
    SQT = P.sb("SQT", 128, DM)
    SMC = P.sb("SMC", 128, 8)
    epscol = P.sb("epscol", 128, 1)

    kofs2 = P.sb("kofs2", 128, 1, U32)
    P.dma("gpsimd", kofs[:, :], kofsI[:, :])
    P.dma("gpsimd", kofs2[:, :], kofs2I[:, :])
    P.op("gpsimd", lambda h: h.memset(bigc[:, :], 70000))
    P.op("gpsimd", lambda h: h.memset(ZT[:, :], 0.0))
    P.op("gpsimd", lambda h: h.memset(ONES4[:, :], 1.0))
    c_eps = P.op("gpsimd", lambda h: h.memset(epscol[:, :], 1e-5))
    wall = P.tok("gpsimd")

    z_tok = []
    for tt in range(16):
        z_tok.append(P.dma("sync", mslD[128 * tt:128 * (tt + 1), :], ZT[:, :], [wall]))
        z_tok.append(P.dma("sync", mskD[128 * tt:128 * (tt + 1), :], ONES4[:, :] if False else ZT[:, 0:4], [wall]))
    z_done = z_tok[-1]

    sc_tok = []
    ld_prev = [None, None]
    for g in range(16):
        t_ib = P.dma("gpsimd", idxc[:, 0:1], idxbF[128 * g:128 * (g + 1), :],
                     [sc_tok[-1] if sc_tok else None])
        t_it = P.dma("gpsimd", idxc[:, 1:2], idxtF[128 * g:128 * (g + 1), :], [])
        t_u = P.op("gpsimd", lambda h: h.tensor_tensor(idxc[:, 2:3], idxc[:, 1:2], kofs[:, :],
                                                       OP.subtract), [t_it, wall])
        t_c = P.op("gpsimd", lambda h: h.tensor_scalar(idxc[:, 3:4], idxc[:, 2:3], 256, None,
                                                       OP.is_lt), [t_u])
        t_l = P.op("gpsimd", lambda h: h.tensor_scalar(idxc[:, 4:5], idxc[:, 0:1], 256, None,
                                                       OP.mult), [t_ib, t_c])
        t_l2 = P.op("gpsimd", lambda h: h.tensor_tensor(idxc[:, 4:5], idxc[:, 4:5], idxc[:, 2:3],
                                                        OP.add), [t_l])
        t_sel = P.op("vector", lambda h: h.select(idxc[:, 5:6], idxc[:, 3:4], idxc[:, 4:5],
                                                  bigc[:, :]), [t_l2])
        buf = MT[g % 2]
        t_ld = P.dma("sync", buf[:, :], moeF[128 * g:128 * (g + 1), :], [ld_prev[g % 2]])
        t_s1 = P.op("gpsimd", lambda h, buf=buf: h.indirect_dma_start(
            mslD[:, :], IndirectOffsetOnAxis(ap=idxc[:, 5:6], axis=0), buf[:, :], None,
            oob_is_err=False), [t_sel, t_ld, z_done], step=16)
        t_s2 = P.op("gpsimd", lambda h: h.indirect_dma_start(
            mskD[:, :], IndirectOffsetOnAxis(ap=idxc[:, 5:6], axis=0), ONES4[:, :], None,
            oob_is_err=False), [t_s1], step=16)
        ld_prev[g % 2] = t_s1
        if debug:
            sc_tok.append(P.op("gpsimd", lambda h, g=g: h.dma_start(
                out=lrX[:, g:g + 1], in_=idxc[:, 5:6], allow_slow_non_contiguous=True),
                [t_s2], step=16))
        sc_tok.append(t_s2)
    sc_done = sc_tok[-1]
    if debug:
        for tt in range(16):
            sc_done = P.dma("gpsimd", mslX[128 * tt:128 * (tt + 1), :],
                            mslD[128 * tt:128 * (tt + 1), :], [sc_done])
            sc_done = P.dma("gpsimd", mskX[128 * tt:128 * (tt + 1), :],
                            mskD[128 * tt:128 * (tt + 1), :], [sc_done])

    fin = []
    prev = None
    prev_t1 = None
    prev_t2a = None
    prev_t13 = None
    for tt in range(16):
        tsl = slice(128 * tt, 128 * (tt + 1))
        l1 = P.dma("sync", MS[:, :], mslD[tsl, :], [sc_done, prev])
        l2 = P.dma("sync", MK[:, :], mskD[tsl, :], [sc_done, prev_t1])
        l3 = P.dma("sync", X3T[:, :], x3sI[tsl, :], [prev_t13])
        l4 = P.dma("gpsimd", SHT[:, :], shsI[tsl, :], [prev_t2a])
        t1 = P.op("vector", lambda h: h.tensor_scalar(SMC[:, 0:1], MK[:, 0:1], -1.0, 1.0,
                                                      OP.mult, OP.add), [l2])
        t2a = P.op("vector", lambda h: h.tensor_copy(TMP[:, :], SHT[:, :]), [l4])
        t2 = P.op("vector", lambda h: h.tensor_scalar(TMP[:, :], TMP[:, :], SMC[:, 0:1], None,
                                                      OP.mult), [t1, t2a])
        t3 = P.op("vector", lambda h: h.tensor_tensor(FF[:, :], TMP[:, :], MS[:, :], OP.add),
                  [t2, l1])
        t4 = P.op("vector", lambda h: h.tensor_reduce(SMC[:, 1:2], FF[:, :], axis=AX.X,
                                                      op=OP.add), [t3])
        t5 = P.op("vector", lambda h: h.tensor_scalar(SMC[:, 1:2], SMC[:, 1:2], 1.0 / DM, None,
                                                      OP.mult), [t4])
        t6 = P.op("vector", lambda h: h.tensor_scalar(FF[:, :], FF[:, :], SMC[:, 1:2], None,
                                                      OP.subtract), [t5])
        t7 = P.op("scalar", lambda h: h.activation(SQT[:, :], FF[:, :], AF.Square), [t6])
        t8 = P.op("vector", lambda h: h.tensor_reduce(SMC[:, 2:3], SQT[:, :], axis=AX.X,
                                                      op=OP.add), [t7])
        t9 = P.op("vector", lambda h: h.tensor_scalar(SMC[:, 2:3], SMC[:, 2:3], 1.0 / DM, None,
                                                      OP.mult), [t8])
        t10 = P.op("scalar", lambda h: h.activation(SMC[:, 2:3], SMC[:, 2:3], AF.Sqrt,
                                                    bias=epscol[:, 0:1]), [t9, c_eps])
        t11 = P.op("vector", lambda h: h.reciprocal(SMC[:, 2:3], SMC[:, 2:3]), [t10])
        t12 = P.op("vector", lambda h: h.tensor_scalar(FF[:, :], FF[:, :], SMC[:, 2:3], None,
                                                       OP.mult), [t11])
        if debug and tt == 0:
            d1 = P.op("vector", lambda h: h.tensor_copy(MT[0][:, :], SHT[:, :]), [l4, t2])
            P._dk = [P.dma("gpsimd", shX[:, :], MT[0][:, :], [d1]),
                     P.dma("gpsimd", tmpX[:, :], TMP[:, :], [t2]),
                     P.dma("gpsimd", ffX[:, :], FF[:, :], [t3])]
            t3b = P._dk[-1]
        t13 = P.op("vector", lambda h: h.tensor_tensor(FF[:, :], FF[:, :], X3T[:, :], OP.add),
                   [t12, l3] + ([t3b] if debug and tt == 0 else []))
        prev = P.dma("sync", x4O[tsl, :], FF[:, :], [t13])
        prev_t1 = t1
        prev_t2a = t2a
        prev_t13 = t13
        fin.append(prev)
    return P.finish(final_waits=fin)


def host_maps_K4(inputs, x3_parts, sh_parts, moe_full):
    ib = inputs["L1_ffn_index_b"].astype(np.int32).reshape(-1, 1)
    it = inputs["L1_ffn_index_t"].astype(np.int32).reshape(-1, 1)
    maps = []
    for k in range(NCORE):
        maps.append({
            "x3sI": np.ascontiguousarray(x3_parts[k]),
            "shsI": np.ascontiguousarray(sh_parts[k]),
            "moeF": np.ascontiguousarray(moe_full),
            "idxbF": ib, "idxtF": it,
            "kofsI": np.full((128, 1), 256 * k, np.int32),
            "kofs2I": np.full((128, 1), 256 * (k + 1), np.int32),
        })
    return maps


_NC_CACHE = {}


def _get_nc(name, builder):
    if name not in _NC_CACHE:
        _NC_CACHE[name] = builder()
    return _NC_CACHE[name]


def kernel(**inputs):
    ncA = _get_nc("A", build_launch_A)
    resA = run_bass_kernel_spmd(ncA, host_maps_A(inputs), core_ids=list(range(NCORE)))
    x1 = [np.asarray(resA.results[c]["x1T"]) for c in range(NCORE)]
    y = [np.asarray(resA.results[c]["y0T"]) for c in range(NCORE)]

    ncBa = _get_nc("Ba", build_launch_Battn)
    ncBf = _get_nc("Bf", build_launch_Bffn)
    for d in range(6):
        r1 = run_bass_kernel_spmd(ncBa, host_maps_Battn(inputs, y, d), core_ids=list(range(NCORE)))
        y = [np.asarray(r1.results[c]["youtT"]) for c in range(NCORE)]
        r2 = run_bass_kernel_spmd(ncBf, host_maps_Bffn(inputs, y, d), core_ids=list(range(NCORE)))
        y = [np.asarray(r2.results[c]["youtT"]) for c in range(NCORE)]

    ncC = _get_nc("C", build_launch_C)
    resC = run_bass_kernel_spmd(ncC, host_maps_C(x1, y), core_ids=list(range(NCORE)))
    x2 = [np.asarray(resC.results[c]["o1T"]) for c in range(NCORE)]
    ln2 = [np.asarray(resC.results[c]["o2T"]) for c in range(NCORE)]

    ncK2a = _get_nc("K2a", build_launch_K2a)
    r2a = run_bass_kernel_spmd(ncK2a, host_maps_K2a(inputs, ln2), core_ids=list(range(NCORE)))
    attn_parts = [np.asarray(r2a.results[k]["attnO"]) for k in range(NCORE)]

    ncK2b = _get_nc("K2b", build_launch_K2b)
    r2b = run_bass_kernel_spmd(ncK2b, host_maps_K2b(inputs, x2, attn_parts),
                               core_ids=list(range(NCORE)))
    x3_parts = [np.asarray(r2b.results[k]["x3O"]) for k in range(NCORE)]
    ln3_parts = [np.asarray(r2b.results[k]["ln3O"]) for k in range(NCORE)]
    sh_parts = [np.asarray(r2b.results[k]["shO"]) for k in range(NCORE)]

    ln3F = assemble_tn_full(ln3_parts)
    shF = assemble_tn_full(sh_parts)
    ib = inputs["L1_ffn_index_b"].astype(np.int32)
    it = inputs["L1_ffn_index_t"].astype(np.int32)
    ncK3 = _get_nc("K3", build_launch_K3)
    base3 = host_maps_K3(inputs)
    maps3 = []
    for k in range(NCORE):
        m = dict(base3)
        m["ln3F"] = ln3F
        m["shF"] = shF
        m["idxbI"] = np.ascontiguousarray(ib[256 * k:256 * (k + 1)].reshape(-1, 1))
        m["idxtI"] = np.ascontiguousarray(it[256 * k:256 * (k + 1)].reshape(-1, 1))
        maps3.append(m)
    r3 = run_bass_kernel_spmd(ncK3, maps3, core_ids=list(range(NCORE)))
    moe_full = np.concatenate([np.asarray(r3.results[k]["moeO"]) for k in range(NCORE)], 0)

    ncK4 = _get_nc("K4", build_launch_K4)
    r4 = run_bass_kernel_spmd(ncK4, host_maps_K4(inputs, x3_parts, sh_parts, moe_full),
                              core_ids=list(range(NCORE)))
    out = np.zeros((NB, NT, DM), np.float32)
    for k in range(NCORE):
        part = np.asarray(r4.results[k]["x4O"])
        for b in range(NB):
            out[b, 256 * k:256 * (k + 1)] = part[256 * b:256 * (b + 1)]
    return out








